# version 05
# baseline (speedup 1.0000x reference)
import numpy as np
import ml_dtypes
from contextlib import ExitStack
import concourse.bass as bass
import concourse.mybir as mybir
from concourse.bass_utils import run_bass_kernel_spmd

F32 = mybir.dt.float32
BF16 = mybir.dt.bfloat16
I32 = mybir.dt.int32
AF = mybir.ActivationFunctionType
ALU = mybir.AluOpType
AX = mybir.AxisListType
BF = ml_dtypes.bfloat16

NCORE = 8
TOK = 2048
NT = TOK // 128
D = 1024
HID = 3072
NH = 16
HD = 64
SEQ = 8192

STAGES = ["ffn0a", "attn0", "ffn0b", "ffn1a", "attn1", "ffn1b"]
LAST_RES = None
DBG_ON = False
DBG = {}
A0_NCH = None
A0_PRESCAN = False
A0_PHASE = None
A0_CSTOP = None


class Call:
    __slots__ = ("name", "args", "kwargs", "incs")
    def __init__(self, name, args, kwargs):
        self.name, self.args, self.kwargs = name, args, kwargs
        self.incs = []
    def then_inc(self, sem, n):
        Rec.CUR.semv[sem.name] = Rec.CUR.semv.get(sem.name, 0) + n
        self.incs.append((sem, n))
        return self


class Wait:
    __slots__ = ("sem", "val")
    def __init__(self, sem, val):
        self.sem, self.val = sem, val


class EngProxy:
    def __init__(self, rec, eng):
        object.__setattr__(self, "_rec", rec)
        object.__setattr__(self, "_eng", eng)
    def wait_ge(self, sem, val):
        self._rec.q[self._eng].append(Wait(sem, val))
    def wait(self, sem):
        self._rec.q[self._eng].append(Wait(sem, self._rec.semv.get(sem.name, 0)))
    def __getattr__(self, name):
        rec, eng = self._rec, self._eng
        def f(*a, **k):
            c = Call(name, a, k)
            rec.q[eng].append(c)
            return c
        return f


class Rec:
    CUR = None
    ENGS = ("sync", "tensor", "vector", "scalar", "gpsimd")
    def __init__(self):
        self.q = {e: [] for e in self.ENGS}
        self.semv = {}
        Rec.CUR = self
        for e in self.ENGS:
            setattr(self, e, EngProxy(self, e))
    def val(self, sem):
        return self.semv.get(sem.name, 0)
    def replay(self, eng_name, eng):
        for c in self.q[eng_name]:
            if isinstance(c, Wait):
                eng.wait_ge(c.sem, c.val)
            else:
                inst = getattr(eng, c.name)(*c.args, **c.kwargs)
                for (s, n) in c.incs:
                    inst.then_inc(s, n)
    def mark(self, eng, sem, n=1):
        for item in reversed(self.q[eng]):
            if isinstance(item, Call):
                if any(s.name == sem.name for (s, _) in item.incs):
                    return
                if item.incs:
                    getattr(self, eng).wait_ge(item.incs[-1][0], self.val(item.incs[-1][0]))
                    getattr(self, eng).nop().then_inc(sem, n)
                    return
                item.then_inc(sem, n)
                return
        getattr(self, eng).nop().then_inc(sem, n)

    def edge(self, eng, sem):
        self.mark(eng, sem)
        getattr(self, eng).wait_ge(sem, self.val(sem))

    def done_token(self, eng):
        for item in reversed(self.q[eng]):
            if isinstance(item, Call):
                if item.incs:
                    s = item.incs[-1][0]
                    return (s, self.val(s))
                if item.name.startswith("dma"):
                    item.then_inc(Rec.BAR[eng], 16)
                    return (Rec.BAR[eng], self.val(Rec.BAR[eng]))
                item.then_inc(Rec.BAR[eng], 1)
                return (Rec.BAR[eng], self.val(Rec.BAR[eng]))
        return None

    def barrier(self, sems):
        Rec.BAR = sems
        targets = [t for e in self.ENGS if (t := self.done_token(e)) is not None]
        for e in self.ENGS:
            for (s, v) in targets:
                self.q[e].append(Wait(s, v))


def build_program(stages):
    nc = bass.Bass("TRN2", target_bir_lowering=False, detect_race_conditions=False, dynamic_dma_scratch_size=16384)
    es = ExitStack()
    dram = {}
    def din(name, shape, dt=F32):
        dram[name] = nc.dram_tensor(name, list(shape), dt, kind="ExternalInput").ap()
        return dram[name]
    X = din("x", (TOK, D))
    OUT = nc.dram_tensor("out", [TOK, D], F32, kind="ExternalOutput").ap()
    din("wgT", (D, HID), I32); din("wuT", (D, HID), I32); din("wdT", (HID, D), I32)
    din("lam", (4, 512))
    din("crT", (512, D)); din("ciT", (512, D))
    din("gbd_rep", (128, 3, D))
    din("fsb_rep", (128, 2, D))
    din("wfT", (D, D))
    din("eye", (128, 128))
    din("eyef", (128, 128))
    din("mskA", (128, 512))
    din("inclTf", (128, 128))
    din("blksel", (128, 256))
    din("wmy1", (D, 1024))
    din("wmy2", (D, 16))
    din("womy", (256, D))
    din("attc_my", (1, 128))
    din("prod", (512, 512))
    din("cmask", (1, 16))
    din("invfix", (1, 1024))
    din("parn", (128, 4, 128))
    din("qmask", (1, 8))

    sb = lambda name, shp, dt: es.enter_context(nc.sbuf_tensor(name, list(shp), dt))
    resid = sb("resid", (128, NT, D), F32)
    eye = sb("eye_s", (128, 128), BF16)
    eyef = sb("eyef_s", (128, 128), F32)
    mskA = sb("mskA_s", (128, 512), BF16)
    inclTf_sb = sb("inclTf_s", (128, 128), F32)
    onescol_f = sb("onescol_f", (128, 1), F32)
    onesrow_f = sb("onesrow_f", (1, 128), F32)
    onesrow_b = sb("onesrow_b", (1, 128), BF16)
    rowst = sb("rowst", (1, 4), F32)
    colbf_s = sb("colbf_s", (128, 1), BF16)
    lam = sb("lam_s", (4, 512), F32)
    cmask = sb("cmask_s", (1, 16), F32)
    qmask = sb("qmask_s", (1, 8), F32)
    xnorm = sb("xnorm", (128, 2, D), BF16)
    xT = sb("xT", (128, 8, TOK), BF16)
    stats = sb("stats", (128, NT, 8), F32)
    ccol = sb("ccol", (128, 8), F32)
    sq_scr = sb("sq_scr", (128, D), BF16)

    pb = [es.enter_context(nc.psum_tensor(f"pb{i}", [128, 512], F32)) for i in range(6)]
    ptr = es.enter_context(nc.psum_tensor("ptr", [128, 1024], BF16))
    ptr2 = es.enter_context(nc.psum_tensor("ptr2", [128, 1024], BF16))

    sems = {}
    for s in ("ld", "gld", "tc", "vc", "sc", "gc", "st", "cc"):
        sems[s] = es.enter_context(nc.semaphore(s))
    ld, gld, tc, vc, sc, gc, st, cc = (sems[k] for k in ("ld", "gld", "tc", "vc", "sc", "gc", "st", "cc"))
    bar_sems = {"sync": ld, "tensor": tc, "vector": vc, "scalar": sc, "gpsimd": gc}

    GROUPS = [[0, 1, 2, 3], [4, 5, 6, 7]]
    idram = {}
    def dint(name, shape, dt):
        idram[name] = nc.dram_tensor(name, list(shape), dt, kind="Internal").ap()
        return idram[name]
    dint("xn_contrib", (D, TOK), BF16)
    dint("xn_gather", (4 * D, TOK), BF16)
    dint("ors_contrib", (SEQ, D), BF16)
    dint("ors_out", (TOK, D), BF16)
    dint("s_contrib", (512, 2 * D), BF16)
    dint("s_gather", (4 * 512, 2 * D), BF16)

    r = Rec()

    r.sync.dma_start(out=resid[:, :, :], in_=X.rearrange("(t p) d -> p t d", p=128)).then_inc(ld, 16)
    r.sync.dma_start(out=lam[:, :], in_=dram["lam"]).then_inc(ld, 16)
    r.sync.dma_start(out=cmask[:, :], in_=dram["cmask"]).then_inc(ld, 16)
    r.sync.dma_start(out=qmask[:, :], in_=dram["qmask"]).then_inc(ld, 16)
    r.vector.memset(ccol[:, 0:1], 1e-5)
    r.vector.memset(ccol[:, 1:2], 1.0 / 127.0)
    r.vector.memset(ccol[:, 2:3], 1e-6)
    r.vector.memset(ccol[:, 3:4], 0.0)
    r.vector.memset(ccol[:, 4:5], 1.0)
    r.vector.memset(ccol[:, 6:7], -1.0)
    r.gpsimd.dma_start(out=eye[:, :], in_=dram["eye"]).then_inc(gld, 16)
    r.gpsimd.dma_start(out=eyef[:, :], in_=dram["eyef"]).then_inc(gld, 16)
    r.gpsimd.dma_start(out=mskA[:, :], in_=dram["mskA"]).then_inc(gld, 16)
    r.sync.dma_start(out=inclTf_sb[:, :], in_=dram["inclTf"]).then_inc(ld, 16)
    r.vector.memset(onescol_f[:, :], 1.0)
    r.vector.memset(onesrow_f[:, :], 1.0)
    r.vector.memset(onesrow_b[:, :], 1.0)
    r.mark("vector", vc)
    if True:
        r.barrier(bar_sems)

        ctx = dict(nc=nc, r=r, es=es, sb=sb, dram=dram, idram=idram, sems=sems, pb=pb,
                   ptr=ptr, ptr2=ptr2, resid=resid, xnorm=xnorm, xT=xT, stats=stats,
                   sq_scr=sq_scr, eye=eye, eyef=eyef, mskA=mskA, inclTf_sb=inclTf_sb,
                   onescol_f=onescol_f, onesrow_f=onesrow_f, onesrow_b=onesrow_b,
                   rowst=rowst, colbf_s=colbf_s, lam=lam, cmask=cmask, ccol=ccol,
                   qmask=qmask, bar_sems=bar_sems, GROUPS=GROUPS, ffn1_es=None)

        for stg in stages:
            if stg in ("ffn0a", "ffn0b"):
                emit_ln(ctx, gamma_beta=False)
                r.barrier(bar_sems)
                emit_ffn0(ctx)
            elif stg in ("ffn1a", "ffn1b"):
                if ctx["ffn1_es"] is None:
                    prep_ffn1(ctx)
                emit_ln(ctx, gamma_beta=False)
                r.barrier(bar_sems)
                emit_ffn1(ctx)
            elif stg == "attn0":
                emit_ln(ctx, gamma_beta=False)
                r.barrier(bar_sems)
                emit_attn0(ctx)
            elif stg == "ln":
                emit_ln(ctx, gamma_beta=False)
            elif stg == "attn1":
                if ctx["ffn1_es"] is not None:
                    ctx["ffn1_es"].close()
                    ctx["ffn1_es"] = None
                emit_attn1(ctx)
            r.barrier(bar_sems)

        r.sync.dma_start(out=OUT.rearrange("(t p) d -> p t d", p=128), in_=resid[:, :, :]).then_inc(st, 16)
        for nm, ap in list(DBG.items()):
            shp = list(ap.shape)
            dbg_out = nc.dram_tensor("dbg_" + nm, [128] + shp[1:], F32, kind="ExternalOutput").ap()
            r.gpsimd.dma_start(out=dbg_out, in_=ap).then_inc(sems["gld"], 16)
        r.sync.wait(st)
        r.sync.wait(sems["gld"])

        with nc.Block() as block:
            block.sync(lambda e: r.replay("sync", e))
            block.tensor(lambda e: r.replay("tensor", e))
            block.vector(lambda e: r.replay("vector", e))
            block.scalar(lambda e: r.replay("scalar", e))
            block.gpsimd(lambda e: r.replay("gpsimd", e))
    if ctx["ffn1_es"] is not None:
        ctx["ffn1_es"].close()
        ctx["ffn1_es"] = None
    es.close()
    return nc


def emit_ln(ctx, gamma_beta, tokmajor_out=None, want_T=True):
    r = ctx["r"]; resid = ctx["resid"]; xnorm = ctx["xnorm"]; xT = ctx["xT"]
    stats = ctx["stats"]; sq = ctx["sq_scr"]; eye = ctx["eye"]
    ptr, ptr2 = ctx["ptr"], ctx["ptr2"]
    tc, vc, sc = (ctx["sems"][k] for k in ("tc", "vc", "sc"))
    gbd = ctx.get("gbd_rep")
    EPS = 1e-5
    for t in range(NT):
        x = resid[:, t, :]
        stt = stats[:, t, :]
        if t > 0:
            r.scalar.wait_ge(sc, r.val(sc))
        r.scalar.activation(sq[:, :], x, AF.Square, accum_out=stt[:, 0:1]).then_inc(sc, 1)
        r.vector.tensor_reduce(out=stt[:, 1:2], in_=x, axis=AX.X, op=ALU.add)
    r.vector.wait_ge(sc, r.val(sc))
    r.edge("vector", vc)
    S_ = lambda c: stats[:, :, c:c + 1]
    r.vector.tensor_scalar(out=S_(2), in0=S_(1), scalar1=1.0 / D, scalar2=None, op0=ALU.mult)
    r.vector.tensor_scalar(out=S_(3), in0=S_(0), scalar1=1.0 / D, scalar2=None, op0=ALU.mult)
    r.edge("vector", vc)
    r.vector.tensor_tensor(out=S_(4), in0=S_(2), in1=S_(2), op=ALU.mult)
    r.edge("vector", vc)
    r.vector.tensor_tensor(out=S_(4), in0=S_(3), in1=S_(4), op=ALU.subtract)
    r.mark("vector", vc)
    r.scalar.wait_ge(vc, r.val(vc))
    r.scalar.activation(S_(5), S_(4), AF.Sqrt, bias=ctx["ccol"][:, 0:1]).then_inc(sc, 1)
    r.vector.wait_ge(sc, r.val(sc))
    r.vector.reciprocal(out=S_(6), in_=S_(5))
    r.edge("vector", vc)
    if gamma_beta:
        r.vector.tensor_tensor(out=S_(3), in0=S_(6), in1=S_(6), op=ALU.mult)
        r.edge("vector", vc)
        r.vector.tensor_tensor(out=S_(3), in0=S_(3), in1=S_(4), op=ALU.mult)
        r.mark("vector", vc)
        r.scalar.wait_ge(vc, r.val(vc))
        r.scalar.activation(S_(5), S_(3), AF.Sqrt, bias=ctx["ccol"][:, 0:1]).then_inc(sc, 1)
        r.vector.wait_ge(sc, r.val(sc))
        r.vector.reciprocal(out=S_(0), in_=S_(5))
        r.edge("vector", vc)
        r.vector.tensor_tensor(out=S_(6), in0=S_(6), in1=S_(0), op=ALU.mult)
        r.edge("vector", vc)
    r.vector.tensor_tensor(out=S_(7), in0=S_(2), in1=S_(6), op=ALU.mult)
    r.edge("vector", vc)
    r.vector.tensor_scalar(out=S_(7), in0=S_(7), scalar1=-1.0, scalar2=None, op0=ALU.mult)
    r.mark("vector", vc)
    r.scalar.wait_ge(vc, r.val(vc))
    if DBG_ON:
        DBG["xT"] = ctx["xT"][:, :, :]
        DBG["stats"] = ctx["stats"][:, :, :]
    tile_ready = {}
    for t in range(NT):
        stt = stats[:, t, :]
        xb = tokmajor_out[:, t, :] if tokmajor_out is not None else xnorm[:, t % 2, :]
        if t >= 2 and tokmajor_out is None:
            r.scalar.wait_ge(tc, tile_ready.get(t - 2, 0))
        r.scalar.activation(xb, resid[:, t, :], AF.Identity,
                            scale=stt[:, 6:7], bias=stt[:, 7:8]).then_inc(sc, 1)
        if gamma_beta:
            r.vector.wait_ge(sc, r.val(sc))
            r.vector.tensor_tensor(out=xb, in0=xb, in1=gbd[:, 0, :], op=ALU.mult)
            r.vector.tensor_tensor(out=xb, in0=xb, in1=gbd[:, 1, :], op=ALU.add).then_inc(vc, 1)
        if want_T:
            bank = ptr if t % 2 == 0 else ptr2
            r.tensor.wait_ge(sc, r.val(sc))
            if gamma_beta:
                r.tensor.wait_ge(vc, r.val(vc))
            if t >= 2:
                r.tensor.wait_ge(vc, tile_ready[t - 2 + 1000])
            for dt in range(8):
                r.tensor.transpose(bank[:, dt * 128:(dt + 1) * 128], xb[:, dt * 128:(dt + 1) * 128], eye[:, :])
            r.mark("tensor", tc)
            tile_ready[t] = r.val(tc)
            r.vector.wait_ge(tc, r.val(tc))
            xtv = xT[:, :, t * 128:(t + 1) * 128]
            r.vector.tensor_copy(out=xtv, in_=bank[:, :].rearrange("p (dt c) -> p dt c", dt=8)).then_inc(vc, 1)
            tile_ready[t + 1000] = r.val(vc)
    if want_T:
        r.tensor.wait_ge(vc, r.val(vc))
    r.scalar.wait_ge(sc, r.val(sc))


def emit_ffn0(ctx):
    r = ctx["r"]; nc = ctx["nc"]
    xT = ctx["xT"]; resid = ctx["resid"]; pb = ctx["pb"]
    dram = ctx["dram"]
    ld, gld, tc, vc, sc = (ctx["sems"][k] for k in ("ld", "gld", "tc", "vc", "sc"))
    SCALE = 1.0 / 127.0
    wgT3 = dram["wgT"].rearrange("(dt p) H -> p dt H", p=128)
    wuT3 = dram["wuT"].rearrange("(dt p) H -> p dt H", p=128)
    wdT3 = dram["wdT"].rearrange("(ht p) d -> p ht d", p=128)
    NTB = 4

    sfx = str(ctx.setdefault("ffn0_calls", 0))
    ctx["ffn0_calls"] += 1
    with nc.sbuf_tensor("hact" + sfx, [128, 24, 512], BF16) as hact, \
         nc.sbuf_tensor("wdt_sb" + sfx, [128, 24, 1024], BF16) as wdt_sb, \
         nc.sbuf_tensor("wgu" + sfx, [128, 2, 2, 8, 128], BF16) as wgu, \
         nc.sbuf_tensor("sg" + sfx, [128, 2, 512], BF16) as sg:
        for i in range(4):
            r.gpsimd.dma_start(out=wdt_sb[:, 6 * i:6 * (i + 1), :],
                               in_=wdT3[:, 6 * i:6 * (i + 1), :]).then_inc(gld, 16)
        gld0 = r.val(gld)
        for ht in (0, 1):
            r.gpsimd.dma_start(out=wgu[:, ht % 2, 0, :, :], in_=wgT3[:, :, ht * 128:(ht + 1) * 128]).then_inc(gld, 16)
            r.gpsimd.dma_start(out=wgu[:, ht % 2, 1, :, :], in_=wuT3[:, :, ht * 128:(ht + 1) * 128]).then_inc(gld, 16)
        if DBG_ON:
            DBG["hact"] = hact[:, :, :]
        hp_free = {0: (0, 0), 1: (0, 0)}
        for tb in range(NTB):
            tok0 = tb * 512
            for ht in range(24):
                idx = tb * 24 + ht
                par = idx % 2
                r.tensor.wait_ge(gld, gld0 + 32 * (idx + 1))
                pg, pu = pb[0 + par], pb[2 + par]
                scm, vcm = hp_free[par]
                r.tensor.wait_ge(sc, scm)
                r.tensor.wait_ge(vc, vcm)
                rhs = xT[:, :, tok0:tok0 + 512]
                for dt in range(8):
                    r.tensor.matmul(pg[:, :], wgu[:, par, 0, dt, :], rhs[:, dt, :],
                                    start=(dt == 0), stop=(dt == 7))
                for dt in range(8):
                    r.tensor.matmul(pu[:, :], wgu[:, par, 1, dt, :], rhs[:, dt, :],
                                    start=(dt == 0), stop=(dt == 7))
                r.mark("tensor", tc)
                tmark = r.val(tc)
                r.scalar.wait_ge(tc, tmark)
                r.scalar.wait_ge(vc, hp_free[par][1])
                r.scalar.activation(sg[:, par, :], pg[:, :], AF.Silu, scale=ctx["ccol"][:, 1:2]).then_inc(sc, 1)
                r.vector.wait_ge(sc, r.val(sc))
                r.vector.tensor_tensor(out=hact[:, ht, :], in0=pu[:, :], in1=sg[:, par, :], op=ALU.mult).then_inc(vc, 1)
                hp_free[par] = (r.val(sc), r.val(vc))
                nxt = idx + 2
                if nxt < NTB * 24:
                    nht = nxt % 24
                    r.gpsimd.wait_ge(tc, r.val(tc))
                    r.gpsimd.dma_start(out=wgu[:, par, 0, :, :],
                                       in_=wgT3[:, :, nht * 128:(nht + 1) * 128]).then_inc(gld, 16)
                    r.gpsimd.dma_start(out=wgu[:, par, 1, :, :],
                                       in_=wuT3[:, :, nht * 128:(nht + 1) * 128]).then_inc(gld, 16)
            r.tensor.wait_ge(vc, r.val(vc))
            r.tensor.wait_ge(gld, gld0)
            dn_free = {0: 0, 1: 0}
            for tt in range(4):
                for ds in range(2):
                    bp = (tt * 2 + ds) % 2
                    bank = pb[4 + bp]
                    r.tensor.wait_ge(vc, dn_free[bp])
                    for ht in range(24):
                        r.tensor.matmul(bank[:, :], hact[:, ht, tt * 128:(tt + 1) * 128],
                                        wdt_sb[:, ht, ds * 512:(ds + 1) * 512],
                                        start=(ht == 0), stop=(ht == 23))
                    r.mark("tensor", tc)
                    r.vector.wait_ge(tc, r.val(tc))
                    rs = resid[:, tb * 4 + tt, ds * 512:(ds + 1) * 512]
                    r.vector.scalar_tensor_tensor(out=rs, in0=bank[:, :], scalar=0.5 * SCALE * SCALE,
                                                  in1=rs, op0=ALU.mult, op1=ALU.add).then_inc(vc, 1)
                    dn_free[bp] = r.val(vc)
        r.tensor.wait_ge(vc, r.val(vc))


def prep_ffn1(ctx):
    from contextlib import ExitStack as _FES
    r = ctx["r"]; nc = ctx["nc"]; dram = ctx["dram"]
    ld, vc = ctx["sems"]["ld"], ctx["sems"]["vc"]
    fes = _FES()
    sfx = str(ctx.setdefault("prep1_calls", 0))
    ctx["prep1_calls"] += 1
    fsb_rep = fes.enter_context(nc.sbuf_tensor("fsb_rep_s" + sfx, [128, 2, D], F32))
    wfs = fes.enter_context(nc.sbuf_tensor("wfs" + sfx, [128, 8, D], BF16))
    ctx["fsb_rep"] = fsb_rep
    ctx["wfs"] = wfs
    ctx["ffn1_es"] = fes
    r.sync.dma_start(out=fsb_rep[:, :, :], in_=dram["fsb_rep"]).then_inc(ld, 16)
    with nc.sbuf_tensor("wf_f32" + sfx, [128, D], F32) as wtmp:
        wfT3 = dram["wfT"].rearrange("(dt p) o -> p dt o", p=128)
        r.vector.wait(ld)
        for dt in range(8):
            r.sync.dma_start(out=wtmp[:, :], in_=wfT3[:, dt, :]).then_inc(ld, 16)
            r.vector.wait(ld)
            r.vector.tensor_scalar(out=wtmp[:, :], in0=wtmp[:, :], scalar1=0.0,
                                   scalar2=2.0, op0=ALU.is_ge, op1=ALU.mult)
            r.vector.tensor_scalar(out=wtmp[:, :], in0=wtmp[:, :], scalar1=1.0,
                                   scalar2=None, op0=ALU.subtract)
            r.vector.tensor_tensor(out=wfs[:, dt, :], in0=wtmp[:, :], in1=fsb_rep[:, 0, :], op=ALU.mult).then_inc(vc, 1)
            r.sync.wait_ge(vc, r.val(vc))
    r.barrier(ctx["bar_sems"])


def emit_ffn1(ctx):
    r = ctx["r"]
    xT = ctx["xT"]; resid = ctx["resid"]; pb = ctx["pb"]; wfs = ctx["wfs"]
    fsb = ctx["fsb_rep"]
    tc, vc = (ctx["sems"][k] for k in ("tc", "vc"))
    dn_free = {0: 0, 1: 0}
    sfx = str(ctx.setdefault("ffn1_calls", 0))
    ctx["ffn1_calls"] += 1
    with ctx["nc"].sbuf_tensor("f1tmp" + sfx, [128, 512], F32) as tmp:
        for tt in range(NT):
            for ds in range(2):
                bp = (tt * 2 + ds) % 2
                bank = pb[bp]
                r.tensor.wait_ge(vc, dn_free[bp])
                for dt in range(8):
                    r.tensor.matmul(bank[:, :], xT[:, dt, tt * 128:(tt + 1) * 128],
                                    wfs[:, dt, ds * 512:(ds + 1) * 512],
                                    start=(dt == 0), stop=(dt == 7))
                r.mark("tensor", tc)
                r.vector.wait_ge(tc, r.val(tc))
                r.vector.tensor_tensor(out=tmp[:, :], in0=bank[:, :],
                                       in1=fsb[:, 1, ds * 512:(ds + 1) * 512], op=ALU.add)
                r.vector.tensor_scalar(out=tmp[:, :], in0=tmp[:, :], scalar1=100.0,
                                       scalar2=-100.0, op0=ALU.min, op1=ALU.max)
                rs = resid[:, tt, ds * 512:(ds + 1) * 512]
                r.vector.scalar_tensor_tensor(out=rs, in0=tmp[:, :], scalar=0.5,
                                              in1=rs, op0=ALU.mult, op1=ALU.add).then_inc(vc, 1)
                dn_free[bp] = r.val(vc)
        r.tensor.wait_ge(vc, r.val(vc))


def rev_slice(ap2d, lo, hi):
    v = ap2d[..., lo:hi]
    step, count = v.ap[-1]
    v.ap[-1] = [-step, count]
    v.offset = v.offset + step * (count - 1)
    return v


def _colbf(ctx, r, col_f32, vc):
    r.vector.tensor_copy(out=ctx["colbf_s"][:, 0:1], in_=col_f32).then_inc(vc, 1)
    r.tensor.wait_ge(vc, Rec.CUR.val(vc))
    return ctx["colbf_s"][:, 0:1]


def emit_attn0(ctx):
    r = ctx["r"]; nc = ctx["nc"]; dram = ctx["dram"]; idram = ctx["idram"]
    pb = ctx["pb"]; resid = ctx["resid"]; xT = ctx["xT"]; eye = ctx["eye"]; ccol = ctx["ccol"]
    ld, gld, tc, vc, sc, gc, cc, st = (ctx["sems"][k] for k in ("ld", "gld", "tc", "vc", "sc", "gc", "cc", "st"))
    NCH = 64
    LOG_RTK = float(np.log(np.sqrt(64.0)))

    r.sync.dma_start(out=idram["xn_contrib"].rearrange("(dt p) t -> p dt t", p=128),
                     in_=xT[:, :, :]).then_inc(st, 16)
    r.gpsimd.wait(st)
    r.gpsimd.collective_compute("AllGather", ALU.bypass, replica_groups=ctx["GROUPS"],
                                ins=[idram["xn_contrib"]], outs=[idram["xn_gather"]]).then_inc(cc, 1)
    for e in Rec.ENGS:
        getattr(r, e).wait_ge(cc, r.val(cc))
    if A0_PHASE == "gather":
        return

    xg4 = idram["xn_gather"].rearrange("(pp dt p) t -> p pp dt t", p=128, dt=8)
    xTflat = xT[:, :, :].rearrange("p a b -> p (a b)")
    qT_w = xTflat[:, 0:SEQ]
    kT_w = xTflat[:, SEQ:2 * SEQ]

    from contextlib import ExitStack as _ES
    _es0 = _ES()
    ktok = _es0.enter_context(nc.sbuf_tensor("ktok", [128, NCH, 128], BF16))
    vb = _es0.enter_context(nc.sbuf_tensor("vb", [128, NCH, 128], BF16))
    gtok = _es0.enter_context(nc.sbuf_tensor("gtok", [128, NCH, 128], BF16))
    ow = _es0.enter_context(nc.sbuf_tensor("ow", [128, NCH, 128], BF16))
    wmy_sb = _es0.enter_context(nc.sbuf_tensor("wmy_sb", [128, 8, 512], BF16))
    wmy2_sb = _es0.enter_context(nc.sbuf_tensor("wmy2_sb", [128, 8, 4], BF16))
    womy_sb = _es0.enter_context(nc.sbuf_tensor("womy_sb", [128, 1, 1024], BF16))
    xgb = _es0.enter_context(nc.sbuf_tensor("xgb", [128, 2, 8, 256], BF16))
    bdw = _es0.enter_context(nc.sbuf_tensor("bdw", [128, NCH, 12], F32))
    gwork = _es0.enter_context(nc.sbuf_tensor("gwork", [128, NCH, 12], F32))
    tabf = _es0.enter_context(nc.sbuf_tensor("tabf", [128, 512], F32))
    blksel_sb = _es0.enter_context(nc.sbuf_tensor("blksel_s", [128, 256], F32))
    r.sync.dma_start(out=blksel_sb[:, :], in_=dram["blksel"]).then_inc(ld, 16)
    ctx["blksel_sb"] = blksel_sb
    qr = _es0.enter_context(nc.sbuf_tensor("qr", [128, 2, 128], BF16))
    sqw2 = _es0.enter_context(nc.sbuf_tensor("sqw2", [128, 256], BF16))
    hcol = _es0.enter_context(nc.sbuf_tensor("hcol", [128, 16], F32))
    normw_c = _es0.enter_context(nc.sbuf_tensor("normw_c", [128, 128], BF16))
    ltab = _es0.enter_context(nc.sbuf_tensor("ltab", [128, 1, 512], BF16))
    PA = _es0.enter_context(nc.sbuf_tensor("PA", [128, 1, 2, 256], BF16))
    Pk_r = _es0.enter_context(nc.sbuf_tensor("Pk_r", [128, 1, 2, 128], BF16))
    attnT_r = _es0.enter_context(nc.sbuf_tensor("attnT_r", [128, 1, 2, 128], BF16))
    kcdT_r = _es0.enter_context(nc.sbuf_tensor("kcdT_r", [128, 1, 128], BF16))
    Wkb_r = _es0.enter_context(nc.sbuf_tensor("Wkb_r", [128, 1, 2, 64], BF16))
    kd_r = _es0.enter_context(nc.sbuf_tensor("kd_r", [128, 1, 2, 64], BF16))
    vp_r = _es0.enter_context(nc.sbuf_tensor("vp_r", [128, 1, 2, 64], BF16))
    vn_r = _es0.enter_context(nc.sbuf_tensor("vn_r", [128, 1, 2, 64], BF16))
    Sf = _es0.enter_context(nc.sbuf_tensor("Sf", [128, 64], F32))
    Sbf = _es0.enter_context(nc.sbuf_tensor("Sbf", [128, 64], BF16))
    oft = _es0.enter_context(nc.sbuf_tensor("oft", [128, 2, 128], BF16))
    if True:
        with nc.sbuf_tensor("am_sb", [1, 128], F32) as am_sb, \
             nc.sbuf_tensor("ones_r3", [1, 128], F32) as ones_r3:
            r.sync.dma_start(out=am_sb[:, :], in_=dram["attc_my"]).then_inc(ld, 16)
            r.vector.memset(ones_r3[:, :], 1.0)
            r.mark("vector", vc)
            r.scalar.wait(ld)
            r.scalar.wait_ge(vc, r.val(vc))
            r.scalar.activation(am_sb[0:1, 4:8], am_sb[0:1, 4:8], AF.Exp).then_inc(sc, 1)
            r.vector.wait_ge(sc, r.val(sc))
            r.vector.tensor_scalar(out=am_sb[0:1, 4:8], in0=am_sb[0:1, 4:8], scalar1=-1.0,
                                   scalar2=None, op0=ALU.mult)
            r.mark("vector", vc)
            r.tensor.wait_ge(vc, r.val(vc))
            r.tensor.matmul(pb[5][:, 0:8], ones_r3[:, :], am_sb[0:1, 0:8], start=True, stop=True)
            r.tensor.matmul(pb[5][:, 8:72], ones_r3[:, :], am_sb[0:1, 8:72], start=True, stop=True)
            r.mark("tensor", tc)
            r.vector.wait_ge(tc, r.val(tc))
            r.vector.tensor_copy(out=hcol[:, 0:8], in_=pb[5][:, 0:8])
            r.vector.tensor_copy(out=normw_c[:, 0:64], in_=pb[5][:, 8:72])
            r.vector.tensor_copy(out=normw_c[:, 64:128], in_=pb[5][:, 8:72]).then_inc(vc, 1)
            r.tensor.wait_ge(vc, r.val(vc))

        wm3 = dram["wmy1"].rearrange("(dt p) o -> p dt o", p=128)
        wm23 = dram["wmy2"].rearrange("(dt p) o -> p dt o", p=128)
        wo3 = dram["womy"].rearrange("(wt p) o -> p wt o", p=128)
        for w in range(1 if A0_NCH is not None else 2):
            hb = 2 * w
            r.gpsimd.wait_ge(tc, r.val(tc))
            for j in range(4):
                r.gpsimd.dma_start(out=wmy_sb[:, :, j * 128:(j + 1) * 128],
                                   in_=wm3[:, :, j * 256 + hb * 64: j * 256 + hb * 64 + 128]).then_inc(gld, 16)
            r.gpsimd.dma_start(out=wmy2_sb[:, :, :], in_=wm23[:, :, 4 * w: 4 * w + 4]).then_inc(gld, 16)
            r.gpsimd.dma_start(out=womy_sb[:, 0:1, :], in_=wo3[:, w:w + 1, :]).then_inc(gld, 16)
            wv_ld = r.val(gld)
            dtb = hcol[:, hb:hb + 2]
            negEA = hcol[:, 4 + hb:4 + hb + 2]
            pj_marks = {}
            for tg in range(32):
                par = tg % 2
                r.sync.wait_ge(tc, pj_marks.get(("x", par), 0))
                r.sync.dma_start(out=xgb[:, par, :, :],
                                 in_=xg4[:, tg // 8, :, (tg % 8) * 256:(tg % 8 + 1) * 256]).then_inc(ld, 16)
                xld = r.val(ld)
                for ti in range(2):
                    ch = tg * 2 + ti
                    bank = pb[ti % 2]
                    bank2 = pb[2]
                    r.tensor.wait_ge(ld, xld)
                    r.tensor.wait_ge(gld, wv_ld)
                    r.tensor.wait_ge(vc, pj_marks.get(("e", ti % 2), 0))
                    r.tensor.wait_ge(sc, pj_marks.get("ba", 0))
                    for dt in range(8):
                        r.tensor.matmul(bank[:, :], xgb[:, par, dt, ti * 128:(ti + 1) * 128],
                                        wmy_sb[:, dt, :], start=(dt == 0), stop=(dt == 7))
                    for dt in range(8):
                        r.tensor.matmul(bank2[:, 0:4], xgb[:, par, dt, ti * 128:(ti + 1) * 128],
                                        wmy2_sb[:, dt, :], start=(dt == 0), stop=(dt == 7))
                    r.mark("tensor", tc)
                    pj_marks[("x", par)] = r.val(tc)
                    tm = r.val(tc)
                    r.scalar.wait_ge(tc, tm)
                    r.scalar.activation(bdw[:, ch, 2:4], bank2[:, 0:2], AF.Sigmoid).then_inc(sc, 1)
                    r.vector.wait_ge(sc, r.val(sc))
                    r.vector.tensor_tensor(out=bdw[:, ch, 0:2], in0=bank2[:, 2:4], in1=dtb, op=ALU.add)
                    r.mark("vector", vc)
                    r.scalar.wait_ge(vc, r.val(vc))
                    r.scalar.activation(bdw[:, ch, 0:2], bdw[:, ch, 0:2], AF.Exp)
                    r.scalar.activation(bdw[:, ch, 0:2], bdw[:, ch, 0:2], AF.Ln, bias=ccol[:, 4:5]).then_inc(sc, 1)
                    pj_marks["ba"] = r.val(sc)
                    r.vector.wait_ge(sc, r.val(sc))
                    r.vector.tensor_tensor(out=bdw[:, ch, 0:2], in0=bdw[:, ch, 0:2], in1=negEA, op=ALU.mult)
                    r.edge("vector", vc)
                    r.vector.tensor_copy(out=qr[:, ti % 2, :], in_=bank[:, 0:128])
                    r.vector.tensor_copy(out=ktok[:, ch, :], in_=bank[:, 128:256]).then_inc(vc, 1)
                    r.scalar.wait_ge(vc, r.val(vc))
                    r.scalar.activation(sqw2[:, 0:128], qr[:, ti % 2, :], AF.Square)
                    r.scalar.activation(sqw2[:, 128:256], ktok[:, ch, :], AF.Square).then_inc(sc, 1)
                    r.vector.wait_ge(sc, r.val(sc))
                    r.vector.tensor_reduce(out=bdw[:, ch, 4:8],
                                           in_=sqw2[:, :].rearrange("p (h c) -> p h c", h=4),
                                           axis=AX.X, op=ALU.add)
                    b3 = bdw[:, ch, 2:4].rearrange("p h -> p h ()")
                    b3.ap[-1] = [0, 64]
                    r.vector.tensor_tensor(out=vb[:, ch, :].rearrange("p (h c) -> p h c", h=2),
                                           in0=bank[:, 256:384].rearrange("p (h c) -> p h c", h=2),
                                           in1=b3, op=ALU.mult)
                    r.vector.tensor_copy(out=gtok[:, ch, :], in_=bank[:, 384:512]).then_inc(vc, 1)
                    pj_marks[("e", ti % 2)] = r.val(vc)
                    r.tensor.wait_ge(vc, r.val(vc))
                    r.tensor.transpose(ctx["ptr"][:, 0:128], qr[:, ti % 2, :], eye[:, :])
                    r.tensor.transpose(ctx["ptr"][:, 128:256], ktok[:, ch, :], eye[:, :])
                    r.mark("tensor", tc)
                    r.vector.wait_ge(tc, r.val(tc))
                    r.vector.tensor_copy(out=qT_w[:, ch * 128:(ch + 1) * 128], in_=ctx["ptr"][:, 0:128])
                    r.vector.tensor_copy(out=kT_w[:, ch * 128:(ch + 1) * 128], in_=ctx["ptr"][:, 128:256]).then_inc(vc, 1)
                    r.tensor.wait_ge(vc, r.val(vc))
            r.mark("vector", vc)
            r.scalar.wait_ge(vc, r.val(vc))
            r.scalar.activation(bdw[:, :, 4:6],
                                bdw[:, :, 4:6], AF.Ln, bias=ccol[:, 2:3])
            r.scalar.activation(bdw[:, :, 6:8],
                                bdw[:, :, 6:8], AF.Ln, bias=ccol[:, 2:3])
            r.scalar.activation(bdw[:, :, 8:10],
                                bdw[:, :, 2:4], AF.Ln).then_inc(sc, 1)
            r.vector.wait_ge(sc, r.val(sc))
            r.vector.tensor_scalar(out=bdw[:, :, 4:6], in0=bdw[:, :, 4:6], scalar1=-0.5,
                                   scalar2=-LOG_RTK, op0=ALU.mult, op1=ALU.add)
            r.vector.tensor_scalar(out=bdw[:, :, 6:8], in0=bdw[:, :, 6:8], scalar1=-0.5,
                                   scalar2=None, op0=ALU.mult)
            r.edge("vector", vc)
            r.vector.tensor_tensor(out=bdw[:, :, 8:10], in0=bdw[:, :, 8:10], in1=bdw[:, :, 6:8], op=ALU.add)
            r.mark("vector", vc)
            r.scalar.wait_ge(vc, r.val(vc))
            r.scalar.activation(bdw[:, :, 10:12],
                                bdw[:, :, 6:8], AF.Exp).then_inc(sc, 1)
            r.vector.wait_ge(sc, r.val(sc))

            r.vector.memset(Sf[:, :], 0.0)
            r.vector.tensor_copy(out=Sbf[:, :], in_=Sf[:, :]).then_inc(vc, 1)
            r.tensor.wait_ge(vc, r.val(vc))
            nch_run = 0 if A0_PHASE == "proj" else (NCH if A0_NCH is None else (A0_NCH + (1 if A0_PRESCAN else 0)))
            for ch in range(nch_run):
                prescan = (A0_NCH is not None) and A0_PRESCAN and (ch == A0_NCH)
                par = ch % 2
                bankA = pb[4]
                r.tensor.wait_ge(vc, r.val(vc))
                r.tensor.matmul(bankA[:, 0:2], ctx["inclTf_sb"][:, :], bdw[:, ch, 0:2], start=True, stop=True)
                r.tensor.matmul(bankA[:, 2:4], ctx["blksel_sb"][:, 0:128], bdw[:, ch, 0:2], start=True, stop=True)
                r.tensor.matmul(bankA[:, 4:6], ctx["blksel_sb"][:, 128:256], bdw[:, ch, 0:2], start=True, stop=True)
                r.mark("tensor", tc)
                r.vector.wait_ge(tc, r.val(tc))
                r.vector.tensor_copy(out=gwork[:, ch, 0:2], in_=bankA[:, 0:2])
                r.vector.tensor_tensor(out=hcol[:, 14:16], in0=bankA[:, 2:4],
                                       in1=bdw[:, ch, 6:8], op=ALU.add)
                r.mark("vector", vc)
                r.scalar.wait_ge(vc, r.val(vc))
                for h2 in range(2):
                    r.scalar.activation(gwork[:, ch, 2 + h2:3 + h2], bankA[:, 0 + h2:1 + h2],
                                        AF.Identity, bias=bdw[:, ch, 4 + h2:5 + h2])
                    r.scalar.activation(gwork[:, ch, 4 + h2:5 + h2], bankA[:, 0 + h2:1 + h2],
                                        AF.Identity, bias=bdw[:, ch, 8 + h2:9 + h2])
                    r.scalar.activation(gwork[:, ch, 6 + h2:7 + h2], bankA[:, 0 + h2:1 + h2],
                                        AF.Exp, scale=ccol[:, 6:7], bias=hcol[:, 14 + h2:15 + h2])
                r.scalar.activation(gwork[:, ch, 8:10], bankA[:, 2:4], AF.Exp)
                r.scalar.activation(gwork[:, ch, 10:12], bankA[:, 4:6], AF.Exp)
                r.mark("scalar", sc)
                if prescan and A0_CSTOP == "g":
                    continue
                r.tensor.wait_ge(sc, r.val(sc))
                bankB = pb[5]
                for j, col in enumerate((2, 3, 4, 5)):
                    r.tensor.transpose(bankB[0:1, j * 128:(j + 1) * 128],
                                       gwork[:, ch, col:col + 1], ctx["eyef"][:, :])
                r.mark("tensor", tc)
                r.vector.wait_ge(tc, r.val(tc))
                r.vector.tensor_copy(out=tabf[0:1, :], in_=bankB[0:1, 0:512]).then_inc(vc, 1)
                r.tensor.wait_ge(vc, r.val(vc))
                r.tensor.matmul(bankB[:, :], ctx["onesrow_f"][:, :], tabf[0:1, :], start=True, stop=True)
                r.mark("tensor", tc)
                r.vector.wait_ge(tc, r.val(tc))
                for blk in range(4):
                    gcol = gwork[:, ch, 0 + (blk % 2):1 + (blk % 2)]
                    r.vector.tensor_scalar(out=tabf[:, blk * 128:(blk + 1) * 128],
                                           in0=bankB[:, blk * 128:(blk + 1) * 128],
                                           scalar1=gcol, scalar2=20.0, op0=ALU.subtract, op1=ALU.min)
                r.mark("vector", vc)
                r.scalar.wait_ge(vc, r.val(vc))
                r.scalar.activation(ltab[:, 0, :], tabf[:, :], AF.Exp).then_inc(sc, 1)
                r.vector.wait_ge(sc, r.val(sc))
                r.vector.tensor_tensor(out=ltab[:, 0, :], in0=ltab[:, 0, :], in1=ctx["mskA"][:, :], op=ALU.mult)
                r.mark("vector", vc)
                tab_mark = r.val(vc)
                if prescan and A0_CSTOP == "tab":
                    continue
                r.tensor.wait_ge(vc, tab_mark)
                for h in range(2):
                    bk = pb[h]
                    r.tensor.matmul(bk[:, 0:128],
                                    kT_w[h * 64:(h + 1) * 64, ch * 128:(ch + 1) * 128],
                                    kT_w[h * 64:(h + 1) * 64, ch * 128:(ch + 1) * 128],
                                    start=True, stop=True)
                    r.tensor.matmul(bk[:, 128:256],
                                    kT_w[h * 64:(h + 1) * 64, ch * 128:(ch + 1) * 128],
                                    qT_w[h * 64:(h + 1) * 64, ch * 128:(ch + 1) * 128],
                                    start=True, stop=True)
                r.mark("tensor", tc)
                r.vector.wait_ge(tc, r.val(tc))
                for h in range(2):
                    skc = bdw[:, ch, 10 + h:11 + h]
                    r.vector.scalar_tensor_tensor(out=PA[:, 0, h, 0:128], in0=pb[h][:, 0:128],
                                                  scalar=skc, in1=ltab[:, 0, 256 + h * 128:256 + (h + 1) * 128],
                                                  op0=ALU.mult, op1=ALU.mult)
                    r.vector.scalar_tensor_tensor(out=attnT_r[:, 0, h, :], in0=pb[h][:, 128:256],
                                                  scalar=skc, in1=ltab[:, 0, h * 128:(h + 1) * 128],
                                                  op0=ALU.mult, op1=ALU.mult)
                r.edge("vector", vc)
                for h in range(2):
                    r.vector.tensor_tensor(out=PA[:, 0, h, 128:256], in0=PA[:, 0, h, 0:128],
                                           in1=eye[:, :], op=ALU.add)
                r.mark("vector", vc)
                pk_mark = r.val(vc)
                if prescan and A0_CSTOP == "kk":
                    continue
                r.mark("vector", vc)
                r.scalar.wait_ge(vc, r.val(vc))
                r.scalar.activation(hcol[:, 8:10], gwork[:, ch, 4:6], AF.Exp).then_inc(sc, 1)
                r.vector.wait_ge(sc, r.val(sc))
                for h in range(2):
                    r.vector.tensor_scalar(out=Wkb_r[:, 0, h, :], in0=ktok[:, ch, h * 64:(h + 1) * 64],
                                           scalar1=hcol[:, 8 + h:9 + h], scalar2=None, op0=ALU.mult)
                    r.vector.tensor_scalar(out=kd_r[:, 0, h, :], in0=ktok[:, ch, h * 64:(h + 1) * 64],
                                           scalar1=gwork[:, ch, 6 + h:7 + h], scalar2=None, op0=ALU.mult)
                r.mark("vector", vc)
                wkd_mark = r.val(vc)
                if prescan and A0_CSTOP == "wkd":
                    continue
                bE = {0: pb[2], 1: pb[3]}

                def _tr_pk():
                    for h in range(2):
                        r.tensor.transpose(ctx["ptr"][:, h * 128:(h + 1) * 128],
                                           PA[:, 0, h, 0:128], eye[:, :])
                    r.mark("tensor", tc)
                    r.vector.wait_ge(tc, r.val(tc))
                    for h in range(2):
                        r.vector.tensor_copy(out=Pk_r[:, 0, h, :],
                                             in_=ctx["ptr"][:, h * 128:(h + 1) * 128])
                    r.mark("vector", vc)
                    r.tensor.wait_ge(vc, r.val(vc))

                r.tensor.wait_ge(vc, pk_mark)
                _tr_pk()
                for h in range(2):
                    r.tensor.matmul(bE[h][:, 0:128], Pk_r[:, 0, h, :], PA[:, 0, h, 0:128],
                                    start=True, stop=True)
                    r.tensor.matmul(bE[h][:, 256:384], PA[:, 0, h, 0:128], Pk_r[:, 0, h, :],
                                    start=True, stop=True)
                r.mark("tensor", tc)
                r.vector.wait_ge(tc, r.val(tc))
                for h in range(2):
                    r.vector.tensor_copy(out=PA[:, 0, h, 0:128], in_=bE[h][:, 0:128])
                    r.vector.tensor_copy(out=Pk_r[:, 0, h, :], in_=bE[h][:, 256:384])
                r.mark("vector", vc)
                r.tensor.wait_ge(vc, r.val(vc))
                for it in range(5):
                    for h in range(2):
                        r.tensor.matmul(bE[h][:, 0:256], Pk_r[:, 0, h, :], PA[:, 0, h, :],
                                        start=True, stop=True)
                        if it < 4:
                            r.tensor.matmul(bE[h][:, 256:384], PA[:, 0, h, 0:128],
                                            Pk_r[:, 0, h, :], start=True, stop=True)
                    r.mark("tensor", tc)
                    r.vector.wait_ge(tc, r.val(tc))
                    for h in range(2):
                        r.vector.tensor_copy(out=PA[:, 0, h, 0:128], in_=bE[h][:, 0:128])
                        r.vector.tensor_tensor(out=PA[:, 0, h, 128:256], in0=PA[:, 0, h, 128:256],
                                               in1=bE[h][:, 128:256], op=ALU.add)
                        if it < 4:
                            r.vector.tensor_copy(out=Pk_r[:, 0, h, :], in_=bE[h][:, 256:384])
                    r.mark("vector", vc)
                    r.tensor.wait_ge(vc, r.val(vc))
                r.tensor.wait_ge(vc, wkd_mark)
                for h in range(2):
                    r.tensor.matmul(bE[h][:, 256:320], PA[:, 0, h, 128:256],
                                    vb[:, ch, h * 64:(h + 1) * 64], start=True, stop=True)
                    r.tensor.matmul(bE[h][0:64, 320:448], Wkb_r[:, 0, h, :], PA[:, 0, h, 128:256],
                                    start=True, stop=True)
                r.mark("tensor", tc)
                r.vector.wait_ge(tc, r.val(tc))
                for h in range(2):
                    r.vector.tensor_copy(out=vp_r[:, 0, h, :], in_=bE[h][:, 256:320])
                    r.vector.tensor_scalar(out=kcdT_r[h * 64:(h + 1) * 64, 0, :],
                                           in0=bE[h][0:64, 320:448], scalar1=-1.0, scalar2=None,
                                           op0=ALU.mult)
                r.mark("vector", vc)
                r.tensor.wait_ge(vc, r.val(vc))
                if not prescan:
                    r.mark("vector", vc)
                    r.scalar.wait_ge(vc, r.val(vc))
                    r.scalar.activation(hcol[:, 10:11], gwork[:, ch, 2:3], AF.Exp)
                    r.scalar.activation(hcol[:, 11:12], gwork[:, ch, 3:4], AF.Exp).then_inc(sc, 1)
                    r.scalar.wait_ge(sc, r.val(sc))
                    bV = {0: pb[2], 1: pb[3]}
                    bS = {0: pb[0], 1: pb[1]}
                    for s in range(2):
                        ss = slice(s * 64, (s + 1) * 64)
                        scol = s * 64
                        for h in range(2):
                            hs = slice(h * 64, (h + 1) * 64)
                            r.tensor.matmul(bV[h][0:64, 0:64], kcdT_r[hs, 0, scol:scol + 64],
                                            Sbf[hs, :], start=True, stop=True)
                            r.tensor.matmul(bV[h][0:64, 64:128],
                                            qT_w[hs, ch * 128 + scol: ch * 128 + scol + 64],
                                            Sbf[hs, :], start=True, stop=True)
                        r.mark("tensor", tc)
                        r.vector.wait_ge(tc, r.val(tc))
                        for h in range(2):
                            r.vector.tensor_tensor(out=vn_r[ss, 0, h, :], in0=bV[h][0:64, 0:64],
                                                   in1=vp_r[ss, 0, h, :], op=ALU.add)
                        r.mark("vector", vc)
                        r.tensor.wait_ge(vc, r.val(vc))
                        for h in range(2):
                            r.tensor.matmul(bS[h][0:64, 0:64], attnT_r[ss, 0, h, scol:scol + 64],
                                            vn_r[ss, 0, h, :], start=True, stop=True)
                            r.tensor.matmul(bS[h][0:64, 64:128], kd_r[ss, 0, h, :],
                                            vn_r[ss, 0, h, :], start=True, stop=True)
                        r.mark("tensor", tc)
                        r.scalar.wait_ge(tc, r.val(tc))
                        for h in range(2):
                            hs = slice(h * 64, (h + 1) * 64)
                            r.scalar.activation(ow[ss, ch, hs], bV[h][0:64, 64:128], AF.Identity,
                                                scale=hcol[ss, 10 + h:11 + h])
                        r.mark("scalar", sc)
                        r.vector.wait_ge(tc, r.val(tc))
                        r.vector.wait_ge(sc, r.val(sc))
                        for h in range(2):
                            hs = slice(h * 64, (h + 1) * 64)
                            r.vector.tensor_tensor(out=ow[ss, ch, hs], in0=bS[h][0:64, 0:64],
                                                   in1=ow[ss, ch, hs], op=ALU.add)
                        for h in range(2):
                            hs = slice(h * 64, (h + 1) * 64)
                            eglc = (8 + h) if s == h else (10 + h)
                            r.vector.scalar_tensor_tensor(out=Sf[hs, :], in0=Sf[hs, :],
                                                          scalar=gwork[hs, ch, eglc:eglc + 1],
                                                          in1=bS[h][0:64, 64:128], op0=ALU.mult, op1=ALU.add)
                        r.edge("vector", vc)
                        for h in range(2):
                            hs = slice(h * 64, (h + 1) * 64)
                            r.vector.tensor_copy(out=Sbf[hs, :], in_=Sf[hs, :])
                        r.mark("vector", vc)
                        r.tensor.wait_ge(vc, r.val(vc))
            ors_marks = {}
            for ch in range(NCH if A0_NCH is None else 0):
                r.scalar.wait_ge(vc, r.val(vc))
                r.scalar.activation(sqw2[:, 0:128], ow[:, ch, :], AF.Square).then_inc(sc, 1)
                r.vector.wait_ge(sc, r.val(sc))
                r.vector.tensor_reduce(out=hcol[:, 12:14], in_=sqw2[:, 0:128].rearrange("p (h c) -> p h c", h=2),
                                       axis=AX.X, op=ALU.add)
                r.edge("vector", vc)
                r.vector.tensor_scalar(out=hcol[:, 12:14], in0=hcol[:, 12:14], scalar1=1.0 / 64.0,
                                       scalar2=None, op0=ALU.mult)
                r.mark("vector", vc)
                r.scalar.wait_ge(vc, r.val(vc))
                r.scalar.activation(hcol[:, 12:14], hcol[:, 12:14], AF.Sqrt, bias=ccol[:, 0:1]).then_inc(sc, 1)
                r.vector.wait_ge(sc, r.val(sc))
                r.vector.reciprocal(out=hcol[:, 12:14], in_=hcol[:, 12:14])
                r.edge("vector", vc)
                rinv = hcol[:, 12:14].rearrange("p h -> p h ()")
                rinv.ap[-1] = [0, 64]
                r.vector.tensor_tensor(out=ow[:, ch, :].rearrange("p (h c) -> p h c", h=2),
                                       in0=ow[:, ch, :].rearrange("p (h c) -> p h c", h=2),
                                       in1=rinv, op=ALU.mult)
                r.edge("vector", vc)
                r.vector.tensor_tensor(out=ow[:, ch, :], in0=ow[:, ch, :], in1=normw_c[:, :], op=ALU.mult)
                r.mark("vector", vc)
                r.scalar.wait_ge(vc, r.val(vc))
                r.scalar.activation(sqw2[:, 0:128], gtok[:, ch, :], AF.Silu).then_inc(sc, 1)
                r.vector.wait_ge(sc, r.val(sc))
                r.vector.tensor_tensor(out=ow[:, ch, :], in0=ow[:, ch, :], in1=sqw2[:, 0:128], op=ALU.mult).then_inc(vc, 1)
                r.tensor.wait_ge(vc, r.val(vc))
                r.tensor.transpose(ctx["ptr"][:, 0:128], ow[:, ch, :], eye[:, :])
                r.mark("tensor", tc)
                r.vector.wait_ge(tc, r.val(tc))
                r.vector.tensor_copy(out=oft[:, ch % 2, :], in_=ctx["ptr"][:, 0:128]).then_inc(vc, 1)
                oft_mark = r.val(vc)
                for ds in range(2):
                    bank = pb[ds]
                    slot = (ch % 2) * 2 + ds
                    r.tensor.wait_ge(vc, oft_mark)
                    r.tensor.wait_ge(vc, ors_marks.get(("v", slot), 0))
                    r.tensor.matmul(bank[:, :], oft[:, ch % 2, :], womy_sb[:, 0, ds * 512:(ds + 1) * 512],
                                    start=True, stop=True)
                    r.mark("tensor", tc)
                    r.vector.wait_ge(tc, r.val(tc))
                    r.vector.wait_ge(st if w == 0 else gld, ors_marks.get(("d", slot), 0))
                    dstslot = ctx["xnorm"][:, ch % 2, ds * 512:(ds + 1) * 512]
                    r.vector.tensor_copy(out=dstslot, in_=bank[:, :]).then_inc(vc, 1)
                    ors_marks[("v", slot)] = r.val(vc)
                    dst = idram["ors_contrib"][ch * 128:(ch + 1) * 128, ds * 512:(ds + 1) * 512]
                    if w == 0:
                        r.sync.wait_ge(vc, r.val(vc))
                        r.sync.dma_start(out=dst, in_=dstslot).then_inc(st, 16)
                        ors_marks[("d", slot)] = r.val(st)
                    else:
                        r.gpsimd.wait_ge(vc, r.val(vc))
                        r.gpsimd.dma_start(out=dst, in_=dstslot, accum_op=ALU.add).then_inc(gld, 16)
                        ors_marks[("d", slot)] = r.val(gld)
            r.sync.wait(st)
            r.gpsimd.wait(gld)
            r.barrier(ctx["bar_sems"])
        if A0_NCH is not None:
            nr = A0_NCH + (1 if A0_PRESCAN else 0)
            for nm, ap in (("Sf", Sf[:, :]), ("Sbf", Sbf[:, :]), ("kcdT_r", kcdT_r[:, :, :]),
                           ("vp_r", vp_r[:, :, :, :]), ("vn_r", vn_r[:, :, :, :]),
                           ("bdw", bdw[:, :, :]), ("gwork", gwork[:, 0:nr, :]),
                           ("ktok", ktok[:, :, :]), ("vbd", vb[:, :, :]),
                           ("attnT_r", attnT_r[:, :, :, :]), ("PA", PA[:, :, :, :]),
                           ("Pk_r", Pk_r[:, :, :, :]), ("ltab", ltab[:, :, :]),
                           ("hcol", hcol[:, 0:12]), ("owd", ow[:, 0:max(A0_NCH, 1), :]),
                           ("kd_r", kd_r[:, :, :, :]), ("Wkb_r", Wkb_r[:, :, :, :])):
                DBG[nm] = ap
            _es0.close()
            return
        r.gpsimd.collective_compute("ReduceScatter", ALU.add, replica_groups=ctx["GROUPS"],
                                    ins=[idram["ors_contrib"]], outs=[idram["ors_out"]]).then_inc(cc, 1)
        r.sync.wait_ge(cc, r.val(cc))
        r.vector.wait_ge(cc, r.val(cc))
        for t in range(NT):
            stag = xgb[:, t % 2, 0:4, :].rearrange("p a b -> p (a b)")
            r.sync.wait_ge(vc, r.val(vc) if t >= 2 else 0)
            r.sync.dma_start(out=stag, in_=idram["ors_out"][t * 128:(t + 1) * 128, :]).then_inc(ld, 16)
            r.vector.wait(ld)
            r.vector.tensor_tensor(out=resid[:, t, :], in0=resid[:, t, :], in1=stag, op=ALU.add).then_inc(vc, 1)
    _es0.close()
    r.tensor.wait_ge(vc, r.val(vc))


def emit_attn1(ctx):
    r = ctx["r"]; nc = ctx["nc"]; es = ctx["es"]; dram = ctx["dram"]; idram = ctx["idram"]
    pb = ctx["pb"]; resid = ctx["resid"]; eye = ctx["eye"]; ccol = ctx["ccol"]
    ld, gld, tc, vc, sc, gc, cc = (ctx["sems"][k] for k in ("ld", "gld", "tc", "vc", "sc", "gc", "cc"))
    _TC[0] = tc
    lam = ctx["lam"]; cmask = ctx["cmask"]
    TWO_PI = 2.0 * np.pi

    gbd = None
    FcT = es.enter_context(nc.sbuf_tensor("FcT", [128, 4, 512], BF16))
    FsT = es.enter_context(nc.sbuf_tensor("FsT", [128, 4, 512], BF16))
    parn_bf = es.enter_context(nc.sbuf_tensor("parn_bf", [128, 4, 128], BF16))
    invfix = es.enter_context(nc.sbuf_tensor("invfix_s", [1, 1024], BF16))
    Rr = es.enter_context(nc.sbuf_tensor("Rr", [128, 4, 516], BF16))
    Rni = es.enter_context(nc.sbuf_tensor("Rni", [128, 4, 516], BF16))
    ATr = es.enter_context(nc.sbuf_tensor("ATr", [128, 16, 128], BF16))
    ATi = es.enter_context(nc.sbuf_tensor("ATi", [128, 16, 128], BF16))
    Cr = es.enter_context(nc.sbuf_tensor("Cr", [128, 4, D], BF16))
    Ci = es.enter_context(nc.sbuf_tensor("Ci", [128, 4, D], BF16))
    Ktr = es.enter_context(nc.sbuf_tensor("Ktr", [128, 4, D], BF16))
    Kti = es.enter_context(nc.sbuf_tensor("Kti", [128, 4, D], BF16))
    lcol = es.enter_context(nc.sbuf_tensor("lcol", [128, 16], F32))
    cmask_rep = es.enter_context(nc.sbuf_tensor("cmask_rep", [128, 16], F32))

    with nc.sbuf_tensor("ones_row", [1, 128], F32) as ones_row:
        r.vector.memset(ones_row[:, :], 1.0)
        r.mark("vector", vc)
        r.tensor.wait_ge(vc, r.val(vc))
        r.tensor.matmul(pb[5][0:128, 0:16], ones_row[:, :], cmask[:, :], start=True, stop=True)
        r.mark("tensor", tc)
        r.vector.wait_ge(tc, r.val(tc))
        r.vector.tensor_copy(out=cmask_rep[:, :], in_=pb[5][0:128, 0:16]).then_inc(vc, 1)
        r.tensor.wait_ge(vc, r.val(vc))

    u = ctx["xT"][:, :, :].rearrange("p a b -> p (a b)").rearrange("p (t d) -> p t d", t=NT)
    with nc.sbuf_tensor("gb_tmp", [128, 2, D], F32) as gb_tmp:
        ctx["gbd_rep"] = gb_tmp
        r.sync.dma_start(out=gb_tmp[:, :, :], in_=dram["gbd_rep"].rearrange("p r d -> p r d")[:, 0:2, :]).then_inc(ld, 16)
        r.vector.wait(ld)
        r.scalar.wait(ld)
        emit_ln(ctx, gamma_beta=True, tokmajor_out=u, want_T=False)
        r.barrier(ctx["bar_sems"])

    with nc.sbuf_tensor("prod_sb", [128, 4, 512], F32) as prod_sb, \
         nc.sbuf_tensor("ttmp", [128, 4, 512], F32) as ttmp, \
         nc.sbuf_tensor("tind", [128, 512], F32) as tind:
        ttmp2 = None
        r.sync.dma_start(out=prod_sb[:, :, :], in_=dram["prod"].rearrange("(mt p) f -> p mt f", p=128)).then_inc(ld, 16)
        r.vector.wait(ld)
        r.scalar.wait(ld)
        def red_steps(buf3, ks):
            for mt in range(4):
                buf = buf3[:, mt, :]
                for (th, am) in ks:
                    r.vector.tensor_scalar(out=tind[:, :], in0=buf, scalar1=float(th), scalar2=float(-am),
                                           op0=ALU.is_ge, op1=ALU.mult)
                    r.edge("vector", vc)
                    r.vector.tensor_tensor(out=buf, in0=buf, in1=tind[:, :], op=ALU.add)
                    r.edge("vector", vc)

        def trig_table(dst, shift, scale_col, fixup=None):
            r.vector.tensor_scalar(out=ttmp[:, :, :], in0=prod_sb[:, :, :], scalar1=1.0 / 1024.0,
                                   scalar2=shift, op0=ALU.mult, op1=ALU.add)
            r.edge("vector", vc)
            red_steps(ttmp[:, :, :], [(1.0, 1.0), (0.5, 1.0)])
            r.mark("vector", vc)
            r.scalar.wait_ge(vc, r.val(vc))
            r.scalar.activation(dst, ttmp[:, :, :], AF.Sin, scale=ccol[:, 5:6]).then_inc(sc, 1)
            r.vector.wait_ge(sc, r.val(sc))
            if scale_col != 1.0:
                r.vector.tensor_scalar(out=dst, in0=dst, scalar1=scale_col, scalar2=None, op0=ALU.mult)
            r.edge("vector", vc)
        r.vector.memset(ccol[:, 5:6], TWO_PI)
        r.edge("vector", vc)
        trig_table(FcT[:, :, :], 0.25, 1.0)
        trig_table(FsT[:, :, :], 0.0, 1.0)
        r.gpsimd.dma_start(out=parn_bf[:, :, :], in_=dram["parn"]).then_inc(gld, 16)
        r.gpsimd.dma_start(out=invfix[:, :], in_=dram["invfix"]).then_inc(gld, 16)
        r.vector.wait(gld)

    r.barrier(ctx["bar_sems"])
    with nc.sbuf_tensor("Rf", [128, 4, 516, 2], F32) as Rf, \
         nc.sbuf_tensor("ttmpB", [128, 4, 512], F32) as ttmp, \
         nc.sbuf_tensor("bf_tmp", [128, 128], BF16) as bf_tmp:
        r.tensor.wait(ld)
        r.tensor.wait(gld)
        for ntile in range(4):
            r.tensor.transpose(pb[5][:, ntile * 4:(ntile + 1) * 4],
                               lam[0:4, ntile * 128:(ntile + 1) * 128],
                               ctx["eyef"][0:4, 0:4])
            r.mark("tensor", tc)
        r.vector.wait_ge(tc, r.val(tc))
        lamT = pb[5]
        for ntile in range(4):
            r.scalar.wait_ge(tc, r.val(tc))
            r.scalar.activation(lcol[:, 2 * ntile:2 * ntile + 1], lamT[:, ntile * 4 + 0: ntile * 4 + 1], AF.Exp)
            r.scalar.activation(lcol[:, 2 * ntile + 1:2 * ntile + 2], lamT[:, ntile * 4 + 1: ntile * 4 + 2], AF.Exp)
        r.mark("scalar", sc)
        r.vector.wait_ge(sc, r.val(sc))
        r.vector.memset(Rf[:, :, 0, 0:1], 1.0)
        r.vector.memset(Rf[:, :, 0, 1:2], 0.0)
        for ntile in range(4):
            bcol = lcol[:, 2 * ntile + 1:2 * ntile + 2]
            acol = lcol[:, 2 * ntile:2 * ntile + 1]
            r.vector.tensor_scalar(out=lcol[:, 10:11], in0=acol, scalar1=-1.0, scalar2=None, op0=ALU.mult)
            r.mark("vector", vc)
            r.scalar.wait_ge(vc, r.val(vc))
            r.scalar.activation(lcol[:, 11:12], lcol[:, 10:11], AF.Exp).then_inc(sc, 1)
            r.vector.wait_ge(sc, r.val(sc))
            r.vector.tensor_scalar(out=lcol[:, 12:13], in0=bcol, scalar1=1.0 / TWO_PI,
                                   scalar2=None, op0=ALU.mult)
            r.vector.tensor_scalar(out=lcol[:, 13:14], in0=bcol, scalar1=1.0 / TWO_PI,
                                   scalar2=0.25, op0=ALU.mult, op1=ALU.add)
            r.edge("vector", vc)
            for cidx in (12, 13):
                buf = lcol[:, cidx:cidx + 1]
                for (kk, am) in ((128.0, 128.0), (64.0, 64.0), (32.0, 32.0), (16.0, 16.0), (8.0, 8.0), (4.0, 4.0), (2.0, 2.0), (1.0, 1.0), (0.5, 1.0)):
                    r.vector.tensor_scalar(out=lcol[:, 10:11], in0=buf, scalar1=kk, scalar2=-am,
                                           op0=ALU.is_ge, op1=ALU.mult)
                    r.edge("vector", vc)
                    r.vector.tensor_tensor(out=buf, in0=buf, in1=lcol[:, 10:11], op=ALU.add)
                    r.edge("vector", vc)
            r.mark("vector", vc)
            r.scalar.wait_ge(vc, r.val(vc))
            r.scalar.activation(lcol[:, 14:15], lcol[:, 12:13], AF.Sin, scale=ccol[:, 5:6])
            r.scalar.activation(lcol[:, 15:16], lcol[:, 13:14], AF.Sin, scale=ccol[:, 5:6]).then_inc(sc, 1)
            r.vector.wait_ge(sc, r.val(sc))
            r.vector.tensor_tensor(out=Rf[:, ntile, 1, 0:1], in0=lcol[:, 11:12], in1=lcol[:, 15:16], op=ALU.mult)
            r.vector.tensor_tensor(out=Rf[:, ntile, 1, 1:2], in0=lcol[:, 11:12], in1=lcol[:, 14:15], op=ALU.mult)
            r.edge("vector", vc)
        n_have = 2
        while n_have < 516:
            n_new = min(n_have, 516 - n_have)
            for ntile in range(4):
                pr = Rf[:, ntile, n_have, 0:1]
                pi = Rf[:, ntile, n_have, 1:2]
                a_r = Rf[:, ntile, n_have - 1, 0:1]; a_i = Rf[:, ntile, n_have - 1, 1:2]
                b_r = Rf[:, ntile, 1, 0:1]; b_i = Rf[:, ntile, 1, 1:2]
                r.vector.tensor_tensor(out=lcol[:, 10:11], in0=a_r, in1=b_r, op=ALU.mult)
                r.vector.tensor_tensor(out=lcol[:, 11:12], in0=a_i, in1=b_i, op=ALU.mult)
                r.vector.tensor_tensor(out=lcol[:, 12:13], in0=a_r, in1=b_i, op=ALU.mult)
                r.vector.tensor_tensor(out=lcol[:, 13:14], in0=a_i, in1=b_r, op=ALU.mult)
                r.edge("vector", vc)
                r.vector.tensor_tensor(out=lcol[:, 10:11], in0=lcol[:, 10:11], in1=lcol[:, 11:12], op=ALU.subtract)
                r.vector.tensor_tensor(out=lcol[:, 11:12], in0=lcol[:, 12:13], in1=lcol[:, 13:14], op=ALU.add)
                r.edge("vector", vc)
                s0 = Rf[:, ntile, 0:n_new, 0:1]; s1 = Rf[:, ntile, 0:n_new, 1:2]
                d0 = Rf[:, ntile, n_have:n_have + n_new, 0:1]
                d1 = Rf[:, ntile, n_have:n_have + n_new, 1:2]
                r.vector.tensor_scalar(out=ttmp[:, ntile, 0:n_new], in0=s1, scalar1=lcol[:, 11:12], scalar2=None, op0=ALU.mult)
                r.vector.tensor_scalar(out=ttmp[:, (ntile + 1) % 4, 0:n_new], in0=s1, scalar1=lcol[:, 10:11], scalar2=None, op0=ALU.mult)
                r.edge("vector", vc)
                r.vector.scalar_tensor_tensor(out=d0, in0=s0, scalar=lcol[:, 10:11], in1=ttmp[:, ntile, 0:n_new],
                                              op0=ALU.mult, op1=ALU.subtract)
                r.vector.scalar_tensor_tensor(out=d1, in0=s0, scalar=lcol[:, 11:12], in1=ttmp[:, (ntile + 1) % 4, 0:n_new],
                                              op0=ALU.mult, op1=ALU.add)
                r.edge("vector", vc)
            n_have += n_new
        for ntile in range(4):
            r.vector.tensor_scalar(out=lcol[:, 2 * ntile:2 * ntile + 1], in0=Rf[:, ntile, 512, 0:1],
                                   scalar1=1.0, scalar2=None, op0=ALU.mult)
            r.vector.tensor_scalar(out=lcol[:, 2 * ntile + 1:2 * ntile + 2], in0=Rf[:, ntile, 512, 1:2],
                                   scalar1=-1.0, scalar2=None, op0=ALU.mult)
        r.edge("vector", vc)
        r.vector.tensor_copy(out=Rr[:, :, :], in_=Rf[:, :, :, 0])
        r.vector.tensor_scalar(out=Rni[:, :, :], in0=Rf[:, :, :, 1], scalar1=-1.0, scalar2=None, op0=ALU.mult)
        r.edge("vector", vc)
        for comp in range(2):
            dstT = ATr if comp == 0 else ATi
            for ntile in range(4):
                for mt in range(4):
                    lo = 511 - (mt * 128 + 127)
                    seg = Rf[:, ntile, lo:lo + 128, comp]
                    rv = rev_slice(seg, 0, 128)
                    r.vector.tensor_copy(out=bf_tmp[:, :], in_=rv).then_inc(vc, 1)
                    r.tensor.wait_ge(vc, r.val(vc))
                    r.tensor.transpose(ctx["ptr2"][:, 0:128], bf_tmp[:, :], eye[:, :])
                    r.mark("tensor", tc)
                    r.vector.wait_ge(tc, r.val(tc))
                    r.vector.tensor_copy(out=dstT[:, mt * 4 + ntile, :], in_=ctx["ptr2"][:, 0:128]).then_inc(vc, 1)
                    r.tensor.wait_ge(vc, r.val(vc))
        r.gpsimd.dma_start(out=Cr[:, :, :], in_=dram["crT"].rearrange("(nt p) d -> p nt d", p=128)).then_inc(gld, 16)
        r.gpsimd.dma_start(out=Ci[:, :, :], in_=dram["ciT"].rearrange("(nt p) d -> p nt d", p=128)).then_inc(gld, 16)
        for ntile in range(4):
            pass
        r.vector.wait(gld)
        for ntile in range(4):
            r.tensor.wait_ge(vc, r.val(vc))
            r.tensor.transpose(pb[5][:, ntile * 4:(ntile + 1) * 4],
                               lam[0:4, ntile * 128:(ntile + 1) * 128],
                               ctx["eyef"][0:4, 0:4])
            r.mark("tensor", tc)
        r.vector.wait_ge(tc, r.val(tc))
        for ntile in range(4):
            lr = pb[5][:, ntile * 4 + 0:ntile * 4 + 1]
            li = pb[5][:, ntile * 4 + 1:ntile * 4 + 2]
            r.scalar.wait_ge(vc, r.val(vc))
            r.scalar.activation(lcol[:, 8:9], lr, AF.Exp)
            r.scalar.activation(lcol[:, 9:10], li, AF.Exp).then_inc(sc, 1)
            r.vector.wait_ge(sc, r.val(sc))
            r.vector.tensor_scalar(out=lcol[:, 8:9], in0=lcol[:, 8:9], scalar1=-1.0, scalar2=None, op0=ALU.mult)
            r.edge("vector", vc)
            r.vector.tensor_tensor(out=lcol[:, 10:11], in0=lcol[:, 8:9], in1=lcol[:, 8:9], op=ALU.mult)
            r.vector.tensor_tensor(out=lcol[:, 11:12], in0=lcol[:, 9:10], in1=lcol[:, 9:10], op=ALU.mult)
            r.edge("vector", vc)
            r.vector.tensor_tensor(out=lcol[:, 10:11], in0=lcol[:, 10:11], in1=lcol[:, 11:12], op=ALU.add)
            r.edge("vector", vc)
            r.vector.reciprocal(out=lcol[:, 10:11], in_=lcol[:, 10:11])
            r.edge("vector", vc)
            r.vector.tensor_scalar(out=lcol[:, 11:12], in0=Rf[:, ntile, 1, 0:1], scalar1=1.0, scalar2=None, op0=ALU.subtract)
            r.edge("vector", vc)
            ni = Rf[:, ntile, 1, 1:2]
            r.vector.tensor_tensor(out=lcol[:, 12:13], in0=lcol[:, 11:12], in1=lcol[:, 8:9], op=ALU.mult)
            r.vector.tensor_tensor(out=lcol[:, 13:14], in0=ni, in1=lcol[:, 9:10], op=ALU.mult)
            r.vector.tensor_tensor(out=lcol[:, 14:15], in0=ni, in1=lcol[:, 8:9], op=ALU.mult)
            r.vector.tensor_tensor(out=lcol[:, 15:16], in0=lcol[:, 11:12], in1=lcol[:, 9:10], op=ALU.mult)
            r.edge("vector", vc)
            r.vector.tensor_tensor(out=lcol[:, 12:13], in0=lcol[:, 12:13], in1=lcol[:, 13:14], op=ALU.add)
            r.vector.tensor_tensor(out=lcol[:, 14:15], in0=lcol[:, 14:15], in1=lcol[:, 15:16], op=ALU.subtract)
            r.edge("vector", vc)
            r.vector.tensor_scalar(out=lcol[:, 12:13], in0=lcol[:, 12:13], scalar1=lcol[:, 10:11], scalar2=None, op0=ALU.mult)
            r.vector.tensor_scalar(out=lcol[:, 14:15], in0=lcol[:, 14:15], scalar1=lcol[:, 10:11], scalar2=None, op0=ALU.mult)
            r.edge("vector", vc)
            for dh in range(2):
                csl = slice(dh * 512, (dh + 1) * 512)
                r.vector.tensor_scalar(out=ttmp[:, 0, :], in0=Ci[:, ntile, csl], scalar1=lcol[:, 14:15], scalar2=None, op0=ALU.mult)
                r.vector.tensor_scalar(out=ttmp[:, 1, :], in0=Cr[:, ntile, csl], scalar1=lcol[:, 14:15], scalar2=None, op0=ALU.mult)
                r.edge("vector", vc)
                r.vector.scalar_tensor_tensor(out=Cr[:, ntile, csl], in0=Cr[:, ntile, csl], scalar=lcol[:, 12:13],
                                              in1=ttmp[:, 0, :], op0=ALU.mult, op1=ALU.subtract)
                r.vector.scalar_tensor_tensor(out=Ci[:, ntile, csl], in0=Ci[:, ntile, csl], scalar=lcol[:, 12:13],
                                              in1=ttmp[:, 1, :], op0=ALU.mult, op1=ALU.add)
                r.edge("vector", vc)
    r.barrier(ctx["bar_sems"])
    with nc.sbuf_tensor("Ksb", [128, 4, D], BF16) as Ksb:

        r.tensor.wait_ge(vc, r.val(vc))
        for jt in range(4):
            for dh in range(2):
                bank = pb[dh]
                for ntile in range(4):
                    r.tensor.matmul(bank[:, :], Rr[:, ntile, jt * 128:(jt + 1) * 128],
                                    Cr[:, ntile, dh * 512:(dh + 1) * 512],
                                    start=(ntile == 0), stop=False)
                for ntile in range(4):
                    r.tensor.matmul(bank[:, :], Rni[:, ntile, jt * 128:(jt + 1) * 128],
                                    Ci[:, ntile, dh * 512:(dh + 1) * 512],
                                    start=False, stop=(ntile == 3))
                r.mark("tensor", tc)
                r.vector.wait_ge(tc, r.val(tc))
                r.vector.tensor_copy(out=Ksb[:, jt, dh * 512:(dh + 1) * 512], in_=bank[:, :]).then_inc(vc, 1)
                r.tensor.wait_ge(vc, r.val(vc))
        for ft in range(4):
            for comp in range(2):
                FT = FcT if comp == 0 else FsT
                KT = Ktr if comp == 0 else Kti
                for dh in range(2):
                    bank = pb[2 + (dh % 2)]
                    knyq = (comp == 1 and ft == 0)
                    for mt in range(4):
                        r.tensor.matmul(bank[:, :], FT[:, mt, ft * 128:(ft + 1) * 128],
                                        Ksb[:, mt, dh * 512:(dh + 1) * 512],
                                        start=(mt == 0), stop=(mt == 3 and not knyq))
                    if knyq:
                        for mt in range(4):
                            r.tensor.matmul(bank[:, :], parn_bf[:, mt, :],
                                            Ksb[:, mt, dh * 512:(dh + 1) * 512],
                                            start=False, stop=(mt == 3))
                    r.mark("tensor", tc)
                    r.vector.wait_ge(tc, r.val(tc))
                    r.vector.tensor_scalar(out=KT[:, ft, dh * 512:(dh + 1) * 512], in0=bank[:, :],
                                           scalar1=2.0 / 1024.0, scalar2=None, op0=ALU.mult).then_inc(vc, 1)
                    r.tensor.wait_ge(vc, r.val(vc))


    if DBG_ON:
        for nm, ap in [("FcT", FcT), ("FsT", FsT), ("Ktr", Ktr), ("Kti", Kti), ("Rr", Rr), ("Cr", Cr), ("Ci", Ci)]:
            DBG[nm] = ap[:, :, :]
    r.barrier(ctx["bar_sems"])

    with nc.sbuf_tensor("Zr", [128, 4, D], BF16) as Zr, \
         nc.sbuf_tensor("Zi", [128, 4, D], BF16) as Zi, \
         nc.sbuf_tensor("Pr", [128, 4, D], BF16) as Pr, \
         nc.sbuf_tensor("Pi", [128, 4, D], BF16) as Pi, \
         nc.sbuf_tensor("vt2", [128, D], BF16) as vt2, \
         nc.sbuf_tensor("pd_s", [128, D], F32) as pd:
        r.sync.dma_start(out=pd[:, :], in_=dram["gbd_rep"][:, 2, :]).then_inc(ld, 16)
        vt1 = ctx["sq_scr"]
        Usc = ctx["xnorm"]
        r.vector.memset(Zr[:, :, :], 0.0)
        r.vector.memset(Zi[:, :, :], 0.0)
        r.edge("vector", vc)
        inj_free = {}

        def injections(c):
            for comp in range(2):
                AT = ATr if comp == 0 else ATi
                E = Pr if comp == 0 else Pi
                for ntile in range(4):
                    for dh in range(2):
                        bid = (ntile * 2 + dh) % 4
                        bank = pb[bid]
                        r.tensor.wait_ge(vc, inj_free.get(bid, 0))
                        for mt in range(4):
                            r.tensor.matmul(bank[:, :], AT[:, mt * 4 + ntile, :],
                                            u[:, c * 4 + mt, dh * 512:(dh + 1) * 512],
                                            start=(mt == 0), stop=(mt == 3))
                        r.mark("tensor", tc)
                        r.vector.wait_ge(tc, r.val(tc))
                        r.vector.tensor_copy(out=E[:, ntile, dh * 512:(dh + 1) * 512], in_=bank[:, :]).then_inc(vc, 1)
                        inj_free[bid] = r.val(vc)

        def z_update2():
            for ntile in range(4):
                lr_ = lcol[:, 2 * ntile:2 * ntile + 1]
                lni = lcol[:, 2 * ntile + 1:2 * ntile + 2]
                zr = Zr[:, ntile, :]; zi = Zi[:, ntile, :]
                r.vector.tensor_scalar(out=vt1[:, :], in0=zi, scalar1=lni, scalar2=None, op0=ALU.mult)
                r.edge("vector", vc)
                r.vector.tensor_scalar(out=zi, in0=zi, scalar1=lr_, scalar2=None, op0=ALU.mult)
                r.edge("vector", vc)
                r.vector.scalar_tensor_tensor(out=zi, in0=zr, scalar=lni, in1=zi, op0=ALU.mult, op1=ALU.subtract)
                r.edge("vector", vc)
                r.vector.tensor_scalar(out=zi, in0=zi, scalar1=-1.0, scalar2=None, op0=ALU.mult)
                r.vector.scalar_tensor_tensor(out=zr, in0=zr, scalar=lr_, in1=vt1[:, :], op0=ALU.mult, op1=ALU.add)
                r.edge("vector", vc)
                er = Pr[:, ntile, :]; ei = Pi[:, ntile, :]
                r.vector.tensor_tensor(out=vt1[:, :], in0=Cr[:, ntile, :], in1=er, op=ALU.mult)
                r.vector.tensor_tensor(out=vt2[:, :], in0=Ci[:, ntile, :], in1=ei, op=ALU.mult)
                r.edge("vector", vc)
                r.vector.tensor_tensor(out=zr, in0=zr, in1=vt1[:, :], op=ALU.add)
                r.edge("vector", vc)
                r.vector.tensor_tensor(out=zr, in0=zr, in1=vt2[:, :], op=ALU.subtract)
                r.vector.tensor_tensor(out=vt1[:, :], in0=Cr[:, ntile, :], in1=ei, op=ALU.mult)
                r.vector.tensor_tensor(out=vt2[:, :], in0=Ci[:, ntile, :], in1=er, op=ALU.mult)
                r.edge("vector", vc)
                r.vector.tensor_tensor(out=zi, in0=zi, in1=vt1[:, :], op=ALU.add)
                r.edge("vector", vc)
                r.vector.tensor_tensor(out=zi, in0=zi, in1=vt2[:, :], op=ALU.add)
                r.edge("vector", vc)

        for c in range(4):
            injections(c)
            r.vector.wait_ge(vc, r.val(vc))
            z_update2()

        r.mark("vector", vc)
        if DBG_ON:
            _dz1r = ctx["nc"].dram_tensor("dbg_Z1r", [128, 4, D], F32, kind="ExternalOutput").ap()
            _dz1i = ctx["nc"].dram_tensor("dbg_Z1i", [128, 4, D], F32, kind="ExternalOutput").ap()
            r.gpsimd.wait_ge(vc, r.val(vc))
            r.gpsimd.dma_start(out=_dz1r, in_=Zr[:, :, :]).then_inc(gld, 16)
            r.gpsimd.dma_start(out=_dz1i, in_=Zi[:, :, :]).then_inc(gld, 16)
        r.sync.wait(gld)
        r.sync.wait_ge(vc, r.val(vc))
        r.sync.dma_start(out=idram["s_contrib"][:, 0:D].rearrange("(nt p) d -> p nt d", p=128),
                         in_=Zr[:, :, :]).then_inc(ctx["sems"]["st"], 16)
        r.sync.dma_start(out=idram["s_contrib"][:, D:2 * D].rearrange("(nt p) d -> p nt d", p=128),
                         in_=Zi[:, :, :]).then_inc(ctx["sems"]["st"], 16)
        r.gpsimd.wait(ctx["sems"]["st"])
        r.gpsimd.collective_compute("AllGather", ALU.bypass, replica_groups=ctx["GROUPS"],
                                    ins=[idram["s_contrib"]], outs=[idram["s_gather"]]).then_inc(cc, 1)
        r.gpsimd.wait_ge(cc, r.val(cc))
        r.vector.wait_ge(cc, r.val(cc))
        r.vector.memset(Zr[:, :, :], 0.0)
        r.vector.memset(Zi[:, :, :], 0.0)
        r.mark("vector", vc)
        for p_ in range(4):
            r.gpsimd.wait_ge(vc, r.val(vc))
            r.gpsimd.dma_start(out=Pr[:, :, :], in_=idram["s_gather"][p_ * 512:(p_ + 1) * 512, 0:D]
                               .rearrange("(nt p) d -> p nt d", p=128)).then_inc(gld, 16)
            r.gpsimd.dma_start(out=Pi[:, :, :], in_=idram["s_gather"][p_ * 512:(p_ + 1) * 512, D:2 * D]
                               .rearrange("(nt p) d -> p nt d", p=128)).then_inc(gld, 16)
            r.vector.wait(gld)
            for ntile in range(4):
                a_r = lcol[:, 2 * ntile:2 * ntile + 1]; a_ni = lcol[:, 2 * ntile + 1:2 * ntile + 2]
                r.vector.tensor_tensor(out=lcol[:, 8:9], in0=a_r, in1=a_r, op=ALU.mult)
                r.vector.tensor_tensor(out=lcol[:, 9:10], in0=a_ni, in1=a_ni, op=ALU.mult)
                r.vector.tensor_tensor(out=lcol[:, 10:11], in0=a_r, in1=a_ni, op=ALU.mult)
                r.edge("vector", vc)
                r.vector.tensor_tensor(out=lcol[:, 8:9], in0=lcol[:, 8:9], in1=lcol[:, 9:10], op=ALU.subtract)
                r.vector.tensor_scalar(out=lcol[:, 9:10], in0=lcol[:, 10:11], scalar1=2.0, scalar2=None, op0=ALU.mult)
                r.edge("vector", vc)
                r.vector.tensor_tensor(out=lcol[:, 10:11], in0=lcol[:, 8:9], in1=lcol[:, 8:9], op=ALU.mult)
                r.vector.tensor_tensor(out=lcol[:, 11:12], in0=lcol[:, 9:10], in1=lcol[:, 9:10], op=ALU.mult)
                r.vector.tensor_tensor(out=lcol[:, 12:13], in0=lcol[:, 8:9], in1=lcol[:, 9:10], op=ALU.mult)
                r.edge("vector", vc)
                r.vector.tensor_tensor(out=lcol[:, 10:11], in0=lcol[:, 10:11], in1=lcol[:, 11:12], op=ALU.subtract)
                r.vector.tensor_scalar(out=lcol[:, 11:12], in0=lcol[:, 12:13], scalar1=2.0, scalar2=None, op0=ALU.mult)
                r.edge("vector", vc)
                r.vector.tensor_tensor(out=lcol[:, 12:13], in0=lcol[:, 10:11], in1=lcol[:, 10:11], op=ALU.mult)
                r.vector.tensor_tensor(out=lcol[:, 13:14], in0=lcol[:, 11:12], in1=lcol[:, 11:12], op=ALU.mult)
                r.vector.tensor_tensor(out=lcol[:, 14:15], in0=lcol[:, 10:11], in1=lcol[:, 11:12], op=ALU.mult)
                r.edge("vector", vc)
                r.vector.tensor_tensor(out=lcol[:, 12:13], in0=lcol[:, 12:13], in1=lcol[:, 13:14], op=ALU.subtract)
                r.vector.tensor_scalar(out=lcol[:, 13:14], in0=lcol[:, 14:15], scalar1=2.0, scalar2=None, op0=ALU.mult)
                r.edge("vector", vc)
                m0 = cmask_rep[:, p_ * 3 + 0:p_ * 3 + 1]
                m1 = cmask_rep[:, p_ * 3 + 1:p_ * 3 + 2]
                m2 = cmask_rep[:, p_ * 3 + 2:p_ * 3 + 3]
                r.vector.tensor_scalar(out=lcol[:, 14:15], in0=lcol[:, 10:11], scalar1=m1, scalar2=None, op0=ALU.mult)
                r.vector.tensor_scalar(out=lcol[:, 15:16], in0=lcol[:, 11:12], scalar1=m1, scalar2=None, op0=ALU.mult)
                r.edge("vector", vc)
                r.vector.scalar_tensor_tensor(out=lcol[:, 14:15], in0=lcol[:, 12:13], scalar=m2, in1=lcol[:, 14:15],
                                              op0=ALU.mult, op1=ALU.add)
                r.vector.scalar_tensor_tensor(out=lcol[:, 15:16], in0=lcol[:, 13:14], scalar=m2, in1=lcol[:, 15:16],
                                              op0=ALU.mult, op1=ALU.add)
                r.edge("vector", vc)
                r.vector.tensor_scalar(out=lcol[:, 14:15], in0=lcol[:, 14:15], scalar1=m0, scalar2=None, op0=ALU.add)
                r.edge("vector", vc)
                sr = Pr[:, ntile, :]; si = Pi[:, ntile, :]
                r.vector.tensor_scalar(out=vt1[:, :], in0=sr, scalar1=lcol[:, 14:15], scalar2=None, op0=ALU.mult)
                r.vector.tensor_scalar(out=vt2[:, :], in0=si, scalar1=lcol[:, 15:16], scalar2=None, op0=ALU.mult)
                r.edge("vector", vc)
                r.vector.tensor_tensor(out=Zr[:, ntile, :], in0=Zr[:, ntile, :], in1=vt1[:, :], op=ALU.add)
                r.edge("vector", vc)
                r.vector.tensor_tensor(out=Zr[:, ntile, :], in0=Zr[:, ntile, :], in1=vt2[:, :], op=ALU.add)
                r.vector.tensor_scalar(out=vt1[:, :], in0=si, scalar1=lcol[:, 14:15], scalar2=None, op0=ALU.mult)
                r.vector.tensor_scalar(out=vt2[:, :], in0=sr, scalar1=lcol[:, 15:16], scalar2=None, op0=ALU.mult)
                r.edge("vector", vc)
                r.vector.tensor_tensor(out=Zi[:, ntile, :], in0=Zi[:, ntile, :], in1=vt1[:, :], op=ALU.add)
                r.edge("vector", vc)
                r.vector.tensor_tensor(out=Zi[:, ntile, :], in0=Zi[:, ntile, :], in1=vt2[:, :], op=ALU.subtract)
                r.edge("vector", vc)
            r.mark("vector", vc)

        r.mark("vector", vc)
        if DBG_ON:
            _dvr = ctx["nc"].dram_tensor("dbg_Vr", [128, 4, D], F32, kind="ExternalOutput").ap()
            _dvi = ctx["nc"].dram_tensor("dbg_Vi", [128, 4, D], F32, kind="ExternalOutput").ap()
            r.gpsimd.wait_ge(vc, r.val(vc))
            r.gpsimd.dma_start(out=_dvr, in_=Zr[:, :, :]).then_inc(gld, 16)
            r.gpsimd.dma_start(out=_dvi, in_=Zi[:, :, :]).then_inc(gld, 16)
        r.sync.wait(gld)
        for c in range(4):
            zq_mark = r.val(vc)
            r.vector.wait_ge(vc, r.val(vc))
            for ft in range(4):
                for comp in range(2):
                    FT = FcT if comp == 0 else FsT
                    for dh in range(2):
                        bid = comp * 2 + dh
                        bank = pb[bid]
                        r.tensor.wait_ge(vc, inj_free.get(bid, zq_mark))
                        nyq = (comp == 1 and ft == 0)
                        for mt in range(4):
                            r.tensor.matmul(bank[:, :], FT[:, mt, ft * 128:(ft + 1) * 128],
                                            u[:, c * 4 + mt, dh * 512:(dh + 1) * 512],
                                            start=(mt == 0), stop=(mt == 3 and not nyq))
                        if nyq:
                            for mt in range(4):
                                r.tensor.matmul(bank[:, :], parn_bf[:, mt, :],
                                                u[:, c * 4 + mt, dh * 512:(dh + 1) * 512],
                                                start=False, stop=(mt == 3))
                        r.mark("tensor", tc)
                        r.vector.wait_ge(tc, r.val(tc))
                        r.vector.tensor_copy(out=Usc[:, comp, dh * 512:(dh + 1) * 512], in_=bank[:, :]).then_inc(vc, 1)
                        inj_free[bid] = r.val(vc)
                r.vector.tensor_tensor(out=vt1[:, :], in0=Usc[:, 1, :], in1=Kti[:, ft, :], op=ALU.mult)
                r.vector.tensor_tensor(out=vt2[:, :], in0=Usc[:, 1, :], in1=Ktr[:, ft, :], op=ALU.mult)
                r.edge("vector", vc)
                r.vector.tensor_tensor(out=Pr[:, ft, :], in0=Usc[:, 0, :], in1=Ktr[:, ft, :], op=ALU.mult)
                r.vector.tensor_tensor(out=Pi[:, ft, :], in0=Usc[:, 0, :], in1=Kti[:, ft, :], op=ALU.mult)
                r.edge("vector", vc)
                r.vector.tensor_tensor(out=Pr[:, ft, :], in0=Pr[:, ft, :], in1=vt1[:, :], op=ALU.subtract)
                r.vector.tensor_tensor(out=Pi[:, ft, :], in0=Pi[:, ft, :], in1=vt2[:, :], op=ALU.add)
                r.edge("vector", vc)
                if ft == 0:
                    r.vector.tensor_tensor(out=Pr[0:1, 0, :], in0=Usc[0:1, 0, :], in1=Ktr[0:1, 0, :], op=ALU.mult)
                    r.vector.tensor_tensor(out=Pi[0:1, 0, :], in0=Usc[0:1, 1, :], in1=Kti[0:1, 0, :], op=ALU.mult)
                    r.edge("vector", vc)
            r.mark("vector", vc)
            pw_mark = r.val(vc)
            r.tensor.wait_ge(vc, pw_mark)
            ev_free = {}
            for lt in range(4):
                for dh in range(2):
                    bid = 4 + (lt * 2 + dh) % 2
                    bank = pb[bid]
                    r.tensor.wait_ge(vc, ev_free.get(bid, 0))
                    for ft in range(4):
                        r.tensor.matmul(bank[:, :], FcT[:, ft, lt * 128:(lt + 1) * 128],
                                        Pr[:, ft, dh * 512:(dh + 1) * 512], start=(ft == 0), stop=False)
                    for ft in range(4):
                        r.tensor.matmul(bank[:, :], FsT[:, ft, lt * 128:(lt + 1) * 128],
                                        Pi[:, ft, dh * 512:(dh + 1) * 512], start=False, stop=False)
                    r.tensor.matmul(bank[:, :], invfix[0:1, lt * 128:(lt + 1) * 128],
                                    Pr[0:1, 0, dh * 512:(dh + 1) * 512], start=False, stop=False)
                    r.tensor.matmul(bank[:, :], invfix[0:1, 512 + lt * 128:512 + (lt + 1) * 128],
                                    Pi[0:1, 0, dh * 512:(dh + 1) * 512], start=False, stop=False)
                    for ntile in range(4):
                        r.tensor.matmul(bank[:, :], Rr[:, ntile, 1 + lt * 128:1 + (lt + 1) * 128],
                                        Zr[:, ntile, dh * 512:(dh + 1) * 512], start=False, stop=False)
                    for ntile in range(4):
                        r.tensor.matmul(bank[:, :], Rni[:, ntile, 1 + lt * 128:1 + (lt + 1) * 128],
                                        Zi[:, ntile, dh * 512:(dh + 1) * 512], start=False, stop=(ntile == 3))
                    r.mark("tensor", tc)
                    r.vector.wait_ge(tc, r.val(tc))
                    rs = resid[:, c * 4 + lt, dh * 512:(dh + 1) * 512]
                    r.vector.scalar_tensor_tensor(out=rs, in0=bank[:, :], scalar=1.0, in1=rs,
                                                  op0=ALU.mult, op1=ALU.add).then_inc(vc, 1)
                    ev_free[bid] = r.val(vc)
            injections(c)
            r.vector.wait_ge(vc, r.val(vc))
            z_update2()
        r.mark("vector", vc)
        if DBG_ON:
            _dzfr = ctx["nc"].dram_tensor("dbg_Zfr", [128, 4, D], F32, kind="ExternalOutput").ap()
            _dzfi = ctx["nc"].dram_tensor("dbg_Zfi", [128, 4, D], F32, kind="ExternalOutput").ap()
            r.gpsimd.wait_ge(vc, r.val(vc))
            r.gpsimd.dma_start(out=_dzfr, in_=Zr[:, :, :]).then_inc(gld, 16)
            r.gpsimd.dma_start(out=_dzfi, in_=Zi[:, :, :]).then_inc(gld, 16)
        r.sync.wait(gld)
        r.vector.wait(ld)
        for t in range(NT):
            r.vector.tensor_tensor(out=vt1[:, :], in0=u[:, t, :], in1=pd[:, :], op=ALU.mult)
            r.edge("vector", vc)
            r.vector.tensor_tensor(out=resid[:, t, :], in0=resid[:, t, :], in1=vt1[:, :], op=ALU.add)
            r.edge("vector", vc)
    r.tensor.wait_ge(vc, r.val(vc))


def _bfview(nc, r, ap_f32, bf_tmp, vc):
    rows = ap_f32.shape[0]
    cols = ap_f32.shape[-1]
    tc_ = Rec.CUR.semv
    r.vector.wait_ge(_TC[0], Rec.CUR.val(_TC[0]))
    r.vector.tensor_copy(out=bf_tmp[0:rows, 0:cols], in_=ap_f32).then_inc(vc, 1)
    r.tensor.wait_ge(vc, Rec.CUR.val(vc))
    return bf_tmp[0:rows, 0:cols]


_TC = [None]


def _prep_inputs(inputs, stages):
    x = np.ascontiguousarray(inputs["L0_attn_x"])
    wgT = np.ascontiguousarray(inputs["L0_ffn_w_gate"].T)
    wuT = np.ascontiguousarray(inputs["L0_ffn_w_up"].T)
    wdT = np.ascontiguousarray(inputs["L0_ffn_w_down"].T)
    lam = np.zeros((4, 512), np.float32)
    lam[0] = inputs["L1_attn_Lambda_real"]
    lam[1] = inputs["L1_attn_Lambda_imag"]
    crT = np.ascontiguousarray(inputs["L1_attn_C_real"].T)
    ciT = np.ascontiguousarray(inputs["L1_attn_C_imag"].T)
    gbd = np.zeros((4, D), np.float32)
    gbd[0] = inputs["L1_attn_gamma"]; gbd[1] = inputs["L1_attn_beta"]; gbd[2] = inputs["L1_attn_param_D"]
    gbd_rep = np.ascontiguousarray(np.broadcast_to(gbd[None, :3], (128, 3, D)))
    fsb = np.stack([inputs["L1_ffn_scale"], inputs["L1_ffn_bias"]]).astype(np.float32)
    fsb_rep = np.ascontiguousarray(np.broadcast_to(fsb[None], (128, 2, D)))
    wfT = np.ascontiguousarray(inputs["L1_ffn_weights"].T)
    eye = np.eye(128, dtype=np.float32)
    mm = np.arange(128)
    samb = (mm[:, None] // 64 == mm[None, :] // 64)
    strictT = (samb & (mm[:, None] < mm[None, :])).astype(np.float32)
    inclT = (samb & (mm[:, None] <= mm[None, :])).astype(np.float32)
    mskA = np.concatenate([inclT, inclT, strictT, strictT], axis=1)
    blksel = np.concatenate([samb.astype(np.float32), (~samb).astype(np.float32)], axis=1)
    ar = np.arange(512, dtype=np.int64)
    prod = ((ar[:, None] * ar[None, :]) % 1024).astype(np.float32)
    parn = np.zeros((128, 4, 128), np.float32)
    parn[:, :, 0] = ((-1.0) ** np.arange(512)).reshape(4, 128).T
    in_maps = []
    for c in range(NCORE):
        b, q = c // 4, c % 4
        cm = np.zeros((1, 16), np.float32)
        for p in range(4):
            k = q - 1 - p
            if 0 <= k <= 2:
                cm[0, p * 3 + k] = 1.0
        qm = np.zeros((1, 8), np.float32)
        qm[0, 0] = q; qm[0, 1] = b
        hsel = slice(4 * q * 64, 4 * q * 64 + 256)
        wmy1 = np.concatenate([inputs["L0_attn_Wq"][:, hsel], inputs["L0_attn_Wk"][:, hsel],
                               inputs["L0_attn_Wv"][:, hsel], inputs["L0_attn_Wg"][:, hsel]], axis=1)
        wmy2 = np.zeros((D, 16), np.float32)
        for w_ in range(2):
            wmy2[:, 4 * w_ + 0:4 * w_ + 2] = inputs["L0_attn_Wb"][:, 4 * q + 2 * w_: 4 * q + 2 * w_ + 2]
            wmy2[:, 4 * w_ + 2:4 * w_ + 4] = inputs["L0_attn_Wa"][:, 4 * q + 2 * w_: 4 * q + 2 * w_ + 2]
        womy = inputs["L0_attn_Wo"][4 * q * 64: 4 * q * 64 + 256, :]
        attc_my = np.zeros((1, 128), np.float32)
        attc_my[0, 0:4] = inputs["L0_attn_dt_bias"][4 * q: 4 * q + 4]
        attc_my[0, 4:8] = inputs["L0_attn_A_log"][4 * q: 4 * q + 4]
        attc_my[0, 8:72] = inputs["L0_attn_norm_w"]
        m = dict(
            x=np.ascontiguousarray(x[b, q * TOK:(q + 1) * TOK, :]),
            wgT=wgT, wuT=wuT, wdT=wdT,
            wmy1=wmy1, wmy2=wmy2, womy=womy, attc_my=attc_my,
            lam=lam, crT=crT, ciT=ciT, gbd_rep=gbd_rep, fsb_rep=fsb_rep, wfT=wfT,
            eye=eye, eyef=np.eye(128, dtype=np.float32), mskA=mskA, blksel=blksel,
            inclTf=inclT.astype(np.float32), prod=prod, parn=parn, cmask=cm, qmask=qm,
            invfix=np.concatenate([np.full(512, -0.5), 0.5 * ((-1.0) ** np.arange(512))]).astype(np.float32)[None, :],
        )
        m = {k: np.ascontiguousarray(v) for k, v in m.items()}
        in_maps.append(m)
    return in_maps


def kernel(_stages=None, **inputs):
    stages = _stages if _stages is not None else STAGES
    in_maps = _prep_inputs(inputs, stages)
    nc = build_program(stages)
    res = run_bass_kernel_spmd(nc, in_maps, core_ids=list(range(NCORE)))
    global LAST_RES
    LAST_RES = res
    out = np.zeros((2, SEQ, D), np.float32)
    for c in range(NCORE):
        b, q = c // 4, c % 4
        out[b, q * TOK:(q + 1) * TOK, :] = res.results[c]["out"]
    return out


# version 06
# speedup vs baseline: 1.0117x; 1.0117x over previous
import numpy as np
import ml_dtypes
from contextlib import ExitStack
import concourse.bass as bass
import concourse.mybir as mybir
from concourse.bass_utils import run_bass_kernel_spmd

F32 = mybir.dt.float32
BF16 = mybir.dt.bfloat16
I32 = mybir.dt.int32
AF = mybir.ActivationFunctionType
ALU = mybir.AluOpType
AX = mybir.AxisListType
BF = ml_dtypes.bfloat16

NCORE = 8
TOK = 2048
NT = TOK // 128
D = 1024
HID = 3072
NH = 16
HD = 64
SEQ = 8192

STAGES = ["ffn0a", "attn0", "ffn0b", "ffn1a", "attn1", "ffn1b"]
LAST_RES = None
DBG_ON = False
DBG = {}
A0_NCH = None
A0_PRESCAN = False
A0_PHASE = None
A0_CSTOP = None


class Call:
    __slots__ = ("name", "args", "kwargs", "incs")
    def __init__(self, name, args, kwargs):
        self.name, self.args, self.kwargs = name, args, kwargs
        self.incs = []
    def then_inc(self, sem, n):
        Rec.CUR.semv[sem.name] = Rec.CUR.semv.get(sem.name, 0) + n
        self.incs.append((sem, n))
        return self


class Wait:
    __slots__ = ("sem", "val")
    def __init__(self, sem, val):
        self.sem, self.val = sem, val


class EngProxy:
    def __init__(self, rec, eng):
        object.__setattr__(self, "_rec", rec)
        object.__setattr__(self, "_eng", eng)
    def wait_ge(self, sem, val):
        self._rec.q[self._eng].append(Wait(sem, val))
    def wait(self, sem):
        self._rec.q[self._eng].append(Wait(sem, self._rec.semv.get(sem.name, 0)))
    def __getattr__(self, name):
        rec, eng = self._rec, self._eng
        def f(*a, **k):
            c = Call(name, a, k)
            rec.q[eng].append(c)
            return c
        return f


class Rec:
    CUR = None
    ENGS = ("sync", "tensor", "vector", "scalar", "gpsimd")
    def __init__(self):
        self.q = {e: [] for e in self.ENGS}
        self.semv = {}
        Rec.CUR = self
        for e in self.ENGS:
            setattr(self, e, EngProxy(self, e))
    def val(self, sem):
        return self.semv.get(sem.name, 0)
    def replay(self, eng_name, eng):
        for c in self.q[eng_name]:
            if isinstance(c, Wait):
                eng.wait_ge(c.sem, c.val)
            else:
                inst = getattr(eng, c.name)(*c.args, **c.kwargs)
                for (s, n) in c.incs:
                    inst.then_inc(s, n)
    def mark(self, eng, sem, n=1):
        for item in reversed(self.q[eng]):
            if isinstance(item, Call):
                if any(s.name == sem.name for (s, _) in item.incs):
                    return
                if item.incs:
                    getattr(self, eng).wait_ge(item.incs[-1][0], self.val(item.incs[-1][0]))
                    getattr(self, eng).nop().then_inc(sem, n)
                    return
                item.then_inc(sem, n)
                return
        getattr(self, eng).nop().then_inc(sem, n)

    def edge(self, eng, sem):
        self.mark(eng, sem)
        getattr(self, eng).wait_ge(sem, self.val(sem))

    def done_token(self, eng):
        for item in reversed(self.q[eng]):
            if isinstance(item, Call):
                if item.incs:
                    s = item.incs[-1][0]
                    return (s, self.val(s))
                if item.name.startswith("dma"):
                    item.then_inc(Rec.BAR[eng], 16)
                    return (Rec.BAR[eng], self.val(Rec.BAR[eng]))
                item.then_inc(Rec.BAR[eng], 1)
                return (Rec.BAR[eng], self.val(Rec.BAR[eng]))
        return None

    def barrier(self, sems):
        Rec.BAR = sems
        targets = [t for e in self.ENGS if (t := self.done_token(e)) is not None]
        for e in self.ENGS:
            for (s, v) in targets:
                self.q[e].append(Wait(s, v))


def build_program(stages):
    nc = bass.Bass("TRN2", target_bir_lowering=False, detect_race_conditions=False, dynamic_dma_scratch_size=16384)
    es = ExitStack()
    dram = {}
    def din(name, shape, dt=F32):
        dram[name] = nc.dram_tensor(name, list(shape), dt, kind="ExternalInput").ap()
        return dram[name]
    X = din("x", (TOK, D))
    OUT = nc.dram_tensor("out", [TOK, D], F32, kind="ExternalOutput").ap()
    din("wgT", (D, HID), I32); din("wuT", (D, HID), I32); din("wdT", (HID, D), I32)
    din("lam", (4, 512))
    din("crT", (512, D)); din("ciT", (512, D))
    din("gbd_rep", (128, 3, D))
    din("fsb_rep", (128, 2, D))
    din("wfT", (D, D))
    din("eye", (128, 128))
    din("eyef", (128, 128))
    din("mskA", (128, 512))
    din("inclTf", (128, 128))
    din("blksel", (128, 256))
    din("wmy1", (D, 1024))
    din("wmy2", (D, 16))
    din("womy", (256, D))
    din("attc_my", (1, 128))
    din("prod", (512, 512))
    din("cmask", (1, 16))
    din("invfix", (1, 1024))
    din("parn", (128, 4, 128))
    din("qmask", (1, 8))

    sb = lambda name, shp, dt: es.enter_context(nc.sbuf_tensor(name, list(shp), dt))
    resid = sb("resid", (128, NT, D), F32)
    eye = sb("eye_s", (128, 128), BF16)
    eyef = sb("eyef_s", (128, 128), F32)
    mskA = sb("mskA_s", (128, 512), BF16)
    inclTf_sb = sb("inclTf_s", (128, 128), F32)
    onescol_f = sb("onescol_f", (128, 1), F32)
    onesrow_f = sb("onesrow_f", (1, 128), F32)
    onesrow_b = sb("onesrow_b", (1, 128), BF16)
    rowst = sb("rowst", (1, 4), F32)
    colbf_s = sb("colbf_s", (128, 1), BF16)
    lam = sb("lam_s", (4, 512), F32)
    cmask = sb("cmask_s", (1, 16), F32)
    qmask = sb("qmask_s", (1, 8), F32)
    xnorm = sb("xnorm", (128, 2, D), BF16)
    xT = sb("xT", (128, 8, TOK), BF16)
    stats = sb("stats", (128, NT, 8), F32)
    ccol = sb("ccol", (128, 8), F32)
    sq_scr = sb("sq_scr", (128, D), BF16)

    pb = [es.enter_context(nc.psum_tensor(f"pb{i}", [128, 512], F32)) for i in range(6)]
    ptr = es.enter_context(nc.psum_tensor("ptr", [128, 1024], BF16))
    ptr2 = es.enter_context(nc.psum_tensor("ptr2", [128, 1024], BF16))

    sems = {}
    for s in ("ld", "gld", "tc", "vc", "sc", "gc", "st", "cc"):
        sems[s] = es.enter_context(nc.semaphore(s))
    ld, gld, tc, vc, sc, gc, st, cc = (sems[k] for k in ("ld", "gld", "tc", "vc", "sc", "gc", "st", "cc"))
    bar_sems = {"sync": ld, "tensor": tc, "vector": vc, "scalar": sc, "gpsimd": gc}

    GROUPS = [[0, 1, 2, 3], [4, 5, 6, 7]]
    idram = {}
    def dint(name, shape, dt):
        idram[name] = nc.dram_tensor(name, list(shape), dt, kind="Internal").ap()
        return idram[name]
    dint("xn_contrib", (D, TOK), BF16)
    dint("xn_gather", (4 * D, TOK), BF16)
    dint("ors_contrib", (SEQ, D), BF16)
    dint("ors_out", (TOK, D), BF16)
    dint("s_contrib", (512, 2 * D), BF16)
    dint("s_gather", (4 * 512, 2 * D), BF16)

    r = Rec()

    r.sync.dma_start(out=resid[:, :, :], in_=X.rearrange("(t p) d -> p t d", p=128)).then_inc(ld, 16)
    r.sync.dma_start(out=lam[:, :], in_=dram["lam"]).then_inc(ld, 16)
    r.sync.dma_start(out=cmask[:, :], in_=dram["cmask"]).then_inc(ld, 16)
    r.sync.dma_start(out=qmask[:, :], in_=dram["qmask"]).then_inc(ld, 16)
    r.vector.memset(ccol[:, 0:1], 1e-5)
    r.vector.memset(ccol[:, 1:2], 1.0 / 127.0)
    r.vector.memset(ccol[:, 2:3], 1e-6)
    r.vector.memset(ccol[:, 3:4], 0.0)
    r.vector.memset(ccol[:, 4:5], 1.0)
    r.gpsimd.dma_start(out=eye[:, :], in_=dram["eye"]).then_inc(gld, 16)
    r.gpsimd.dma_start(out=eyef[:, :], in_=dram["eyef"]).then_inc(gld, 16)
    r.gpsimd.dma_start(out=mskA[:, :], in_=dram["mskA"]).then_inc(gld, 16)
    r.sync.dma_start(out=inclTf_sb[:, :], in_=dram["inclTf"]).then_inc(ld, 16)
    r.vector.memset(onescol_f[:, :], 1.0)
    r.vector.memset(onesrow_f[:, :], 1.0)
    r.vector.memset(onesrow_b[:, :], 1.0)
    r.mark("vector", vc)
    if True:
        r.barrier(bar_sems)

        ctx = dict(nc=nc, r=r, es=es, sb=sb, dram=dram, idram=idram, sems=sems, pb=pb,
                   ptr=ptr, ptr2=ptr2, resid=resid, xnorm=xnorm, xT=xT, stats=stats,
                   sq_scr=sq_scr, eye=eye, eyef=eyef, mskA=mskA, inclTf_sb=inclTf_sb,
                   onescol_f=onescol_f, onesrow_f=onesrow_f, onesrow_b=onesrow_b,
                   rowst=rowst, colbf_s=colbf_s, lam=lam, cmask=cmask, ccol=ccol,
                   qmask=qmask, bar_sems=bar_sems, GROUPS=GROUPS, ffn1_es=None)

        for stg in stages:
            if stg in ("ffn0a", "ffn0b"):
                emit_ln(ctx, gamma_beta=False)
                r.barrier(bar_sems)
                emit_ffn0(ctx)
            elif stg in ("ffn1a", "ffn1b"):
                if ctx["ffn1_es"] is None:
                    prep_ffn1(ctx)
                emit_ln(ctx, gamma_beta=False)
                r.barrier(bar_sems)
                emit_ffn1(ctx)
            elif stg == "attn0":
                emit_ln(ctx, gamma_beta=False)
                r.barrier(bar_sems)
                emit_attn0(ctx)
            elif stg == "ln":
                emit_ln(ctx, gamma_beta=False)
            elif stg == "attn1":
                if ctx["ffn1_es"] is not None:
                    ctx["ffn1_es"].close()
                    ctx["ffn1_es"] = None
                emit_attn1(ctx)
            r.barrier(bar_sems)

        r.sync.dma_start(out=OUT.rearrange("(t p) d -> p t d", p=128), in_=resid[:, :, :]).then_inc(st, 16)
        for nm, ap in list(DBG.items()):
            shp = list(ap.shape)
            dbg_out = nc.dram_tensor("dbg_" + nm, [128] + shp[1:], F32, kind="ExternalOutput").ap()
            r.gpsimd.dma_start(out=dbg_out, in_=ap).then_inc(sems["gld"], 16)
        r.sync.wait(st)
        r.sync.wait(sems["gld"])

        with nc.Block() as block:
            block.sync(lambda e: r.replay("sync", e))
            block.tensor(lambda e: r.replay("tensor", e))
            block.vector(lambda e: r.replay("vector", e))
            block.scalar(lambda e: r.replay("scalar", e))
            block.gpsimd(lambda e: r.replay("gpsimd", e))
    if ctx["ffn1_es"] is not None:
        ctx["ffn1_es"].close()
        ctx["ffn1_es"] = None
    es.close()
    return nc


def emit_ln(ctx, gamma_beta, tokmajor_out=None, want_T=True):
    r = ctx["r"]; resid = ctx["resid"]; xnorm = ctx["xnorm"]; xT = ctx["xT"]
    stats = ctx["stats"]; sq = ctx["sq_scr"]; eye = ctx["eye"]
    ptr, ptr2 = ctx["ptr"], ctx["ptr2"]
    tc, vc, sc = (ctx["sems"][k] for k in ("tc", "vc", "sc"))
    gbd = ctx.get("gbd_rep")
    EPS = 1e-5
    for t in range(NT):
        x = resid[:, t, :]
        stt = stats[:, t, :]
        if t > 0:
            r.scalar.wait_ge(sc, r.val(sc))
        r.scalar.activation(sq[:, :], x, AF.Square, accum_out=stt[:, 0:1]).then_inc(sc, 1)
        r.vector.tensor_reduce(out=stt[:, 1:2], in_=x, axis=AX.X, op=ALU.add)
    r.vector.wait_ge(sc, r.val(sc))
    r.edge("vector", vc)
    S_ = lambda c: stats[:, :, c:c + 1]
    r.vector.tensor_scalar(out=S_(2), in0=S_(1), scalar1=1.0 / D, scalar2=None, op0=ALU.mult)
    r.vector.tensor_scalar(out=S_(3), in0=S_(0), scalar1=1.0 / D, scalar2=None, op0=ALU.mult)
    r.edge("vector", vc)
    r.vector.tensor_tensor(out=S_(4), in0=S_(2), in1=S_(2), op=ALU.mult)
    r.edge("vector", vc)
    r.vector.tensor_tensor(out=S_(4), in0=S_(3), in1=S_(4), op=ALU.subtract)
    r.mark("vector", vc)
    r.scalar.wait_ge(vc, r.val(vc))
    r.scalar.activation(S_(5), S_(4), AF.Sqrt, bias=ctx["ccol"][:, 0:1]).then_inc(sc, 1)
    r.vector.wait_ge(sc, r.val(sc))
    r.vector.reciprocal(out=S_(6), in_=S_(5))
    r.edge("vector", vc)
    if gamma_beta:
        r.vector.tensor_tensor(out=S_(3), in0=S_(6), in1=S_(6), op=ALU.mult)
        r.edge("vector", vc)
        r.vector.tensor_tensor(out=S_(3), in0=S_(3), in1=S_(4), op=ALU.mult)
        r.mark("vector", vc)
        r.scalar.wait_ge(vc, r.val(vc))
        r.scalar.activation(S_(5), S_(3), AF.Sqrt, bias=ctx["ccol"][:, 0:1]).then_inc(sc, 1)
        r.vector.wait_ge(sc, r.val(sc))
        r.vector.reciprocal(out=S_(0), in_=S_(5))
        r.edge("vector", vc)
        r.vector.tensor_tensor(out=S_(6), in0=S_(6), in1=S_(0), op=ALU.mult)
        r.edge("vector", vc)
    r.vector.tensor_tensor(out=S_(7), in0=S_(2), in1=S_(6), op=ALU.mult)
    r.edge("vector", vc)
    r.vector.tensor_scalar(out=S_(7), in0=S_(7), scalar1=-1.0, scalar2=None, op0=ALU.mult)
    r.mark("vector", vc)
    r.scalar.wait_ge(vc, r.val(vc))
    if DBG_ON:
        DBG["xT"] = ctx["xT"][:, :, :]
        DBG["stats"] = ctx["stats"][:, :, :]
    tile_ready = {}
    for t in range(NT):
        stt = stats[:, t, :]
        xb = tokmajor_out[:, t, :] if tokmajor_out is not None else xnorm[:, t % 2, :]
        if t >= 2 and tokmajor_out is None:
            r.scalar.wait_ge(tc, tile_ready.get(t - 2, 0))
        r.scalar.activation(xb, resid[:, t, :], AF.Identity,
                            scale=stt[:, 6:7], bias=stt[:, 7:8]).then_inc(sc, 1)
        if gamma_beta:
            r.vector.wait_ge(sc, r.val(sc))
            r.vector.tensor_tensor(out=xb, in0=xb, in1=gbd[:, 0, :], op=ALU.mult)
            r.vector.tensor_tensor(out=xb, in0=xb, in1=gbd[:, 1, :], op=ALU.add).then_inc(vc, 1)
        if want_T:
            bank = ptr if t % 2 == 0 else ptr2
            r.tensor.wait_ge(sc, r.val(sc))
            if gamma_beta:
                r.tensor.wait_ge(vc, r.val(vc))
            if t >= 2:
                r.tensor.wait_ge(vc, tile_ready[t - 2 + 1000])
            for dt in range(8):
                r.tensor.transpose(bank[:, dt * 128:(dt + 1) * 128], xb[:, dt * 128:(dt + 1) * 128], eye[:, :])
            r.mark("tensor", tc)
            tile_ready[t] = r.val(tc)
            r.vector.wait_ge(tc, r.val(tc))
            xtv = xT[:, :, t * 128:(t + 1) * 128]
            r.vector.tensor_copy(out=xtv, in_=bank[:, :].rearrange("p (dt c) -> p dt c", dt=8)).then_inc(vc, 1)
            tile_ready[t + 1000] = r.val(vc)
    if want_T:
        r.tensor.wait_ge(vc, r.val(vc))
    r.scalar.wait_ge(sc, r.val(sc))


def emit_ffn0(ctx):
    r = ctx["r"]; nc = ctx["nc"]
    xT = ctx["xT"]; resid = ctx["resid"]; pb = ctx["pb"]
    dram = ctx["dram"]
    ld, gld, tc, vc, sc = (ctx["sems"][k] for k in ("ld", "gld", "tc", "vc", "sc"))
    SCALE = 1.0 / 127.0
    wgT3 = dram["wgT"].rearrange("(dt p) H -> p dt H", p=128)
    wuT3 = dram["wuT"].rearrange("(dt p) H -> p dt H", p=128)
    wdT3 = dram["wdT"].rearrange("(ht p) d -> p ht d", p=128)
    NTB = 4

    sfx = str(ctx.setdefault("ffn0_calls", 0))
    ctx["ffn0_calls"] += 1
    with nc.sbuf_tensor("hact" + sfx, [128, 24, 512], BF16) as hact, \
         nc.sbuf_tensor("wdt_sb" + sfx, [128, 24, 1024], BF16) as wdt_sb, \
         nc.sbuf_tensor("wgu" + sfx, [128, 2, 2, 8, 128], BF16) as wgu, \
         nc.sbuf_tensor("sg" + sfx, [128, 2, 512], BF16) as sg:
        for i in range(4):
            r.gpsimd.dma_start(out=wdt_sb[:, 6 * i:6 * (i + 1), :],
                               in_=wdT3[:, 6 * i:6 * (i + 1), :]).then_inc(gld, 16)
        gld0 = r.val(gld)
        for ht in (0, 1):
            r.gpsimd.dma_start(out=wgu[:, ht % 2, 0, :, :], in_=wgT3[:, :, ht * 128:(ht + 1) * 128]).then_inc(gld, 16)
            r.gpsimd.dma_start(out=wgu[:, ht % 2, 1, :, :], in_=wuT3[:, :, ht * 128:(ht + 1) * 128]).then_inc(gld, 16)
        if DBG_ON:
            DBG["hact"] = hact[:, :, :]
        hp_free = {0: (0, 0), 1: (0, 0)}
        for tb in range(NTB):
            tok0 = tb * 512
            for ht in range(24):
                idx = tb * 24 + ht
                par = idx % 2
                r.tensor.wait_ge(gld, gld0 + 32 * (idx + 1))
                pg, pu = pb[0 + par], pb[2 + par]
                scm, vcm = hp_free[par]
                r.tensor.wait_ge(sc, scm)
                r.tensor.wait_ge(vc, vcm)
                rhs = xT[:, :, tok0:tok0 + 512]
                for dt in range(8):
                    r.tensor.matmul(pg[:, :], wgu[:, par, 0, dt, :], rhs[:, dt, :],
                                    start=(dt == 0), stop=(dt == 7))
                for dt in range(8):
                    r.tensor.matmul(pu[:, :], wgu[:, par, 1, dt, :], rhs[:, dt, :],
                                    start=(dt == 0), stop=(dt == 7))
                r.mark("tensor", tc)
                tmark = r.val(tc)
                r.scalar.wait_ge(tc, tmark)
                r.scalar.wait_ge(vc, hp_free[par][1])
                r.scalar.activation(sg[:, par, :], pg[:, :], AF.Silu, scale=ctx["ccol"][:, 1:2]).then_inc(sc, 1)
                r.vector.wait_ge(sc, r.val(sc))
                r.vector.tensor_tensor(out=hact[:, ht, :], in0=pu[:, :], in1=sg[:, par, :], op=ALU.mult).then_inc(vc, 1)
                hp_free[par] = (r.val(sc), r.val(vc))
                nxt = idx + 2
                if nxt < NTB * 24:
                    nht = nxt % 24
                    r.gpsimd.wait_ge(tc, r.val(tc))
                    r.gpsimd.dma_start(out=wgu[:, par, 0, :, :],
                                       in_=wgT3[:, :, nht * 128:(nht + 1) * 128]).then_inc(gld, 16)
                    r.gpsimd.dma_start(out=wgu[:, par, 1, :, :],
                                       in_=wuT3[:, :, nht * 128:(nht + 1) * 128]).then_inc(gld, 16)
            r.tensor.wait_ge(vc, r.val(vc))
            r.tensor.wait_ge(gld, gld0)
            dn_free = {0: 0, 1: 0}
            for tt in range(4):
                for ds in range(2):
                    bp = (tt * 2 + ds) % 2
                    bank = pb[4 + bp]
                    r.tensor.wait_ge(vc, dn_free[bp])
                    for ht in range(24):
                        r.tensor.matmul(bank[:, :], hact[:, ht, tt * 128:(tt + 1) * 128],
                                        wdt_sb[:, ht, ds * 512:(ds + 1) * 512],
                                        start=(ht == 0), stop=(ht == 23))
                    r.mark("tensor", tc)
                    r.vector.wait_ge(tc, r.val(tc))
                    rs = resid[:, tb * 4 + tt, ds * 512:(ds + 1) * 512]
                    r.vector.scalar_tensor_tensor(out=rs, in0=bank[:, :], scalar=0.5 * SCALE * SCALE,
                                                  in1=rs, op0=ALU.mult, op1=ALU.add).then_inc(vc, 1)
                    dn_free[bp] = r.val(vc)
        r.tensor.wait_ge(vc, r.val(vc))


def prep_ffn1(ctx):
    from contextlib import ExitStack as _FES
    r = ctx["r"]; nc = ctx["nc"]; dram = ctx["dram"]
    ld, vc = ctx["sems"]["ld"], ctx["sems"]["vc"]
    fes = _FES()
    sfx = str(ctx.setdefault("prep1_calls", 0))
    ctx["prep1_calls"] += 1
    fsb_rep = fes.enter_context(nc.sbuf_tensor("fsb_rep_s" + sfx, [128, 2, D], F32))
    wfs = fes.enter_context(nc.sbuf_tensor("wfs" + sfx, [128, 8, D], BF16))
    ctx["fsb_rep"] = fsb_rep
    ctx["wfs"] = wfs
    ctx["ffn1_es"] = fes
    r.sync.dma_start(out=fsb_rep[:, :, :], in_=dram["fsb_rep"]).then_inc(ld, 16)
    with nc.sbuf_tensor("wf_f32" + sfx, [128, D], F32) as wtmp:
        wfT3 = dram["wfT"].rearrange("(dt p) o -> p dt o", p=128)
        r.vector.wait(ld)
        for dt in range(8):
            r.sync.dma_start(out=wtmp[:, :], in_=wfT3[:, dt, :]).then_inc(ld, 16)
            r.vector.wait(ld)
            r.vector.tensor_scalar(out=wtmp[:, :], in0=wtmp[:, :], scalar1=0.0,
                                   scalar2=2.0, op0=ALU.is_ge, op1=ALU.mult)
            r.vector.tensor_scalar(out=wtmp[:, :], in0=wtmp[:, :], scalar1=1.0,
                                   scalar2=None, op0=ALU.subtract)
            r.vector.tensor_tensor(out=wfs[:, dt, :], in0=wtmp[:, :], in1=fsb_rep[:, 0, :], op=ALU.mult).then_inc(vc, 1)
            r.sync.wait_ge(vc, r.val(vc))
    r.barrier(ctx["bar_sems"])


def emit_ffn1(ctx):
    r = ctx["r"]
    xT = ctx["xT"]; resid = ctx["resid"]; pb = ctx["pb"]; wfs = ctx["wfs"]
    fsb = ctx["fsb_rep"]
    tc, vc = (ctx["sems"][k] for k in ("tc", "vc"))
    dn_free = {0: 0, 1: 0}
    sfx = str(ctx.setdefault("ffn1_calls", 0))
    ctx["ffn1_calls"] += 1
    with ctx["nc"].sbuf_tensor("f1tmp" + sfx, [128, 512], F32) as tmp:
        for tt in range(NT):
            for ds in range(2):
                bp = (tt * 2 + ds) % 2
                bank = pb[bp]
                r.tensor.wait_ge(vc, dn_free[bp])
                for dt in range(8):
                    r.tensor.matmul(bank[:, :], xT[:, dt, tt * 128:(tt + 1) * 128],
                                    wfs[:, dt, ds * 512:(ds + 1) * 512],
                                    start=(dt == 0), stop=(dt == 7))
                r.mark("tensor", tc)
                r.vector.wait_ge(tc, r.val(tc))
                r.vector.tensor_tensor(out=tmp[:, :], in0=bank[:, :],
                                       in1=fsb[:, 1, ds * 512:(ds + 1) * 512], op=ALU.add)
                r.vector.tensor_scalar(out=tmp[:, :], in0=tmp[:, :], scalar1=100.0,
                                       scalar2=-100.0, op0=ALU.min, op1=ALU.max)
                rs = resid[:, tt, ds * 512:(ds + 1) * 512]
                r.vector.scalar_tensor_tensor(out=rs, in0=tmp[:, :], scalar=0.5,
                                              in1=rs, op0=ALU.mult, op1=ALU.add).then_inc(vc, 1)
                dn_free[bp] = r.val(vc)
        r.tensor.wait_ge(vc, r.val(vc))


def rev_slice(ap2d, lo, hi):
    v = ap2d[..., lo:hi]
    step, count = v.ap[-1]
    v.ap[-1] = [-step, count]
    v.offset = v.offset + step * (count - 1)
    return v


def _colbf(ctx, r, col_f32, vc):
    r.vector.tensor_copy(out=ctx["colbf_s"][:, 0:1], in_=col_f32).then_inc(vc, 1)
    r.tensor.wait_ge(vc, Rec.CUR.val(vc))
    return ctx["colbf_s"][:, 0:1]


def emit_attn0(ctx):
    r = ctx["r"]; nc = ctx["nc"]; dram = ctx["dram"]; idram = ctx["idram"]
    pb = ctx["pb"]; resid = ctx["resid"]; xT = ctx["xT"]; eye = ctx["eye"]; ccol = ctx["ccol"]
    ld, gld, tc, vc, sc, gc, cc, st = (ctx["sems"][k] for k in ("ld", "gld", "tc", "vc", "sc", "gc", "cc", "st"))
    NCH = 64
    LOG_RTK = float(np.log(np.sqrt(64.0)))

    r.sync.dma_start(out=idram["xn_contrib"].rearrange("(dt p) t -> p dt t", p=128),
                     in_=xT[:, :, :]).then_inc(st, 16)
    r.gpsimd.wait(st)
    r.gpsimd.collective_compute("AllGather", ALU.bypass, replica_groups=ctx["GROUPS"],
                                ins=[idram["xn_contrib"]], outs=[idram["xn_gather"]]).then_inc(cc, 1)
    for e in Rec.ENGS:
        getattr(r, e).wait_ge(cc, r.val(cc))
    if A0_PHASE == "gather":
        return

    xg4 = idram["xn_gather"].rearrange("(pp dt p) t -> p pp dt t", p=128, dt=8)
    xTflat = xT[:, :, :].rearrange("p a b -> p (a b)")
    qT_w = xTflat[:, 0:SEQ]
    kT_w = xTflat[:, SEQ:2 * SEQ]

    from contextlib import ExitStack as _ES
    _es0 = _ES()
    ktok = _es0.enter_context(nc.sbuf_tensor("ktok", [128, NCH, 128], BF16))
    vb = _es0.enter_context(nc.sbuf_tensor("vb", [128, NCH, 128], BF16))
    gtok = _es0.enter_context(nc.sbuf_tensor("gtok", [128, NCH, 128], BF16))
    ow = _es0.enter_context(nc.sbuf_tensor("ow", [128, NCH, 128], BF16))
    wmy_sb = _es0.enter_context(nc.sbuf_tensor("wmy_sb", [128, 8, 512], BF16))
    wmy2_sb = _es0.enter_context(nc.sbuf_tensor("wmy2_sb", [128, 8, 4], BF16))
    womy_sb = _es0.enter_context(nc.sbuf_tensor("womy_sb", [128, 1, 1024], BF16))
    xgb = _es0.enter_context(nc.sbuf_tensor("xgb", [128, 2, 8, 256], BF16))
    bdw = _es0.enter_context(nc.sbuf_tensor("bdw", [128, NCH, 12], F32))
    gwork = _es0.enter_context(nc.sbuf_tensor("gwork", [128, NCH, 12], F32))
    tabf = _es0.enter_context(nc.sbuf_tensor("tabf", [128, 512], F32))
    blksel_sb = _es0.enter_context(nc.sbuf_tensor("blksel_s", [128, 256], F32))
    r.sync.dma_start(out=blksel_sb[:, :], in_=dram["blksel"]).then_inc(ld, 16)
    ctx["blksel_sb"] = blksel_sb
    qr = _es0.enter_context(nc.sbuf_tensor("qr", [128, 2, 128], BF16))
    sqw2 = _es0.enter_context(nc.sbuf_tensor("sqw2", [128, 256], BF16))
    hcol = _es0.enter_context(nc.sbuf_tensor("hcol", [128, 16], F32))
    normw_c = _es0.enter_context(nc.sbuf_tensor("normw_c", [128, 128], BF16))
    ltab = _es0.enter_context(nc.sbuf_tensor("ltab", [128, 1, 512], BF16))
    PA = _es0.enter_context(nc.sbuf_tensor("PA", [128, 1, 2, 256], BF16))
    Pk_r = _es0.enter_context(nc.sbuf_tensor("Pk_r", [128, 1, 2, 128], BF16))
    attnT_r = _es0.enter_context(nc.sbuf_tensor("attnT_r", [128, 1, 2, 128], BF16))
    kcdT_r = _es0.enter_context(nc.sbuf_tensor("kcdT_r", [128, 1, 128], BF16))
    Wkb_r = _es0.enter_context(nc.sbuf_tensor("Wkb_r", [128, 1, 2, 64], BF16))
    kd_r = _es0.enter_context(nc.sbuf_tensor("kd_r", [128, 1, 2, 64], BF16))
    vp_r = _es0.enter_context(nc.sbuf_tensor("vp_r", [128, 1, 2, 64], BF16))
    vn_r = _es0.enter_context(nc.sbuf_tensor("vn_r", [128, 1, 2, 64], BF16))
    Sf = _es0.enter_context(nc.sbuf_tensor("Sf", [128, 64], F32))
    Sbf = _es0.enter_context(nc.sbuf_tensor("Sbf", [128, 64], BF16))
    oft = _es0.enter_context(nc.sbuf_tensor("oft", [128, 2, 128], BF16))
    if True:
        with nc.sbuf_tensor("am_sb", [1, 128], F32) as am_sb, \
             nc.sbuf_tensor("ones_r3", [1, 128], F32) as ones_r3:
            r.sync.dma_start(out=am_sb[:, :], in_=dram["attc_my"]).then_inc(ld, 16)
            r.vector.memset(ones_r3[:, :], 1.0)
            r.mark("vector", vc)
            r.scalar.wait(ld)
            r.scalar.wait_ge(vc, r.val(vc))
            r.scalar.activation(am_sb[0:1, 4:8], am_sb[0:1, 4:8], AF.Exp).then_inc(sc, 1)
            r.vector.wait_ge(sc, r.val(sc))
            r.vector.tensor_scalar(out=am_sb[0:1, 4:8], in0=am_sb[0:1, 4:8], scalar1=-1.0,
                                   scalar2=None, op0=ALU.mult)
            r.mark("vector", vc)
            r.tensor.wait_ge(vc, r.val(vc))
            r.tensor.matmul(pb[5][:, 0:8], ones_r3[:, :], am_sb[0:1, 0:8], start=True, stop=True)
            r.tensor.matmul(pb[5][:, 8:72], ones_r3[:, :], am_sb[0:1, 8:72], start=True, stop=True)
            r.mark("tensor", tc)
            r.vector.wait_ge(tc, r.val(tc))
            r.vector.tensor_copy(out=hcol[:, 0:8], in_=pb[5][:, 0:8])
            r.vector.tensor_copy(out=normw_c[:, 0:64], in_=pb[5][:, 8:72])
            r.vector.tensor_copy(out=normw_c[:, 64:128], in_=pb[5][:, 8:72]).then_inc(vc, 1)
            r.tensor.wait_ge(vc, r.val(vc))

        wm3 = dram["wmy1"].rearrange("(dt p) o -> p dt o", p=128)
        wm23 = dram["wmy2"].rearrange("(dt p) o -> p dt o", p=128)
        wo3 = dram["womy"].rearrange("(wt p) o -> p wt o", p=128)
        for w in range(1 if A0_NCH is not None else 2):
            hb = 2 * w
            r.gpsimd.wait_ge(tc, r.val(tc))
            for j in range(4):
                r.gpsimd.dma_start(out=wmy_sb[:, :, j * 128:(j + 1) * 128],
                                   in_=wm3[:, :, j * 256 + hb * 64: j * 256 + hb * 64 + 128]).then_inc(gld, 16)
            r.gpsimd.dma_start(out=wmy2_sb[:, :, :], in_=wm23[:, :, 4 * w: 4 * w + 4]).then_inc(gld, 16)
            r.gpsimd.dma_start(out=womy_sb[:, 0:1, :], in_=wo3[:, w:w + 1, :]).then_inc(gld, 16)
            wv_ld = r.val(gld)
            dtb = hcol[:, hb:hb + 2]
            negEA = hcol[:, 4 + hb:4 + hb + 2]
            pj_marks = {}
            for tg in range(32):
                par = tg % 2
                r.sync.wait_ge(tc, pj_marks.get(("x", par), 0))
                r.sync.dma_start(out=xgb[:, par, :, :],
                                 in_=xg4[:, tg // 8, :, (tg % 8) * 256:(tg % 8 + 1) * 256]).then_inc(ld, 16)
                xld = r.val(ld)
                for ti in range(2):
                    ch = tg * 2 + ti
                    bank = pb[ti % 2]
                    bank2 = pb[2]
                    r.tensor.wait_ge(ld, xld)
                    r.tensor.wait_ge(gld, wv_ld)
                    r.tensor.wait_ge(vc, pj_marks.get(("e", ti % 2), 0))
                    r.tensor.wait_ge(sc, pj_marks.get("ba", 0))
                    for dt in range(8):
                        r.tensor.matmul(bank[:, :], xgb[:, par, dt, ti * 128:(ti + 1) * 128],
                                        wmy_sb[:, dt, :], start=(dt == 0), stop=(dt == 7))
                    for dt in range(8):
                        r.tensor.matmul(bank2[:, 0:4], xgb[:, par, dt, ti * 128:(ti + 1) * 128],
                                        wmy2_sb[:, dt, :], start=(dt == 0), stop=(dt == 7))
                    r.mark("tensor", tc)
                    pj_marks[("x", par)] = r.val(tc)
                    tm = r.val(tc)
                    r.scalar.wait_ge(tc, tm)
                    r.scalar.activation(bdw[:, ch, 2:4], bank2[:, 0:2], AF.Sigmoid).then_inc(sc, 1)
                    r.vector.wait_ge(sc, r.val(sc))
                    r.vector.tensor_tensor(out=bdw[:, ch, 0:2], in0=bank2[:, 2:4], in1=dtb, op=ALU.add)
                    r.mark("vector", vc)
                    r.scalar.wait_ge(vc, r.val(vc))
                    r.scalar.activation(bdw[:, ch, 0:2], bdw[:, ch, 0:2], AF.Exp)
                    r.scalar.activation(bdw[:, ch, 0:2], bdw[:, ch, 0:2], AF.Ln, bias=ccol[:, 4:5]).then_inc(sc, 1)
                    pj_marks["ba"] = r.val(sc)
                    r.vector.wait_ge(sc, r.val(sc))
                    r.vector.tensor_tensor(out=bdw[:, ch, 0:2], in0=bdw[:, ch, 0:2], in1=negEA, op=ALU.mult)
                    r.edge("vector", vc)
                    r.vector.tensor_copy(out=qr[:, ti % 2, :], in_=bank[:, 0:128])
                    r.vector.tensor_copy(out=ktok[:, ch, :], in_=bank[:, 128:256]).then_inc(vc, 1)
                    r.scalar.wait_ge(vc, r.val(vc))
                    r.scalar.activation(sqw2[:, 0:128], qr[:, ti % 2, :], AF.Square)
                    r.scalar.activation(sqw2[:, 128:256], ktok[:, ch, :], AF.Square).then_inc(sc, 1)
                    r.vector.wait_ge(sc, r.val(sc))
                    r.vector.tensor_reduce(out=bdw[:, ch, 4:8],
                                           in_=sqw2[:, :].rearrange("p (h c) -> p h c", h=4),
                                           axis=AX.X, op=ALU.add)
                    b3 = bdw[:, ch, 2:4].rearrange("p h -> p h ()")
                    b3.ap[-1] = [0, 64]
                    r.vector.tensor_tensor(out=vb[:, ch, :].rearrange("p (h c) -> p h c", h=2),
                                           in0=bank[:, 256:384].rearrange("p (h c) -> p h c", h=2),
                                           in1=b3, op=ALU.mult)
                    r.vector.tensor_copy(out=gtok[:, ch, :], in_=bank[:, 384:512]).then_inc(vc, 1)
                    pj_marks[("e", ti % 2)] = r.val(vc)
                    r.tensor.wait_ge(vc, r.val(vc))
                    r.tensor.transpose(ctx["ptr"][:, 0:128], qr[:, ti % 2, :], eye[:, :])
                    r.tensor.transpose(ctx["ptr"][:, 128:256], ktok[:, ch, :], eye[:, :])
                    r.mark("tensor", tc)
                    r.vector.wait_ge(tc, r.val(tc))
                    r.vector.tensor_copy(out=qT_w[:, ch * 128:(ch + 1) * 128], in_=ctx["ptr"][:, 0:128])
                    r.vector.tensor_copy(out=kT_w[:, ch * 128:(ch + 1) * 128], in_=ctx["ptr"][:, 128:256]).then_inc(vc, 1)
                    r.tensor.wait_ge(vc, r.val(vc))
            r.mark("vector", vc)
            r.scalar.wait_ge(vc, r.val(vc))
            r.scalar.activation(bdw[:, :, 4:6],
                                bdw[:, :, 4:6], AF.Ln, bias=ccol[:, 2:3])
            r.scalar.activation(bdw[:, :, 6:8],
                                bdw[:, :, 6:8], AF.Ln, bias=ccol[:, 2:3])
            r.scalar.activation(bdw[:, :, 8:10],
                                bdw[:, :, 2:4], AF.Ln).then_inc(sc, 1)
            r.vector.wait_ge(sc, r.val(sc))
            r.vector.tensor_scalar(out=bdw[:, :, 4:6], in0=bdw[:, :, 4:6], scalar1=-0.5,
                                   scalar2=-LOG_RTK, op0=ALU.mult, op1=ALU.add)
            r.vector.tensor_scalar(out=bdw[:, :, 6:8], in0=bdw[:, :, 6:8], scalar1=-0.5,
                                   scalar2=None, op0=ALU.mult)
            r.edge("vector", vc)
            r.vector.tensor_tensor(out=bdw[:, :, 8:10], in0=bdw[:, :, 8:10], in1=bdw[:, :, 6:8], op=ALU.add)
            r.mark("vector", vc)
            r.scalar.wait_ge(vc, r.val(vc))
            r.scalar.activation(bdw[:, :, 10:12],
                                bdw[:, :, 6:8], AF.Exp).then_inc(sc, 1)
            r.vector.wait_ge(sc, r.val(sc))

            r.vector.memset(Sf[:, :], 0.0)
            r.vector.tensor_copy(out=Sbf[:, :], in_=Sf[:, :]).then_inc(vc, 1)
            r.tensor.wait_ge(vc, r.val(vc))
            nch_run = 0 if A0_PHASE == "proj" else (NCH if A0_NCH is None else (A0_NCH + (1 if A0_PRESCAN else 0)))
            for ch in range(nch_run):
                prescan = (A0_NCH is not None) and A0_PRESCAN and (ch == A0_NCH)
                par = ch % 2
                bankA = pb[4]
                r.tensor.wait_ge(vc, r.val(vc))
                r.tensor.matmul(bankA[:, 0:2], ctx["inclTf_sb"][:, :], bdw[:, ch, 0:2], start=True, stop=True)
                r.tensor.matmul(bankA[:, 2:4], ctx["blksel_sb"][:, 0:128], bdw[:, ch, 0:2], start=True, stop=True)
                r.tensor.matmul(bankA[:, 4:6], ctx["blksel_sb"][:, 128:256], bdw[:, ch, 0:2], start=True, stop=True)
                r.mark("tensor", tc)
                r.vector.wait_ge(tc, r.val(tc))
                r.vector.tensor_copy(out=gwork[:, ch, 0:2], in_=bankA[:, 0:2])
                r.vector.tensor_copy(out=gwork[:, ch, 6:8], in_=bankA[:, 2:4])
                r.vector.tensor_copy(out=gwork[:, ch, 8:10], in_=bankA[:, 2:4])
                r.vector.tensor_copy(out=gwork[:, ch, 10:12], in_=bankA[:, 4:6])
                r.edge("vector", vc)
                r.vector.tensor_tensor(out=gwork[:, ch, 2:4], in0=gwork[:, ch, 0:2], in1=bdw[:, ch, 4:6], op=ALU.add)
                r.vector.tensor_tensor(out=gwork[:, ch, 4:6], in0=gwork[:, ch, 0:2], in1=bdw[:, ch, 8:10], op=ALU.add)
                r.vector.tensor_tensor(out=gwork[:, ch, 6:8], in0=gwork[:, ch, 6:8], in1=gwork[:, ch, 0:2], op=ALU.subtract)
                r.edge("vector", vc)
                r.vector.tensor_tensor(out=gwork[:, ch, 6:8], in0=gwork[:, ch, 6:8], in1=bdw[:, ch, 6:8], op=ALU.add)
                r.mark("vector", vc)
                r.scalar.wait_ge(vc, r.val(vc))
                r.scalar.activation(gwork[:, ch, 6:8], gwork[:, ch, 6:8], AF.Exp)
                r.scalar.activation(gwork[:, ch, 8:12], gwork[:, ch, 8:12], AF.Exp).then_inc(sc, 1)
                r.vector.wait_ge(sc, r.val(sc))
                if prescan and A0_CSTOP == "g":
                    continue
                r.tensor.wait_ge(vc, r.val(vc))
                bankB = pb[5]
                for j, col in enumerate((2, 3, 4, 5)):
                    r.tensor.transpose(bankB[0:1, j * 128:(j + 1) * 128],
                                       gwork[:, ch, col:col + 1], ctx["eyef"][:, :])
                r.mark("tensor", tc)
                r.vector.wait_ge(tc, r.val(tc))
                r.vector.tensor_copy(out=tabf[0:1, :], in_=bankB[0:1, 0:512]).then_inc(vc, 1)
                r.tensor.wait_ge(vc, r.val(vc))
                r.tensor.matmul(bankB[:, :], ctx["onesrow_f"][:, :], tabf[0:1, :], start=True, stop=True)
                r.mark("tensor", tc)
                r.vector.wait_ge(tc, r.val(tc))
                for blk in range(4):
                    gcol = gwork[:, ch, 0 + (blk % 2):1 + (blk % 2)]
                    r.vector.tensor_scalar(out=tabf[:, blk * 128:(blk + 1) * 128],
                                           in0=bankB[:, blk * 128:(blk + 1) * 128],
                                           scalar1=gcol, scalar2=20.0, op0=ALU.subtract, op1=ALU.min)
                r.mark("vector", vc)
                r.scalar.wait_ge(vc, r.val(vc))
                r.scalar.activation(ltab[:, 0, :], tabf[:, :], AF.Exp).then_inc(sc, 1)
                r.vector.wait_ge(sc, r.val(sc))
                r.vector.tensor_tensor(out=ltab[:, 0, :], in0=ltab[:, 0, :], in1=ctx["mskA"][:, :], op=ALU.mult)
                r.mark("vector", vc)
                tab_mark = r.val(vc)
                if prescan and A0_CSTOP == "tab":
                    continue
                r.tensor.wait_ge(vc, tab_mark)
                for h in range(2):
                    bk = pb[h]
                    r.tensor.matmul(bk[:, 0:128],
                                    kT_w[h * 64:(h + 1) * 64, ch * 128:(ch + 1) * 128],
                                    kT_w[h * 64:(h + 1) * 64, ch * 128:(ch + 1) * 128],
                                    start=True, stop=True)
                    r.tensor.matmul(bk[:, 128:256],
                                    kT_w[h * 64:(h + 1) * 64, ch * 128:(ch + 1) * 128],
                                    qT_w[h * 64:(h + 1) * 64, ch * 128:(ch + 1) * 128],
                                    start=True, stop=True)
                r.mark("tensor", tc)
                r.vector.wait_ge(tc, r.val(tc))
                for h in range(2):
                    skc = bdw[:, ch, 10 + h:11 + h]
                    r.vector.scalar_tensor_tensor(out=PA[:, 0, h, 0:128], in0=pb[h][:, 0:128],
                                                  scalar=skc, in1=ltab[:, 0, 256 + h * 128:256 + (h + 1) * 128],
                                                  op0=ALU.mult, op1=ALU.mult)
                    r.vector.scalar_tensor_tensor(out=attnT_r[:, 0, h, :], in0=pb[h][:, 128:256],
                                                  scalar=skc, in1=ltab[:, 0, h * 128:(h + 1) * 128],
                                                  op0=ALU.mult, op1=ALU.mult)
                r.edge("vector", vc)
                for h in range(2):
                    r.vector.tensor_tensor(out=PA[:, 0, h, 128:256], in0=PA[:, 0, h, 0:128],
                                           in1=eye[:, :], op=ALU.add)
                r.mark("vector", vc)
                pk_mark = r.val(vc)
                if prescan and A0_CSTOP == "kk":
                    continue
                r.mark("vector", vc)
                r.scalar.wait_ge(vc, r.val(vc))
                r.scalar.activation(hcol[:, 8:10], gwork[:, ch, 4:6], AF.Exp).then_inc(sc, 1)
                r.vector.wait_ge(sc, r.val(sc))
                for h in range(2):
                    r.vector.tensor_scalar(out=Wkb_r[:, 0, h, :], in0=ktok[:, ch, h * 64:(h + 1) * 64],
                                           scalar1=hcol[:, 8 + h:9 + h], scalar2=None, op0=ALU.mult)
                    r.vector.tensor_scalar(out=kd_r[:, 0, h, :], in0=ktok[:, ch, h * 64:(h + 1) * 64],
                                           scalar1=gwork[:, ch, 6 + h:7 + h], scalar2=None, op0=ALU.mult)
                r.mark("vector", vc)
                wkd_mark = r.val(vc)
                if prescan and A0_CSTOP == "wkd":
                    continue
                bE = {0: pb[2], 1: pb[3]}

                def _tr_pk():
                    for h in range(2):
                        r.tensor.transpose(ctx["ptr"][:, h * 128:(h + 1) * 128],
                                           PA[:, 0, h, 0:128], eye[:, :])
                    r.mark("tensor", tc)
                    r.vector.wait_ge(tc, r.val(tc))
                    for h in range(2):
                        r.vector.tensor_copy(out=Pk_r[:, 0, h, :],
                                             in_=ctx["ptr"][:, h * 128:(h + 1) * 128])
                    r.mark("vector", vc)
                    r.tensor.wait_ge(vc, r.val(vc))

                r.tensor.wait_ge(vc, pk_mark)
                _tr_pk()
                for h in range(2):
                    r.tensor.matmul(bE[h][:, 0:128], Pk_r[:, 0, h, :], PA[:, 0, h, 0:128],
                                    start=True, stop=True)
                    r.tensor.matmul(bE[h][:, 256:384], PA[:, 0, h, 0:128], Pk_r[:, 0, h, :],
                                    start=True, stop=True)
                r.mark("tensor", tc)
                r.vector.wait_ge(tc, r.val(tc))
                for h in range(2):
                    r.vector.tensor_copy(out=PA[:, 0, h, 0:128], in_=bE[h][:, 0:128])
                    r.vector.tensor_copy(out=Pk_r[:, 0, h, :], in_=bE[h][:, 256:384])
                r.mark("vector", vc)
                r.tensor.wait_ge(vc, r.val(vc))
                for it in range(5):
                    for h in range(2):
                        r.tensor.matmul(bE[h][:, 0:256], Pk_r[:, 0, h, :], PA[:, 0, h, :],
                                        start=True, stop=True)
                        if it < 4:
                            r.tensor.matmul(bE[h][:, 256:384], PA[:, 0, h, 0:128],
                                            Pk_r[:, 0, h, :], start=True, stop=True)
                    r.mark("tensor", tc)
                    r.vector.wait_ge(tc, r.val(tc))
                    for h in range(2):
                        r.vector.tensor_copy(out=PA[:, 0, h, 0:128], in_=bE[h][:, 0:128])
                        r.vector.tensor_tensor(out=PA[:, 0, h, 128:256], in0=PA[:, 0, h, 128:256],
                                               in1=bE[h][:, 128:256], op=ALU.add)
                        if it < 4:
                            r.vector.tensor_copy(out=Pk_r[:, 0, h, :], in_=bE[h][:, 256:384])
                    r.mark("vector", vc)
                    r.tensor.wait_ge(vc, r.val(vc))
                r.tensor.wait_ge(vc, wkd_mark)
                for h in range(2):
                    r.tensor.matmul(bE[h][:, 256:320], PA[:, 0, h, 128:256],
                                    vb[:, ch, h * 64:(h + 1) * 64], start=True, stop=True)
                    r.tensor.matmul(bE[h][0:64, 320:448], Wkb_r[:, 0, h, :], PA[:, 0, h, 128:256],
                                    start=True, stop=True)
                r.mark("tensor", tc)
                r.vector.wait_ge(tc, r.val(tc))
                for h in range(2):
                    r.vector.tensor_copy(out=vp_r[:, 0, h, :], in_=bE[h][:, 256:320])
                    r.vector.tensor_scalar(out=kcdT_r[h * 64:(h + 1) * 64, 0, :],
                                           in0=bE[h][0:64, 320:448], scalar1=-1.0, scalar2=None,
                                           op0=ALU.mult)
                r.mark("vector", vc)
                r.tensor.wait_ge(vc, r.val(vc))
                if not prescan:
                    r.mark("vector", vc)
                    r.scalar.wait_ge(vc, r.val(vc))
                    r.scalar.activation(hcol[:, 10:11], gwork[:, ch, 2:3], AF.Exp)
                    r.scalar.activation(hcol[:, 11:12], gwork[:, ch, 3:4], AF.Exp).then_inc(sc, 1)
                    r.scalar.wait_ge(sc, r.val(sc))
                    bV = {0: pb[2], 1: pb[3]}
                    bS = {0: pb[0], 1: pb[1]}
                    for s in range(2):
                        ss = slice(s * 64, (s + 1) * 64)
                        scol = s * 64
                        for h in range(2):
                            hs = slice(h * 64, (h + 1) * 64)
                            r.tensor.matmul(bV[h][0:64, 0:64], kcdT_r[hs, 0, scol:scol + 64],
                                            Sbf[hs, :], start=True, stop=True)
                            r.tensor.matmul(bV[h][0:64, 64:128],
                                            qT_w[hs, ch * 128 + scol: ch * 128 + scol + 64],
                                            Sbf[hs, :], start=True, stop=True)
                        r.mark("tensor", tc)
                        r.vector.wait_ge(tc, r.val(tc))
                        for h in range(2):
                            r.vector.tensor_tensor(out=vn_r[ss, 0, h, :], in0=bV[h][0:64, 0:64],
                                                   in1=vp_r[ss, 0, h, :], op=ALU.add)
                        r.mark("vector", vc)
                        r.tensor.wait_ge(vc, r.val(vc))
                        for h in range(2):
                            r.tensor.matmul(bS[h][0:64, 0:64], attnT_r[ss, 0, h, scol:scol + 64],
                                            vn_r[ss, 0, h, :], start=True, stop=True)
                            r.tensor.matmul(bS[h][0:64, 64:128], kd_r[ss, 0, h, :],
                                            vn_r[ss, 0, h, :], start=True, stop=True)
                        r.mark("tensor", tc)
                        r.scalar.wait_ge(tc, r.val(tc))
                        for h in range(2):
                            hs = slice(h * 64, (h + 1) * 64)
                            r.scalar.activation(ow[ss, ch, hs], bV[h][0:64, 64:128], AF.Identity,
                                                scale=hcol[ss, 10 + h:11 + h])
                        r.mark("scalar", sc)
                        r.vector.wait_ge(tc, r.val(tc))
                        r.vector.wait_ge(sc, r.val(sc))
                        for h in range(2):
                            hs = slice(h * 64, (h + 1) * 64)
                            r.vector.tensor_tensor(out=ow[ss, ch, hs], in0=bS[h][0:64, 0:64],
                                                   in1=ow[ss, ch, hs], op=ALU.add)
                        for h in range(2):
                            hs = slice(h * 64, (h + 1) * 64)
                            eglc = (8 + h) if s == h else (10 + h)
                            r.vector.scalar_tensor_tensor(out=Sbf[hs, :], in0=Sf[hs, :],
                                                          scalar=gwork[hs, ch, eglc:eglc + 1],
                                                          in1=bS[h][0:64, 64:128], op0=ALU.mult, op1=ALU.add)
                            r.vector.scalar_tensor_tensor(out=Sf[hs, :], in0=Sf[hs, :],
                                                          scalar=gwork[hs, ch, eglc:eglc + 1],
                                                          in1=bS[h][0:64, 64:128], op0=ALU.mult, op1=ALU.add)
                        r.mark("vector", vc)
                        r.tensor.wait_ge(vc, r.val(vc))
            ors_marks = {}
            for ch in range(NCH if A0_NCH is None else 0):
                r.scalar.wait_ge(vc, r.val(vc))
                r.scalar.activation(sqw2[:, 0:128], ow[:, ch, :], AF.Square).then_inc(sc, 1)
                r.vector.wait_ge(sc, r.val(sc))
                r.vector.tensor_reduce(out=hcol[:, 12:14], in_=sqw2[:, 0:128].rearrange("p (h c) -> p h c", h=2),
                                       axis=AX.X, op=ALU.add)
                r.edge("vector", vc)
                r.vector.tensor_scalar(out=hcol[:, 12:14], in0=hcol[:, 12:14], scalar1=1.0 / 64.0,
                                       scalar2=None, op0=ALU.mult)
                r.mark("vector", vc)
                r.scalar.wait_ge(vc, r.val(vc))
                r.scalar.activation(hcol[:, 12:14], hcol[:, 12:14], AF.Sqrt, bias=ccol[:, 0:1]).then_inc(sc, 1)
                r.vector.wait_ge(sc, r.val(sc))
                r.vector.reciprocal(out=hcol[:, 12:14], in_=hcol[:, 12:14])
                r.edge("vector", vc)
                rinv = hcol[:, 12:14].rearrange("p h -> p h ()")
                rinv.ap[-1] = [0, 64]
                r.vector.tensor_tensor(out=ow[:, ch, :].rearrange("p (h c) -> p h c", h=2),
                                       in0=ow[:, ch, :].rearrange("p (h c) -> p h c", h=2),
                                       in1=rinv, op=ALU.mult)
                r.edge("vector", vc)
                r.vector.tensor_tensor(out=ow[:, ch, :], in0=ow[:, ch, :], in1=normw_c[:, :], op=ALU.mult)
                r.mark("vector", vc)
                r.scalar.wait_ge(vc, r.val(vc))
                r.scalar.activation(sqw2[:, 0:128], gtok[:, ch, :], AF.Silu).then_inc(sc, 1)
                r.vector.wait_ge(sc, r.val(sc))
                r.vector.tensor_tensor(out=ow[:, ch, :], in0=ow[:, ch, :], in1=sqw2[:, 0:128], op=ALU.mult).then_inc(vc, 1)
                r.tensor.wait_ge(vc, r.val(vc))
                r.tensor.transpose(ctx["ptr"][:, 0:128], ow[:, ch, :], eye[:, :])
                r.mark("tensor", tc)
                r.vector.wait_ge(tc, r.val(tc))
                r.vector.tensor_copy(out=oft[:, ch % 2, :], in_=ctx["ptr"][:, 0:128]).then_inc(vc, 1)
                oft_mark = r.val(vc)
                for ds in range(2):
                    bank = pb[ds]
                    slot = (ch % 2) * 2 + ds
                    r.tensor.wait_ge(vc, oft_mark)
                    r.tensor.wait_ge(vc, ors_marks.get(("v", slot), 0))
                    r.tensor.matmul(bank[:, :], oft[:, ch % 2, :], womy_sb[:, 0, ds * 512:(ds + 1) * 512],
                                    start=True, stop=True)
                    r.mark("tensor", tc)
                    r.vector.wait_ge(tc, r.val(tc))
                    r.vector.wait_ge(st if w == 0 else gld, ors_marks.get(("d", slot), 0))
                    dstslot = ctx["xnorm"][:, ch % 2, ds * 512:(ds + 1) * 512]
                    r.vector.tensor_copy(out=dstslot, in_=bank[:, :]).then_inc(vc, 1)
                    ors_marks[("v", slot)] = r.val(vc)
                    dst = idram["ors_contrib"][ch * 128:(ch + 1) * 128, ds * 512:(ds + 1) * 512]
                    if w == 0:
                        r.sync.wait_ge(vc, r.val(vc))
                        r.sync.dma_start(out=dst, in_=dstslot).then_inc(st, 16)
                        ors_marks[("d", slot)] = r.val(st)
                    else:
                        r.gpsimd.wait_ge(vc, r.val(vc))
                        r.gpsimd.dma_start(out=dst, in_=dstslot, accum_op=ALU.add).then_inc(gld, 16)
                        ors_marks[("d", slot)] = r.val(gld)
            r.sync.wait(st)
            r.gpsimd.wait(gld)
            r.barrier(ctx["bar_sems"])
        if A0_NCH is not None:
            nr = A0_NCH + (1 if A0_PRESCAN else 0)
            for nm, ap in (("Sf", Sf[:, :]), ("Sbf", Sbf[:, :]), ("kcdT_r", kcdT_r[:, :, :]),
                           ("vp_r", vp_r[:, :, :, :]), ("vn_r", vn_r[:, :, :, :]),
                           ("bdw", bdw[:, :, :]), ("gwork", gwork[:, 0:nr, :]),
                           ("ktok", ktok[:, :, :]), ("vbd", vb[:, :, :]),
                           ("attnT_r", attnT_r[:, :, :, :]), ("PA", PA[:, :, :, :]),
                           ("Pk_r", Pk_r[:, :, :, :]), ("ltab", ltab[:, :, :]),
                           ("hcol", hcol[:, 0:12]), ("owd", ow[:, 0:max(A0_NCH, 1), :]),
                           ("kd_r", kd_r[:, :, :, :]), ("Wkb_r", Wkb_r[:, :, :, :])):
                DBG[nm] = ap
            _es0.close()
            return
        r.gpsimd.collective_compute("ReduceScatter", ALU.add, replica_groups=ctx["GROUPS"],
                                    ins=[idram["ors_contrib"]], outs=[idram["ors_out"]]).then_inc(cc, 1)
        r.sync.wait_ge(cc, r.val(cc))
        r.vector.wait_ge(cc, r.val(cc))
        for t in range(NT):
            stag = xgb[:, t % 2, 0:4, :].rearrange("p a b -> p (a b)")
            r.sync.wait_ge(vc, r.val(vc) if t >= 2 else 0)
            r.sync.dma_start(out=stag, in_=idram["ors_out"][t * 128:(t + 1) * 128, :]).then_inc(ld, 16)
            r.vector.wait(ld)
            r.vector.tensor_tensor(out=resid[:, t, :], in0=resid[:, t, :], in1=stag, op=ALU.add).then_inc(vc, 1)
    _es0.close()
    r.tensor.wait_ge(vc, r.val(vc))


def emit_attn1(ctx):
    r = ctx["r"]; nc = ctx["nc"]; es = ctx["es"]; dram = ctx["dram"]; idram = ctx["idram"]
    pb = ctx["pb"]; resid = ctx["resid"]; eye = ctx["eye"]; ccol = ctx["ccol"]
    ld, gld, tc, vc, sc, gc, cc = (ctx["sems"][k] for k in ("ld", "gld", "tc", "vc", "sc", "gc", "cc"))
    _TC[0] = tc
    lam = ctx["lam"]; cmask = ctx["cmask"]
    TWO_PI = 2.0 * np.pi

    gbd = None
    FcT = es.enter_context(nc.sbuf_tensor("FcT", [128, 4, 512], BF16))
    FsT = es.enter_context(nc.sbuf_tensor("FsT", [128, 4, 512], BF16))
    parn_bf = es.enter_context(nc.sbuf_tensor("parn_bf", [128, 4, 128], BF16))
    invfix = es.enter_context(nc.sbuf_tensor("invfix_s", [1, 1024], BF16))
    Rr = es.enter_context(nc.sbuf_tensor("Rr", [128, 4, 516], BF16))
    Rni = es.enter_context(nc.sbuf_tensor("Rni", [128, 4, 516], BF16))
    ATr = es.enter_context(nc.sbuf_tensor("ATr", [128, 16, 128], BF16))
    ATi = es.enter_context(nc.sbuf_tensor("ATi", [128, 16, 128], BF16))
    Cr = es.enter_context(nc.sbuf_tensor("Cr", [128, 4, D], BF16))
    Ci = es.enter_context(nc.sbuf_tensor("Ci", [128, 4, D], BF16))
    Ktr = es.enter_context(nc.sbuf_tensor("Ktr", [128, 4, D], BF16))
    Kti = es.enter_context(nc.sbuf_tensor("Kti", [128, 4, D], BF16))
    lcol = es.enter_context(nc.sbuf_tensor("lcol", [128, 16], F32))
    cmask_rep = es.enter_context(nc.sbuf_tensor("cmask_rep", [128, 16], F32))

    with nc.sbuf_tensor("ones_row", [1, 128], F32) as ones_row:
        r.vector.memset(ones_row[:, :], 1.0)
        r.mark("vector", vc)
        r.tensor.wait_ge(vc, r.val(vc))
        r.tensor.matmul(pb[5][0:128, 0:16], ones_row[:, :], cmask[:, :], start=True, stop=True)
        r.mark("tensor", tc)
        r.vector.wait_ge(tc, r.val(tc))
        r.vector.tensor_copy(out=cmask_rep[:, :], in_=pb[5][0:128, 0:16]).then_inc(vc, 1)
        r.tensor.wait_ge(vc, r.val(vc))

    u = ctx["xT"][:, :, :].rearrange("p a b -> p (a b)").rearrange("p (t d) -> p t d", t=NT)
    with nc.sbuf_tensor("gb_tmp", [128, 2, D], F32) as gb_tmp:
        ctx["gbd_rep"] = gb_tmp
        r.sync.dma_start(out=gb_tmp[:, :, :], in_=dram["gbd_rep"].rearrange("p r d -> p r d")[:, 0:2, :]).then_inc(ld, 16)
        r.vector.wait(ld)
        r.scalar.wait(ld)
        emit_ln(ctx, gamma_beta=True, tokmajor_out=u, want_T=False)
        r.barrier(ctx["bar_sems"])

    with nc.sbuf_tensor("prod_sb", [128, 4, 512], F32) as prod_sb, \
         nc.sbuf_tensor("ttmp", [128, 4, 512], F32) as ttmp, \
         nc.sbuf_tensor("tind", [128, 512], F32) as tind:
        ttmp2 = None
        r.sync.dma_start(out=prod_sb[:, :, :], in_=dram["prod"].rearrange("(mt p) f -> p mt f", p=128)).then_inc(ld, 16)
        r.vector.wait(ld)
        r.scalar.wait(ld)
        def red_steps(buf3, ks):
            for mt in range(4):
                buf = buf3[:, mt, :]
                for (th, am) in ks:
                    r.vector.tensor_scalar(out=tind[:, :], in0=buf, scalar1=float(th), scalar2=float(-am),
                                           op0=ALU.is_ge, op1=ALU.mult)
                    r.edge("vector", vc)
                    r.vector.tensor_tensor(out=buf, in0=buf, in1=tind[:, :], op=ALU.add)
                    r.edge("vector", vc)

        def trig_table(dst, shift, scale_col, fixup=None):
            r.vector.tensor_scalar(out=ttmp[:, :, :], in0=prod_sb[:, :, :], scalar1=1.0 / 1024.0,
                                   scalar2=shift, op0=ALU.mult, op1=ALU.add)
            r.edge("vector", vc)
            red_steps(ttmp[:, :, :], [(1.0, 1.0), (0.5, 1.0)])
            r.mark("vector", vc)
            r.scalar.wait_ge(vc, r.val(vc))
            r.scalar.activation(dst, ttmp[:, :, :], AF.Sin, scale=ccol[:, 5:6]).then_inc(sc, 1)
            r.vector.wait_ge(sc, r.val(sc))
            if scale_col != 1.0:
                r.vector.tensor_scalar(out=dst, in0=dst, scalar1=scale_col, scalar2=None, op0=ALU.mult)
            r.edge("vector", vc)
        r.vector.memset(ccol[:, 5:6], TWO_PI)
        r.edge("vector", vc)
        trig_table(FcT[:, :, :], 0.25, 1.0)
        trig_table(FsT[:, :, :], 0.0, 1.0)
        r.gpsimd.dma_start(out=parn_bf[:, :, :], in_=dram["parn"]).then_inc(gld, 16)
        r.gpsimd.dma_start(out=invfix[:, :], in_=dram["invfix"]).then_inc(gld, 16)
        r.vector.wait(gld)

    r.barrier(ctx["bar_sems"])
    with nc.sbuf_tensor("Rf", [128, 4, 516, 2], F32) as Rf, \
         nc.sbuf_tensor("ttmpB", [128, 4, 512], F32) as ttmp, \
         nc.sbuf_tensor("bf_tmp", [128, 128], BF16) as bf_tmp:
        r.tensor.wait(ld)
        r.tensor.wait(gld)
        for ntile in range(4):
            r.tensor.transpose(pb[5][:, ntile * 4:(ntile + 1) * 4],
                               lam[0:4, ntile * 128:(ntile + 1) * 128],
                               ctx["eyef"][0:4, 0:4])
            r.mark("tensor", tc)
        r.vector.wait_ge(tc, r.val(tc))
        lamT = pb[5]
        for ntile in range(4):
            r.scalar.wait_ge(tc, r.val(tc))
            r.scalar.activation(lcol[:, 2 * ntile:2 * ntile + 1], lamT[:, ntile * 4 + 0: ntile * 4 + 1], AF.Exp)
            r.scalar.activation(lcol[:, 2 * ntile + 1:2 * ntile + 2], lamT[:, ntile * 4 + 1: ntile * 4 + 2], AF.Exp)
        r.mark("scalar", sc)
        r.vector.wait_ge(sc, r.val(sc))
        r.vector.memset(Rf[:, :, 0, 0:1], 1.0)
        r.vector.memset(Rf[:, :, 0, 1:2], 0.0)
        for ntile in range(4):
            bcol = lcol[:, 2 * ntile + 1:2 * ntile + 2]
            acol = lcol[:, 2 * ntile:2 * ntile + 1]
            r.vector.tensor_scalar(out=lcol[:, 10:11], in0=acol, scalar1=-1.0, scalar2=None, op0=ALU.mult)
            r.mark("vector", vc)
            r.scalar.wait_ge(vc, r.val(vc))
            r.scalar.activation(lcol[:, 11:12], lcol[:, 10:11], AF.Exp).then_inc(sc, 1)
            r.vector.wait_ge(sc, r.val(sc))
            r.vector.tensor_scalar(out=lcol[:, 12:13], in0=bcol, scalar1=1.0 / TWO_PI,
                                   scalar2=None, op0=ALU.mult)
            r.vector.tensor_scalar(out=lcol[:, 13:14], in0=bcol, scalar1=1.0 / TWO_PI,
                                   scalar2=0.25, op0=ALU.mult, op1=ALU.add)
            r.edge("vector", vc)
            for cidx in (12, 13):
                buf = lcol[:, cidx:cidx + 1]
                for (kk, am) in ((128.0, 128.0), (64.0, 64.0), (32.0, 32.0), (16.0, 16.0), (8.0, 8.0), (4.0, 4.0), (2.0, 2.0), (1.0, 1.0), (0.5, 1.0)):
                    r.vector.tensor_scalar(out=lcol[:, 10:11], in0=buf, scalar1=kk, scalar2=-am,
                                           op0=ALU.is_ge, op1=ALU.mult)
                    r.edge("vector", vc)
                    r.vector.tensor_tensor(out=buf, in0=buf, in1=lcol[:, 10:11], op=ALU.add)
                    r.edge("vector", vc)
            r.mark("vector", vc)
            r.scalar.wait_ge(vc, r.val(vc))
            r.scalar.activation(lcol[:, 14:15], lcol[:, 12:13], AF.Sin, scale=ccol[:, 5:6])
            r.scalar.activation(lcol[:, 15:16], lcol[:, 13:14], AF.Sin, scale=ccol[:, 5:6]).then_inc(sc, 1)
            r.vector.wait_ge(sc, r.val(sc))
            r.vector.tensor_tensor(out=Rf[:, ntile, 1, 0:1], in0=lcol[:, 11:12], in1=lcol[:, 15:16], op=ALU.mult)
            r.vector.tensor_tensor(out=Rf[:, ntile, 1, 1:2], in0=lcol[:, 11:12], in1=lcol[:, 14:15], op=ALU.mult)
            r.edge("vector", vc)
        n_have = 2
        while n_have < 516:
            n_new = min(n_have, 516 - n_have)
            for ntile in range(4):
                pr = Rf[:, ntile, n_have, 0:1]
                pi = Rf[:, ntile, n_have, 1:2]
                a_r = Rf[:, ntile, n_have - 1, 0:1]; a_i = Rf[:, ntile, n_have - 1, 1:2]
                b_r = Rf[:, ntile, 1, 0:1]; b_i = Rf[:, ntile, 1, 1:2]
                r.vector.tensor_tensor(out=lcol[:, 10:11], in0=a_r, in1=b_r, op=ALU.mult)
                r.vector.tensor_tensor(out=lcol[:, 11:12], in0=a_i, in1=b_i, op=ALU.mult)
                r.vector.tensor_tensor(out=lcol[:, 12:13], in0=a_r, in1=b_i, op=ALU.mult)
                r.vector.tensor_tensor(out=lcol[:, 13:14], in0=a_i, in1=b_r, op=ALU.mult)
                r.edge("vector", vc)
                r.vector.tensor_tensor(out=lcol[:, 10:11], in0=lcol[:, 10:11], in1=lcol[:, 11:12], op=ALU.subtract)
                r.vector.tensor_tensor(out=lcol[:, 11:12], in0=lcol[:, 12:13], in1=lcol[:, 13:14], op=ALU.add)
                r.edge("vector", vc)
                s0 = Rf[:, ntile, 0:n_new, 0:1]; s1 = Rf[:, ntile, 0:n_new, 1:2]
                d0 = Rf[:, ntile, n_have:n_have + n_new, 0:1]
                d1 = Rf[:, ntile, n_have:n_have + n_new, 1:2]
                r.vector.tensor_scalar(out=ttmp[:, ntile, 0:n_new], in0=s1, scalar1=lcol[:, 11:12], scalar2=None, op0=ALU.mult)
                r.vector.tensor_scalar(out=ttmp[:, (ntile + 1) % 4, 0:n_new], in0=s1, scalar1=lcol[:, 10:11], scalar2=None, op0=ALU.mult)
                r.edge("vector", vc)
                r.vector.scalar_tensor_tensor(out=d0, in0=s0, scalar=lcol[:, 10:11], in1=ttmp[:, ntile, 0:n_new],
                                              op0=ALU.mult, op1=ALU.subtract)
                r.vector.scalar_tensor_tensor(out=d1, in0=s0, scalar=lcol[:, 11:12], in1=ttmp[:, (ntile + 1) % 4, 0:n_new],
                                              op0=ALU.mult, op1=ALU.add)
                r.edge("vector", vc)
            n_have += n_new
        for ntile in range(4):
            r.vector.tensor_scalar(out=lcol[:, 2 * ntile:2 * ntile + 1], in0=Rf[:, ntile, 512, 0:1],
                                   scalar1=1.0, scalar2=None, op0=ALU.mult)
            r.vector.tensor_scalar(out=lcol[:, 2 * ntile + 1:2 * ntile + 2], in0=Rf[:, ntile, 512, 1:2],
                                   scalar1=-1.0, scalar2=None, op0=ALU.mult)
        r.edge("vector", vc)
        r.vector.tensor_copy(out=Rr[:, :, :], in_=Rf[:, :, :, 0])
        r.vector.tensor_scalar(out=Rni[:, :, :], in0=Rf[:, :, :, 1], scalar1=-1.0, scalar2=None, op0=ALU.mult)
        r.edge("vector", vc)
        for comp in range(2):
            dstT = ATr if comp == 0 else ATi
            for ntile in range(4):
                for mt in range(4):
                    lo = 511 - (mt * 128 + 127)
                    seg = Rf[:, ntile, lo:lo + 128, comp]
                    rv = rev_slice(seg, 0, 128)
                    r.vector.tensor_copy(out=bf_tmp[:, :], in_=rv).then_inc(vc, 1)
                    r.tensor.wait_ge(vc, r.val(vc))
                    r.tensor.transpose(ctx["ptr2"][:, 0:128], bf_tmp[:, :], eye[:, :])
                    r.mark("tensor", tc)
                    r.vector.wait_ge(tc, r.val(tc))
                    r.vector.tensor_copy(out=dstT[:, mt * 4 + ntile, :], in_=ctx["ptr2"][:, 0:128]).then_inc(vc, 1)
                    r.tensor.wait_ge(vc, r.val(vc))
        r.gpsimd.dma_start(out=Cr[:, :, :], in_=dram["crT"].rearrange("(nt p) d -> p nt d", p=128)).then_inc(gld, 16)
        r.gpsimd.dma_start(out=Ci[:, :, :], in_=dram["ciT"].rearrange("(nt p) d -> p nt d", p=128)).then_inc(gld, 16)
        for ntile in range(4):
            pass
        r.vector.wait(gld)
        for ntile in range(4):
            r.tensor.wait_ge(vc, r.val(vc))
            r.tensor.transpose(pb[5][:, ntile * 4:(ntile + 1) * 4],
                               lam[0:4, ntile * 128:(ntile + 1) * 128],
                               ctx["eyef"][0:4, 0:4])
            r.mark("tensor", tc)
        r.vector.wait_ge(tc, r.val(tc))
        for ntile in range(4):
            lr = pb[5][:, ntile * 4 + 0:ntile * 4 + 1]
            li = pb[5][:, ntile * 4 + 1:ntile * 4 + 2]
            r.scalar.wait_ge(vc, r.val(vc))
            r.scalar.activation(lcol[:, 8:9], lr, AF.Exp)
            r.scalar.activation(lcol[:, 9:10], li, AF.Exp).then_inc(sc, 1)
            r.vector.wait_ge(sc, r.val(sc))
            r.vector.tensor_scalar(out=lcol[:, 8:9], in0=lcol[:, 8:9], scalar1=-1.0, scalar2=None, op0=ALU.mult)
            r.edge("vector", vc)
            r.vector.tensor_tensor(out=lcol[:, 10:11], in0=lcol[:, 8:9], in1=lcol[:, 8:9], op=ALU.mult)
            r.vector.tensor_tensor(out=lcol[:, 11:12], in0=lcol[:, 9:10], in1=lcol[:, 9:10], op=ALU.mult)
            r.edge("vector", vc)
            r.vector.tensor_tensor(out=lcol[:, 10:11], in0=lcol[:, 10:11], in1=lcol[:, 11:12], op=ALU.add)
            r.edge("vector", vc)
            r.vector.reciprocal(out=lcol[:, 10:11], in_=lcol[:, 10:11])
            r.edge("vector", vc)
            r.vector.tensor_scalar(out=lcol[:, 11:12], in0=Rf[:, ntile, 1, 0:1], scalar1=1.0, scalar2=None, op0=ALU.subtract)
            r.edge("vector", vc)
            ni = Rf[:, ntile, 1, 1:2]
            r.vector.tensor_tensor(out=lcol[:, 12:13], in0=lcol[:, 11:12], in1=lcol[:, 8:9], op=ALU.mult)
            r.vector.tensor_tensor(out=lcol[:, 13:14], in0=ni, in1=lcol[:, 9:10], op=ALU.mult)
            r.vector.tensor_tensor(out=lcol[:, 14:15], in0=ni, in1=lcol[:, 8:9], op=ALU.mult)
            r.vector.tensor_tensor(out=lcol[:, 15:16], in0=lcol[:, 11:12], in1=lcol[:, 9:10], op=ALU.mult)
            r.edge("vector", vc)
            r.vector.tensor_tensor(out=lcol[:, 12:13], in0=lcol[:, 12:13], in1=lcol[:, 13:14], op=ALU.add)
            r.vector.tensor_tensor(out=lcol[:, 14:15], in0=lcol[:, 14:15], in1=lcol[:, 15:16], op=ALU.subtract)
            r.edge("vector", vc)
            r.vector.tensor_scalar(out=lcol[:, 12:13], in0=lcol[:, 12:13], scalar1=lcol[:, 10:11], scalar2=None, op0=ALU.mult)
            r.vector.tensor_scalar(out=lcol[:, 14:15], in0=lcol[:, 14:15], scalar1=lcol[:, 10:11], scalar2=None, op0=ALU.mult)
            r.edge("vector", vc)
            for dh in range(2):
                csl = slice(dh * 512, (dh + 1) * 512)
                r.vector.tensor_scalar(out=ttmp[:, 0, :], in0=Ci[:, ntile, csl], scalar1=lcol[:, 14:15], scalar2=None, op0=ALU.mult)
                r.vector.tensor_scalar(out=ttmp[:, 1, :], in0=Cr[:, ntile, csl], scalar1=lcol[:, 14:15], scalar2=None, op0=ALU.mult)
                r.edge("vector", vc)
                r.vector.scalar_tensor_tensor(out=Cr[:, ntile, csl], in0=Cr[:, ntile, csl], scalar=lcol[:, 12:13],
                                              in1=ttmp[:, 0, :], op0=ALU.mult, op1=ALU.subtract)
                r.vector.scalar_tensor_tensor(out=Ci[:, ntile, csl], in0=Ci[:, ntile, csl], scalar=lcol[:, 12:13],
                                              in1=ttmp[:, 1, :], op0=ALU.mult, op1=ALU.add)
                r.edge("vector", vc)
    r.barrier(ctx["bar_sems"])
    with nc.sbuf_tensor("Ksb", [128, 4, D], BF16) as Ksb:

        r.tensor.wait_ge(vc, r.val(vc))
        for jt in range(4):
            for dh in range(2):
                bank = pb[dh]
                for ntile in range(4):
                    r.tensor.matmul(bank[:, :], Rr[:, ntile, jt * 128:(jt + 1) * 128],
                                    Cr[:, ntile, dh * 512:(dh + 1) * 512],
                                    start=(ntile == 0), stop=False)
                for ntile in range(4):
                    r.tensor.matmul(bank[:, :], Rni[:, ntile, jt * 128:(jt + 1) * 128],
                                    Ci[:, ntile, dh * 512:(dh + 1) * 512],
                                    start=False, stop=(ntile == 3))
                r.mark("tensor", tc)
                r.vector.wait_ge(tc, r.val(tc))
                r.vector.tensor_copy(out=Ksb[:, jt, dh * 512:(dh + 1) * 512], in_=bank[:, :]).then_inc(vc, 1)
                r.tensor.wait_ge(vc, r.val(vc))
        for ft in range(4):
            for comp in range(2):
                FT = FcT if comp == 0 else FsT
                KT = Ktr if comp == 0 else Kti
                for dh in range(2):
                    bank = pb[2 + (dh % 2)]
                    knyq = (comp == 1 and ft == 0)
                    for mt in range(4):
                        r.tensor.matmul(bank[:, :], FT[:, mt, ft * 128:(ft + 1) * 128],
                                        Ksb[:, mt, dh * 512:(dh + 1) * 512],
                                        start=(mt == 0), stop=(mt == 3 and not knyq))
                    if knyq:
                        for mt in range(4):
                            r.tensor.matmul(bank[:, :], parn_bf[:, mt, :],
                                            Ksb[:, mt, dh * 512:(dh + 1) * 512],
                                            start=False, stop=(mt == 3))
                    r.mark("tensor", tc)
                    r.vector.wait_ge(tc, r.val(tc))
                    r.vector.tensor_scalar(out=KT[:, ft, dh * 512:(dh + 1) * 512], in0=bank[:, :],
                                           scalar1=2.0 / 1024.0, scalar2=None, op0=ALU.mult).then_inc(vc, 1)
                    r.tensor.wait_ge(vc, r.val(vc))


    if DBG_ON:
        for nm, ap in [("FcT", FcT), ("FsT", FsT), ("Ktr", Ktr), ("Kti", Kti), ("Rr", Rr), ("Cr", Cr), ("Ci", Ci)]:
            DBG[nm] = ap[:, :, :]
    r.barrier(ctx["bar_sems"])

    with nc.sbuf_tensor("Zr", [128, 4, D], BF16) as Zr, \
         nc.sbuf_tensor("Zi", [128, 4, D], BF16) as Zi, \
         nc.sbuf_tensor("Pr", [128, 4, D], BF16) as Pr, \
         nc.sbuf_tensor("Pi", [128, 4, D], BF16) as Pi, \
         nc.sbuf_tensor("vt2", [128, D], BF16) as vt2, \
         nc.sbuf_tensor("pd_s", [128, D], F32) as pd:
        r.sync.dma_start(out=pd[:, :], in_=dram["gbd_rep"][:, 2, :]).then_inc(ld, 16)
        vt1 = ctx["sq_scr"]
        Usc = ctx["xnorm"]
        r.vector.memset(Zr[:, :, :], 0.0)
        r.vector.memset(Zi[:, :, :], 0.0)
        r.edge("vector", vc)
        inj_free = {}

        def injections(c):
            for comp in range(2):
                AT = ATr if comp == 0 else ATi
                E = Pr if comp == 0 else Pi
                for ntile in range(4):
                    for dh in range(2):
                        bid = (ntile * 2 + dh) % 4
                        bank = pb[bid]
                        r.tensor.wait_ge(vc, inj_free.get(bid, 0))
                        for mt in range(4):
                            r.tensor.matmul(bank[:, :], AT[:, mt * 4 + ntile, :],
                                            u[:, c * 4 + mt, dh * 512:(dh + 1) * 512],
                                            start=(mt == 0), stop=(mt == 3))
                        r.mark("tensor", tc)
                        r.vector.wait_ge(tc, r.val(tc))
                        r.vector.tensor_copy(out=E[:, ntile, dh * 512:(dh + 1) * 512], in_=bank[:, :]).then_inc(vc, 1)
                        inj_free[bid] = r.val(vc)

        def z_update2():
            for ntile in range(4):
                lr_ = lcol[:, 2 * ntile:2 * ntile + 1]
                lni = lcol[:, 2 * ntile + 1:2 * ntile + 2]
                zr = Zr[:, ntile, :]; zi = Zi[:, ntile, :]
                r.vector.tensor_scalar(out=vt1[:, :], in0=zi, scalar1=lni, scalar2=None, op0=ALU.mult)
                r.edge("vector", vc)
                r.vector.tensor_scalar(out=zi, in0=zi, scalar1=lr_, scalar2=None, op0=ALU.mult)
                r.edge("vector", vc)
                r.vector.scalar_tensor_tensor(out=zi, in0=zr, scalar=lni, in1=zi, op0=ALU.mult, op1=ALU.subtract)
                r.edge("vector", vc)
                r.vector.tensor_scalar(out=zi, in0=zi, scalar1=-1.0, scalar2=None, op0=ALU.mult)
                r.vector.scalar_tensor_tensor(out=zr, in0=zr, scalar=lr_, in1=vt1[:, :], op0=ALU.mult, op1=ALU.add)
                r.edge("vector", vc)
                er = Pr[:, ntile, :]; ei = Pi[:, ntile, :]
                r.vector.tensor_tensor(out=vt1[:, :], in0=Cr[:, ntile, :], in1=er, op=ALU.mult)
                r.vector.tensor_tensor(out=vt2[:, :], in0=Ci[:, ntile, :], in1=ei, op=ALU.mult)
                r.edge("vector", vc)
                r.vector.tensor_tensor(out=zr, in0=zr, in1=vt1[:, :], op=ALU.add)
                r.edge("vector", vc)
                r.vector.tensor_tensor(out=zr, in0=zr, in1=vt2[:, :], op=ALU.subtract)
                r.vector.tensor_tensor(out=vt1[:, :], in0=Cr[:, ntile, :], in1=ei, op=ALU.mult)
                r.vector.tensor_tensor(out=vt2[:, :], in0=Ci[:, ntile, :], in1=er, op=ALU.mult)
                r.edge("vector", vc)
                r.vector.tensor_tensor(out=zi, in0=zi, in1=vt1[:, :], op=ALU.add)
                r.edge("vector", vc)
                r.vector.tensor_tensor(out=zi, in0=zi, in1=vt2[:, :], op=ALU.add)
                r.edge("vector", vc)

        for c in range(4):
            injections(c)
            r.vector.wait_ge(vc, r.val(vc))
            z_update2()

        r.mark("vector", vc)
        if DBG_ON:
            _dz1r = ctx["nc"].dram_tensor("dbg_Z1r", [128, 4, D], F32, kind="ExternalOutput").ap()
            _dz1i = ctx["nc"].dram_tensor("dbg_Z1i", [128, 4, D], F32, kind="ExternalOutput").ap()
            r.gpsimd.wait_ge(vc, r.val(vc))
            r.gpsimd.dma_start(out=_dz1r, in_=Zr[:, :, :]).then_inc(gld, 16)
            r.gpsimd.dma_start(out=_dz1i, in_=Zi[:, :, :]).then_inc(gld, 16)
        r.sync.wait(gld)
        r.sync.wait_ge(vc, r.val(vc))
        r.sync.dma_start(out=idram["s_contrib"][:, 0:D].rearrange("(nt p) d -> p nt d", p=128),
                         in_=Zr[:, :, :]).then_inc(ctx["sems"]["st"], 16)
        r.sync.dma_start(out=idram["s_contrib"][:, D:2 * D].rearrange("(nt p) d -> p nt d", p=128),
                         in_=Zi[:, :, :]).then_inc(ctx["sems"]["st"], 16)
        r.gpsimd.wait(ctx["sems"]["st"])
        r.gpsimd.collective_compute("AllGather", ALU.bypass, replica_groups=ctx["GROUPS"],
                                    ins=[idram["s_contrib"]], outs=[idram["s_gather"]]).then_inc(cc, 1)
        r.gpsimd.wait_ge(cc, r.val(cc))
        r.vector.wait_ge(cc, r.val(cc))
        r.vector.memset(Zr[:, :, :], 0.0)
        r.vector.memset(Zi[:, :, :], 0.0)
        r.mark("vector", vc)
        for p_ in range(4):
            r.gpsimd.wait_ge(vc, r.val(vc))
            r.gpsimd.dma_start(out=Pr[:, :, :], in_=idram["s_gather"][p_ * 512:(p_ + 1) * 512, 0:D]
                               .rearrange("(nt p) d -> p nt d", p=128)).then_inc(gld, 16)
            r.gpsimd.dma_start(out=Pi[:, :, :], in_=idram["s_gather"][p_ * 512:(p_ + 1) * 512, D:2 * D]
                               .rearrange("(nt p) d -> p nt d", p=128)).then_inc(gld, 16)
            r.vector.wait(gld)
            for ntile in range(4):
                a_r = lcol[:, 2 * ntile:2 * ntile + 1]; a_ni = lcol[:, 2 * ntile + 1:2 * ntile + 2]
                r.vector.tensor_tensor(out=lcol[:, 8:9], in0=a_r, in1=a_r, op=ALU.mult)
                r.vector.tensor_tensor(out=lcol[:, 9:10], in0=a_ni, in1=a_ni, op=ALU.mult)
                r.vector.tensor_tensor(out=lcol[:, 10:11], in0=a_r, in1=a_ni, op=ALU.mult)
                r.edge("vector", vc)
                r.vector.tensor_tensor(out=lcol[:, 8:9], in0=lcol[:, 8:9], in1=lcol[:, 9:10], op=ALU.subtract)
                r.vector.tensor_scalar(out=lcol[:, 9:10], in0=lcol[:, 10:11], scalar1=2.0, scalar2=None, op0=ALU.mult)
                r.edge("vector", vc)
                r.vector.tensor_tensor(out=lcol[:, 10:11], in0=lcol[:, 8:9], in1=lcol[:, 8:9], op=ALU.mult)
                r.vector.tensor_tensor(out=lcol[:, 11:12], in0=lcol[:, 9:10], in1=lcol[:, 9:10], op=ALU.mult)
                r.vector.tensor_tensor(out=lcol[:, 12:13], in0=lcol[:, 8:9], in1=lcol[:, 9:10], op=ALU.mult)
                r.edge("vector", vc)
                r.vector.tensor_tensor(out=lcol[:, 10:11], in0=lcol[:, 10:11], in1=lcol[:, 11:12], op=ALU.subtract)
                r.vector.tensor_scalar(out=lcol[:, 11:12], in0=lcol[:, 12:13], scalar1=2.0, scalar2=None, op0=ALU.mult)
                r.edge("vector", vc)
                r.vector.tensor_tensor(out=lcol[:, 12:13], in0=lcol[:, 10:11], in1=lcol[:, 10:11], op=ALU.mult)
                r.vector.tensor_tensor(out=lcol[:, 13:14], in0=lcol[:, 11:12], in1=lcol[:, 11:12], op=ALU.mult)
                r.vector.tensor_tensor(out=lcol[:, 14:15], in0=lcol[:, 10:11], in1=lcol[:, 11:12], op=ALU.mult)
                r.edge("vector", vc)
                r.vector.tensor_tensor(out=lcol[:, 12:13], in0=lcol[:, 12:13], in1=lcol[:, 13:14], op=ALU.subtract)
                r.vector.tensor_scalar(out=lcol[:, 13:14], in0=lcol[:, 14:15], scalar1=2.0, scalar2=None, op0=ALU.mult)
                r.edge("vector", vc)
                m0 = cmask_rep[:, p_ * 3 + 0:p_ * 3 + 1]
                m1 = cmask_rep[:, p_ * 3 + 1:p_ * 3 + 2]
                m2 = cmask_rep[:, p_ * 3 + 2:p_ * 3 + 3]
                r.vector.tensor_scalar(out=lcol[:, 14:15], in0=lcol[:, 10:11], scalar1=m1, scalar2=None, op0=ALU.mult)
                r.vector.tensor_scalar(out=lcol[:, 15:16], in0=lcol[:, 11:12], scalar1=m1, scalar2=None, op0=ALU.mult)
                r.edge("vector", vc)
                r.vector.scalar_tensor_tensor(out=lcol[:, 14:15], in0=lcol[:, 12:13], scalar=m2, in1=lcol[:, 14:15],
                                              op0=ALU.mult, op1=ALU.add)
                r.vector.scalar_tensor_tensor(out=lcol[:, 15:16], in0=lcol[:, 13:14], scalar=m2, in1=lcol[:, 15:16],
                                              op0=ALU.mult, op1=ALU.add)
                r.edge("vector", vc)
                r.vector.tensor_scalar(out=lcol[:, 14:15], in0=lcol[:, 14:15], scalar1=m0, scalar2=None, op0=ALU.add)
                r.edge("vector", vc)
                sr = Pr[:, ntile, :]; si = Pi[:, ntile, :]
                r.vector.tensor_scalar(out=vt1[:, :], in0=sr, scalar1=lcol[:, 14:15], scalar2=None, op0=ALU.mult)
                r.vector.tensor_scalar(out=vt2[:, :], in0=si, scalar1=lcol[:, 15:16], scalar2=None, op0=ALU.mult)
                r.edge("vector", vc)
                r.vector.tensor_tensor(out=Zr[:, ntile, :], in0=Zr[:, ntile, :], in1=vt1[:, :], op=ALU.add)
                r.edge("vector", vc)
                r.vector.tensor_tensor(out=Zr[:, ntile, :], in0=Zr[:, ntile, :], in1=vt2[:, :], op=ALU.add)
                r.vector.tensor_scalar(out=vt1[:, :], in0=si, scalar1=lcol[:, 14:15], scalar2=None, op0=ALU.mult)
                r.vector.tensor_scalar(out=vt2[:, :], in0=sr, scalar1=lcol[:, 15:16], scalar2=None, op0=ALU.mult)
                r.edge("vector", vc)
                r.vector.tensor_tensor(out=Zi[:, ntile, :], in0=Zi[:, ntile, :], in1=vt1[:, :], op=ALU.add)
                r.edge("vector", vc)
                r.vector.tensor_tensor(out=Zi[:, ntile, :], in0=Zi[:, ntile, :], in1=vt2[:, :], op=ALU.subtract)
                r.edge("vector", vc)
            r.mark("vector", vc)

        r.mark("vector", vc)
        if DBG_ON:
            _dvr = ctx["nc"].dram_tensor("dbg_Vr", [128, 4, D], F32, kind="ExternalOutput").ap()
            _dvi = ctx["nc"].dram_tensor("dbg_Vi", [128, 4, D], F32, kind="ExternalOutput").ap()
            r.gpsimd.wait_ge(vc, r.val(vc))
            r.gpsimd.dma_start(out=_dvr, in_=Zr[:, :, :]).then_inc(gld, 16)
            r.gpsimd.dma_start(out=_dvi, in_=Zi[:, :, :]).then_inc(gld, 16)
        r.sync.wait(gld)
        for c in range(4):
            zq_mark = r.val(vc)
            r.vector.wait_ge(vc, r.val(vc))
            for ft in range(4):
                for comp in range(2):
                    FT = FcT if comp == 0 else FsT
                    for dh in range(2):
                        bid = comp * 2 + dh
                        bank = pb[bid]
                        r.tensor.wait_ge(vc, inj_free.get(bid, zq_mark))
                        nyq = (comp == 1 and ft == 0)
                        for mt in range(4):
                            r.tensor.matmul(bank[:, :], FT[:, mt, ft * 128:(ft + 1) * 128],
                                            u[:, c * 4 + mt, dh * 512:(dh + 1) * 512],
                                            start=(mt == 0), stop=(mt == 3 and not nyq))
                        if nyq:
                            for mt in range(4):
                                r.tensor.matmul(bank[:, :], parn_bf[:, mt, :],
                                                u[:, c * 4 + mt, dh * 512:(dh + 1) * 512],
                                                start=False, stop=(mt == 3))
                        r.mark("tensor", tc)
                        r.vector.wait_ge(tc, r.val(tc))
                        r.vector.tensor_copy(out=Usc[:, comp, dh * 512:(dh + 1) * 512], in_=bank[:, :]).then_inc(vc, 1)
                        inj_free[bid] = r.val(vc)
                r.vector.tensor_tensor(out=vt1[:, :], in0=Usc[:, 1, :], in1=Kti[:, ft, :], op=ALU.mult)
                r.vector.tensor_tensor(out=vt2[:, :], in0=Usc[:, 1, :], in1=Ktr[:, ft, :], op=ALU.mult)
                r.edge("vector", vc)
                r.vector.tensor_tensor(out=Pr[:, ft, :], in0=Usc[:, 0, :], in1=Ktr[:, ft, :], op=ALU.mult)
                r.vector.tensor_tensor(out=Pi[:, ft, :], in0=Usc[:, 0, :], in1=Kti[:, ft, :], op=ALU.mult)
                r.edge("vector", vc)
                r.vector.tensor_tensor(out=Pr[:, ft, :], in0=Pr[:, ft, :], in1=vt1[:, :], op=ALU.subtract)
                r.vector.tensor_tensor(out=Pi[:, ft, :], in0=Pi[:, ft, :], in1=vt2[:, :], op=ALU.add)
                r.edge("vector", vc)
                if ft == 0:
                    r.vector.tensor_tensor(out=Pr[0:1, 0, :], in0=Usc[0:1, 0, :], in1=Ktr[0:1, 0, :], op=ALU.mult)
                    r.vector.tensor_tensor(out=Pi[0:1, 0, :], in0=Usc[0:1, 1, :], in1=Kti[0:1, 0, :], op=ALU.mult)
                    r.edge("vector", vc)
            r.mark("vector", vc)
            pw_mark = r.val(vc)
            r.tensor.wait_ge(vc, pw_mark)
            ev_free = {}
            for lt in range(4):
                for dh in range(2):
                    bid = 4 + (lt * 2 + dh) % 2
                    bank = pb[bid]
                    r.tensor.wait_ge(vc, ev_free.get(bid, 0))
                    for ft in range(4):
                        r.tensor.matmul(bank[:, :], FcT[:, ft, lt * 128:(lt + 1) * 128],
                                        Pr[:, ft, dh * 512:(dh + 1) * 512], start=(ft == 0), stop=False)
                    for ft in range(4):
                        r.tensor.matmul(bank[:, :], FsT[:, ft, lt * 128:(lt + 1) * 128],
                                        Pi[:, ft, dh * 512:(dh + 1) * 512], start=False, stop=False)
                    r.tensor.matmul(bank[:, :], invfix[0:1, lt * 128:(lt + 1) * 128],
                                    Pr[0:1, 0, dh * 512:(dh + 1) * 512], start=False, stop=False)
                    r.tensor.matmul(bank[:, :], invfix[0:1, 512 + lt * 128:512 + (lt + 1) * 128],
                                    Pi[0:1, 0, dh * 512:(dh + 1) * 512], start=False, stop=False)
                    for ntile in range(4):
                        r.tensor.matmul(bank[:, :], Rr[:, ntile, 1 + lt * 128:1 + (lt + 1) * 128],
                                        Zr[:, ntile, dh * 512:(dh + 1) * 512], start=False, stop=False)
                    for ntile in range(4):
                        r.tensor.matmul(bank[:, :], Rni[:, ntile, 1 + lt * 128:1 + (lt + 1) * 128],
                                        Zi[:, ntile, dh * 512:(dh + 1) * 512], start=False, stop=(ntile == 3))
                    r.mark("tensor", tc)
                    r.vector.wait_ge(tc, r.val(tc))
                    rs = resid[:, c * 4 + lt, dh * 512:(dh + 1) * 512]
                    r.vector.scalar_tensor_tensor(out=rs, in0=bank[:, :], scalar=1.0, in1=rs,
                                                  op0=ALU.mult, op1=ALU.add).then_inc(vc, 1)
                    ev_free[bid] = r.val(vc)
            injections(c)
            r.vector.wait_ge(vc, r.val(vc))
            z_update2()
        r.mark("vector", vc)
        if DBG_ON:
            _dzfr = ctx["nc"].dram_tensor("dbg_Zfr", [128, 4, D], F32, kind="ExternalOutput").ap()
            _dzfi = ctx["nc"].dram_tensor("dbg_Zfi", [128, 4, D], F32, kind="ExternalOutput").ap()
            r.gpsimd.wait_ge(vc, r.val(vc))
            r.gpsimd.dma_start(out=_dzfr, in_=Zr[:, :, :]).then_inc(gld, 16)
            r.gpsimd.dma_start(out=_dzfi, in_=Zi[:, :, :]).then_inc(gld, 16)
        r.sync.wait(gld)
        r.vector.wait(ld)
        for t in range(NT):
            r.vector.tensor_tensor(out=vt1[:, :], in0=u[:, t, :], in1=pd[:, :], op=ALU.mult)
            r.edge("vector", vc)
            r.vector.tensor_tensor(out=resid[:, t, :], in0=resid[:, t, :], in1=vt1[:, :], op=ALU.add)
            r.edge("vector", vc)
    r.tensor.wait_ge(vc, r.val(vc))


def _bfview(nc, r, ap_f32, bf_tmp, vc):
    rows = ap_f32.shape[0]
    cols = ap_f32.shape[-1]
    tc_ = Rec.CUR.semv
    r.vector.wait_ge(_TC[0], Rec.CUR.val(_TC[0]))
    r.vector.tensor_copy(out=bf_tmp[0:rows, 0:cols], in_=ap_f32).then_inc(vc, 1)
    r.tensor.wait_ge(vc, Rec.CUR.val(vc))
    return bf_tmp[0:rows, 0:cols]


_TC = [None]


def _prep_inputs(inputs, stages):
    x = np.ascontiguousarray(inputs["L0_attn_x"])
    wgT = np.ascontiguousarray(inputs["L0_ffn_w_gate"].T)
    wuT = np.ascontiguousarray(inputs["L0_ffn_w_up"].T)
    wdT = np.ascontiguousarray(inputs["L0_ffn_w_down"].T)
    lam = np.zeros((4, 512), np.float32)
    lam[0] = inputs["L1_attn_Lambda_real"]
    lam[1] = inputs["L1_attn_Lambda_imag"]
    crT = np.ascontiguousarray(inputs["L1_attn_C_real"].T)
    ciT = np.ascontiguousarray(inputs["L1_attn_C_imag"].T)
    gbd = np.zeros((4, D), np.float32)
    gbd[0] = inputs["L1_attn_gamma"]; gbd[1] = inputs["L1_attn_beta"]; gbd[2] = inputs["L1_attn_param_D"]
    gbd_rep = np.ascontiguousarray(np.broadcast_to(gbd[None, :3], (128, 3, D)))
    fsb = np.stack([inputs["L1_ffn_scale"], inputs["L1_ffn_bias"]]).astype(np.float32)
    fsb_rep = np.ascontiguousarray(np.broadcast_to(fsb[None], (128, 2, D)))
    wfT = np.ascontiguousarray(inputs["L1_ffn_weights"].T)
    eye = np.eye(128, dtype=np.float32)
    mm = np.arange(128)
    samb = (mm[:, None] // 64 == mm[None, :] // 64)
    strictT = (samb & (mm[:, None] < mm[None, :])).astype(np.float32)
    inclT = (samb & (mm[:, None] <= mm[None, :])).astype(np.float32)
    mskA = np.concatenate([inclT, inclT, strictT, strictT], axis=1)
    blksel = np.concatenate([samb.astype(np.float32), (~samb).astype(np.float32)], axis=1)
    ar = np.arange(512, dtype=np.int64)
    prod = ((ar[:, None] * ar[None, :]) % 1024).astype(np.float32)
    parn = np.zeros((128, 4, 128), np.float32)
    parn[:, :, 0] = ((-1.0) ** np.arange(512)).reshape(4, 128).T
    in_maps = []
    for c in range(NCORE):
        b, q = c // 4, c % 4
        cm = np.zeros((1, 16), np.float32)
        for p in range(4):
            k = q - 1 - p
            if 0 <= k <= 2:
                cm[0, p * 3 + k] = 1.0
        qm = np.zeros((1, 8), np.float32)
        qm[0, 0] = q; qm[0, 1] = b
        hsel = slice(4 * q * 64, 4 * q * 64 + 256)
        wmy1 = np.concatenate([inputs["L0_attn_Wq"][:, hsel], inputs["L0_attn_Wk"][:, hsel],
                               inputs["L0_attn_Wv"][:, hsel], inputs["L0_attn_Wg"][:, hsel]], axis=1)
        wmy2 = np.zeros((D, 16), np.float32)
        for w_ in range(2):
            wmy2[:, 4 * w_ + 0:4 * w_ + 2] = inputs["L0_attn_Wb"][:, 4 * q + 2 * w_: 4 * q + 2 * w_ + 2]
            wmy2[:, 4 * w_ + 2:4 * w_ + 4] = inputs["L0_attn_Wa"][:, 4 * q + 2 * w_: 4 * q + 2 * w_ + 2]
        womy = inputs["L0_attn_Wo"][4 * q * 64: 4 * q * 64 + 256, :]
        attc_my = np.zeros((1, 128), np.float32)
        attc_my[0, 0:4] = inputs["L0_attn_dt_bias"][4 * q: 4 * q + 4]
        attc_my[0, 4:8] = inputs["L0_attn_A_log"][4 * q: 4 * q + 4]
        attc_my[0, 8:72] = inputs["L0_attn_norm_w"]
        m = dict(
            x=np.ascontiguousarray(x[b, q * TOK:(q + 1) * TOK, :]),
            wgT=wgT, wuT=wuT, wdT=wdT,
            wmy1=wmy1, wmy2=wmy2, womy=womy, attc_my=attc_my,
            lam=lam, crT=crT, ciT=ciT, gbd_rep=gbd_rep, fsb_rep=fsb_rep, wfT=wfT,
            eye=eye, eyef=np.eye(128, dtype=np.float32), mskA=mskA, blksel=blksel,
            inclTf=inclT.astype(np.float32), prod=prod, parn=parn, cmask=cm, qmask=qm,
            invfix=np.concatenate([np.full(512, -0.5), 0.5 * ((-1.0) ** np.arange(512))]).astype(np.float32)[None, :],
        )
        m = {k: np.ascontiguousarray(v) for k, v in m.items()}
        in_maps.append(m)
    return in_maps


def kernel(_stages=None, **inputs):
    stages = _stages if _stages is not None else STAGES
    in_maps = _prep_inputs(inputs, stages)
    nc = build_program(stages)
    res = run_bass_kernel_spmd(nc, in_maps, core_ids=list(range(NCORE)))
    global LAST_RES
    LAST_RES = res
    out = np.zeros((2, SEQ, D), np.float32)
    for c in range(NCORE):
        b, q = c // 4, c % 4
        out[b, q * TOK:(q + 1) * TOK, :] = res.results[c]["out"]
    return out
